# version 04
# baseline (speedup 1.0000x reference)
import numpy as np
import concourse.bass as bass

DBG_XM = False
import concourse.mybir as mybir
from concourse.bass_utils import run_bass_kernel_spmd

F32 = mybir.dt.float32
BF16 = mybir.dt.bfloat16
AF = mybir.ActivationFunctionType
ALU = mybir.AluOpType
AX = mybir.AxisListType.X

TPC = 2048
NT = TPC // 128
D = 1024
DC = D // 128
H = 16
DK = 64
NST = 64
LSUB = 32
NSUB = 128 // LSUB
EPS_LN = 1e-5
EPS_ATTN = 1e-6

ENG = ("t", "v", "s", "g", "y")


class Prog:

    def __init__(self, nc):
        self.nc = nc
        self.q = {e: [] for e in ENG}
        self.sems = {}
        self.semval = {}
        self.last_w = {}
        self.readers = {}
        self.waited = {e: {} for e in ENG}
        self._stack = []
        names = ["done_t", "done_v", "done_s", "done_g", "done_y",
                 "cc0", "cc1", "cc2"]
        self.NDMA = 4
        for q in ("y", "g", "s"):
            for j in range(self.NDMA):
                names.append("dma_%s%d" % (q, j))
        for name in names:
            cm = nc.semaphore(name)
            handle = cm.__enter__()
            self._stack.append(cm)
            self.sems[name] = handle
            self.semval[name] = 0
        self._dma_rot = {"y": 0, "g": 0, "s": 0}

    def _wait(self, e, semname, val):
        if self.waited[e].get(semname, 0) >= val:
            return
        sem = self.sems[semname]
        self.q[e].append(lambda eng, s=sem, v=val: eng.wait_ge(s, v))
        self.waited[e][semname] = val

    def _deps(self, e, r, w):
        for k in r:
            lw = self.last_w.get(k)
            if lw:
                self._wait(e, *lw)
        for k in w:
            lw = self.last_w.get(k)
            if lw:
                self._wait(e, *lw)
            for sn, v in self.readers.get(k, {}).items():
                self._wait(e, sn, v)

    def _commit(self, e, semname, inc, r, w):
        self.semval[semname] += inc
        val = self.semval[semname]
        tok = (semname, val)
        for k in w:
            self.last_w[k] = tok
            self.readers[k] = {}
        for k in r:
            d = self.readers.setdefault(k, {})
            if d.get(semname, 0) < val:
                d[semname] = val
        return tok

    def op(self, e, fn, r=(), w=()):
        self._deps(e, r, w)
        semname = "done_" + e
        sem = self.sems[semname]
        self.q[e].append(lambda eng, f=fn, s=sem: f(eng).then_inc(s, 1))
        return self._commit(e, semname, 1, r, w)

    def mm(self, fns, r=(), w=()):
        self._deps("t", r, w)
        sem = self.sems["done_t"]
        for f in fns[:-1]:
            self.q["t"].append(lambda eng, f=f: f(eng))
        self.q["t"].append(lambda eng, f=fns[-1], s=sem: f(eng).then_inc(s, 1))
        return self._commit("t", "done_t", 1, r, w)

    def dma(self, e, fn, r=(), w=()):
        assert e in ("y", "g", "s")
        j = self._dma_rot[e]
        self._dma_rot[e] = (j + 1) % self.NDMA
        semname = "dma_%s%d" % (e, j)
        self._wait(e, semname, self.semval[semname])
        self._deps(e, r, w)
        sem = self.sems[semname]
        self.q[e].append(lambda eng, f=fn, s=sem: f(eng).then_inc(s, 16))
        return self._commit(e, semname, 16, r, w)

    def cc(self, n, fn, r=(), w=()):
        semname = "cc%d" % n
        self._deps("g", r, w)
        sem = self.sems[semname]
        self.q["g"].append(lambda eng, f=fn, s=sem: f(eng).then_inc(s))
        self.semval[semname] += 1
        tok = (semname, 1)
        for k in w:
            self.last_w[k] = tok
            self.readers[k] = {}
        for k in r:
            d = self.readers.setdefault(k, {})
            d[semname] = 1
        return tok

    def barrier(self):
        for e in ENG:
            for sn, v in self.semval.items():
                if v > 0:
                    self._wait(e, sn, v)

    def finalize(self):
        nc = self.nc
        with nc.Block() as block:
            @block.tensor
            def _(eng):
                for f in self.q["t"]:
                    f(eng)

            @block.vector
            def _(eng):
                for f in self.q["v"]:
                    f(eng)

            @block.scalar
            def _(eng):
                for f in self.q["s"]:
                    f(eng)

            @block.gpsimd
            def _(eng):
                for f in self.q["g"]:
                    f(eng)

            @block.sync
            def _(eng):
                for f in self.q["y"]:
                    f(eng)

    def close(self):
        for h in reversed(self._stack):
            h.__exit__(None, None, None)


class Buf:

    def __init__(self, nc, name, shape, dtype, psum=False):
        if psum:
            esz = 2 if dtype == BF16 else 4
            free = 1
            for s in shape[1:]:
                free *= s
            assert (free * esz) % 2048 == 0, (name, shape, dtype)
        cm = nc.psum_tensor(name, shape, dtype) if psum else nc.sbuf_tensor(name, shape, dtype)
        self.t = cm.__enter__()
        self._cm = cm
        self.name = name

    def __getitem__(self, idx):
        return self.t[idx]

    def ap(self):
        return self.t[:]

    def free(self):
        self._cm.__exit__(None, None, None)



CONSTS = {0: EPS_LN, 1: 1.0}


def emit_const_tiles(P, nc, bufs):
    bufs["id_f"] = idf = Buf(nc, "id_f", [128, 128], F32)
    bufs["id_b"] = idb = Buf(nc, "id_b", [128, 128], BF16)
    bufs["ones"] = ones = Buf(nc, "ones_t", [128, 128], F32)
    bufs["cst"] = cst = Buf(nc, "cst", [128, 8], F32)
    P.op("g", lambda e: e.memset(ones.ap(), 1.0), w=["ones"])
    for col, val in CONSTS.items():
        P.op("g", lambda e, col=col, val=val: e.memset(cst.t[:, col:col + 1], float(val)),
             w=["cst"])
    P.op("g", lambda e: e.memset(idf.ap(), 1.0), w=["id_f"])
    P.op("g", lambda e: e.affine_select(idf.ap(), idf.ap(), [[1, 128]], ALU.is_equal,
                                        0.0, base=0, channel_multiplier=-1),
         r=["id_f"], w=["id_f"])
    P.op("v", lambda e: e.tensor_copy(idb.ap(), idf.ap()), r=["id_f"], w=["id_b"])


def load_w_transposed(P, nc, bufs, key, dram_ap, rows, cols, ps):
    rc = rows // 128
    cc = cols // 128
    out = Buf(nc, key, [128, rc, cols], BF16)
    bufs[key] = out
    nat = Buf(nc, key + "_nat", [128, cc, rows], BF16)
    for half in range(2):
        sl = slice(half * (cc // 2), (half + 1) * (cc // 2)) if cc >= 2 else slice(0, cc)
        P.dma("g", lambda e, sl=sl: e.dma_start(
            nat.t[:, sl, :],
            dram_ap.rearrange("(jb p) d -> p jb d", p=128)[:, sl, :]),
            w=[(key, "nat", half)])
        if cc < 2:
            break
    for a in range(rc):
        for jb in range(cc):
            half = 0 if cc < 2 or jb < cc // 2 else 1
            P.mm([lambda e, a=a, jb=jb: e.transpose(
                ps.t[:, 0:128], nat.t[:, jb, a * 128:(a + 1) * 128], bufs["id_b"].ap())],
                r=[(key, "nat", half), "id_b"], w=[ps.name])
            P.op("v", lambda e, a=a, jb=jb: e.tensor_copy(
                out.t[:, a, jb * 128:(jb + 1) * 128], ps.t[:, 0:128]),
                r=[ps.name], w=[key])
    P.barrier()
    nat.free()
    return out


def emit_ln_xnT(P, nc, bufs, x_key, xbuf, out_xnT, out_xn_bf, ps_tr, prefix,
                w_rep=None, b_rep=None, store_xn=None):
    mu = Buf(nc, prefix + "_mu", [128, 4], F32)
    cen = Buf(nc, prefix + "_cen", [128, 1024], F32)
    sq = Buf(nc, prefix + "_sq", [128, 1024], F32)
    cb = Buf(nc, prefix + "_cb", [128, 1024], BF16)
    for i in range(NT):
        xk = (x_key, i)
        P.op("v", lambda e, i=i: e.reduce_sum(mu.t[:, 0:1], xbuf.t[:, i, :], axis=AX),
             r=[xk], w=[(prefix, "mu")])
        P.op("v", lambda e: e.tensor_scalar(mu.t[:, 1:2], mu.t[:, 0:1], 1.0 / D, None, ALU.mult),
             r=[(prefix, "mu")], w=[(prefix, "mu1")])
        P.op("v", lambda e, i=i: e.tensor_scalar(cen.ap(), xbuf.t[:, i, :], mu.t[:, 1:2], None, ALU.subtract),
             r=[xk, (prefix, "mu1")], w=[(prefix, "cen")])
        P.op("s", lambda e: e.activation(sq.ap(), cen.ap(), AF.Square, accum_out=mu.t[:, 2:3]),
             r=[(prefix, "cen")], w=[(prefix, "sq")])
        P.op("s", lambda e: e.activation(mu.t[:, 3:4], mu.t[:, 2:3], AF.Ln,
                                         bias=bufs["cst"].t[:, 0:1], scale=1.0 / D),
             r=[(prefix, "sq"), "cst"], w=[(prefix, "ln")])
        P.op("s", lambda e: e.activation(mu.t[:, 3:4], mu.t[:, 3:4], AF.Exp, scale=-0.5),
             r=[(prefix, "ln")], w=[(prefix, "rstd")])
        if w_rep is None:
            P.op("v", lambda e: e.tensor_scalar(cb.ap(), cen.ap(), mu.t[:, 3:4], None, ALU.mult),
                 r=[(prefix, "cen"), (prefix, "rstd")], w=[(prefix, "cb")])
        else:
            P.op("v", lambda e: e.tensor_scalar(cen.ap(), cen.ap(), mu.t[:, 3:4], None, ALU.mult),
                 r=[(prefix, "cen"), (prefix, "rstd")], w=[(prefix, "cen")])
            P.op("v", lambda e: e.tensor_tensor(cen.ap(), cen.ap(), w_rep.ap(), ALU.mult),
                 r=[(prefix, "cen"), w_rep.name], w=[(prefix, "cen")])
            P.op("v", lambda e: e.tensor_tensor(cb.ap(), cen.ap(), b_rep.ap(), ALU.add),
                 r=[(prefix, "cen"), b_rep.name], w=[(prefix, "cb")])
        if store_xn is not None:
            P.op("s", lambda e, i=i: e.copy(store_xn.t[:, i, :], cb.ap()),
                 r=[(prefix, "cb")], w=[(store_xn.name, i)])
        if out_xn_bf is not None:
            P.op("s", lambda e, i=i: e.copy(out_xn_bf.t[:, i, :], cb.ap()),
                 r=[(prefix, "cb")], w=[(out_xn_bf.name, i)])
        if out_xnT is not None:
            for a in range(DC):
                P.mm([lambda e, a=a: e.transpose(ps_tr.t[:, 0:128],
                                                 cb.t[:, a * 128:(a + 1) * 128], bufs["id_b"].ap())],
                     r=[(prefix, "cb"), "id_b"], w=[ps_tr.name])
                P.op("v", lambda e, a=a, i=i: e.tensor_copy(
                    out_xnT.t[:, a, i * 128:(i + 1) * 128], ps_tr.t[:, 0:128]),
                    r=[ps_tr.name], w=[(out_xnT.name, a, i)])
    P.barrier()
    cb.free(); sq.free(); cen.free(); mu.free()


def emit_phi(P, nc, out_bf, psum_ap, pkey, okey, tmp1, tmp2, prefix):
    P.op("v", lambda e: e.tensor_scalar(tmp1.ap(), psum_ap, 0.0, None, ALU.min),
         r=[pkey], w=[(prefix, "t1")])
    P.op("s", lambda e: e.activation(tmp2.ap(), tmp1.ap(), AF.Exp),
         r=[(prefix, "t1")], w=[(prefix, "t2")])
    P.op("v", lambda e: e.scalar_tensor_tensor(out_bf, psum_ap, 0.0, tmp2.ap(), ALU.max, ALU.add),
         r=[pkey, (prefix, "t2")], w=[okey])


def emit_l0_kv_pass(P, nc, bufs, snaps, stot_sb):
    xnT = bufs["xnT"]
    ps_k = Buf(nc, "ps_kv", [128, 1024], F32, psum=True)
    ps_v = ps_k
    ps_S = Buf(nc, "ps_S", [64, H * 128], F32, psum=True)
    scum = Buf(nc, "scum", [64, H * 65], F32)
    P.op("g", lambda e: e.memset(scum.t[0:64, :], 0.0), w=["scum"])
    phik = Buf(nc, "phik", [128, 1024], BF16)
    vaug = Buf(nc, "vaug", [128, H * 65], BF16)
    t1 = Buf(nc, "kv_t1", [128, 1024], F32)
    t2 = Buf(nc, "kv_t2", [128, 1024], F32)
    P.op("g", lambda e: e.memset(vaug.t[:].rearrange("p (h c) -> p h c", h=H)[:, :, 64:65], 1.0),
         w=[("vaug", "ones")])
    for i in range(NT):
        P.mm([(lambda e, a=a, half=half, i=i: e.matmul(
            ps_k.t[:, half * 512:(half + 1) * 512],
            xnT.t[:, a, i * 128:(i + 1) * 128],
            bufs["WkT"].t[:, a, half * 512:(half + 1) * 512],
            start=(a == 0), stop=(a == DC - 1)))
            for half in range(2) for a in range(DC)],
            r=[("xnT", a, i) for a in range(DC)] + ["WkT"], w=["ps_kv"])
        if i == 0 and "dbg_kps0" in bufs:
            P.op("s", lambda e: e.copy(bufs["dbg_kps0"].ap(), ps_k.ap()), r=["ps_kv"], w=["dkps0"])
        emit_phi(P, nc, phik.ap(), ps_k.ap(), "ps_kv", "phik", t1, t2, "kvphi")
        P.mm([(lambda e, a=a, half=half, i=i: e.matmul(
            ps_v.t[:, half * 512:(half + 1) * 512],
            xnT.t[:, a, i * 128:(i + 1) * 128],
            bufs["WvT"].t[:, a, half * 512:(half + 1) * 512],
            start=(a == 0), stop=(a == DC - 1)))
            for half in range(2) for a in range(DC)],
            r=[("xnT", a, i) for a in range(DC)] + ["WvT"], w=["ps_kv"])
        P.op("v", lambda e: e.tensor_copy(
            vaug.t[:].rearrange("p (h c) -> p h c", h=H)[:, :, 0:64],
            ps_v.t[:].rearrange("p (h c) -> p h c", h=H)),
            r=["ps_kv"], w=[("vaug", "data")])
        P.mm([(lambda e, h=h: e.matmul(
            ps_S.t[0:64, h * 128:h * 128 + 65],
            phik.t[:, h * 64:(h + 1) * 64],
            vaug.t[:, h * 65:(h + 1) * 65],
            start=True, stop=True))
            for h in range(H)],
            r=["phik", ("vaug", "data"), ("vaug", "ones")], w=["ps_S"])
        P.op("v", lambda e: e.tensor_tensor(
            scum.t[0:64, :].rearrange("p (h c) -> p h c", h=H),
            scum.t[0:64, :].rearrange("p (h c) -> p h c", h=H),
            ps_S.t[0:64, :].rearrange("p (h c) -> p h c", h=H)[:, :, 0:65],
            ALU.add),
            r=["ps_S", "scum"], w=["scum"])
        if snaps is not None and i == 0 and "dbg_phik0" in bufs:
            P.op("s", lambda e: e.copy(bufs["dbg_phik0"].ap(), phik.ap()), r=["phik"], w=["dphik0"])
            P.op("s", lambda e: e.copy(bufs["dbg_vaug0"].ap(), vaug.ap()),
                 r=[("vaug", "data"), ("vaug", "ones")], w=["dvaug0"])
        if snaps is not None:
            P.op("v", lambda e, i=i: e.tensor_copy(
                snaps.t[(i % 2) * 64:(i % 2) * 64 + 64, i // 2, :], scum.t[0:64, :]),
                 r=["scum"], w=[("snaps", i)])
    if stot_sb is not None:
        P.op("v", lambda e: e.tensor_copy(stot_sb.t[0:64, :], scum.t[0:64, :]),
             r=["scum"], w=["stot_sb"])
    P.barrier()
    t2.free(); t1.free(); vaug.free(); phik.free(); scum.free(); ps_S.free(); ps_k.free()



def build_A(dbg=False):
    nc = bass.Bass("TRN2", target_bir_lowering=False)
    x = nc.dram_tensor("x", [TPC, D], F32, kind="ExternalInput").ap()
    wk = nc.dram_tensor("Wk", [D, D], F32, kind="ExternalInput").ap()
    wv = nc.dram_tensor("Wv", [D, D], F32, kind="ExternalInput").ap()
    stot = nc.dram_tensor("stot", [64, H * 65], F32, kind="ExternalOutput").ap()
    snaps_o = nc.dram_tensor("snaps_o", [64, NT * H * 65], F32, kind="ExternalOutput").ap() if dbg else None
    phik0_o = nc.dram_tensor("phik0_o", [128, 1024], F32, kind="ExternalOutput").ap() if dbg else None
    vaug0_o = nc.dram_tensor("vaug0_o", [128, H * 65], F32, kind="ExternalOutput").ap() if dbg else None
    kps0_o = nc.dram_tensor("kps0_o", [128, 1024], F32, kind="ExternalOutput").ap() if dbg else None
    xnT_o = nc.dram_tensor("xnT_o", [128, 8 * 128], F32, kind="ExternalOutput").ap() if dbg else None
    wkT_o = nc.dram_tensor("wkT_o", [128, 1024], F32, kind="ExternalOutput").ap() if dbg else None
    P = Prog(nc)
    bufs = {}
    ps_tr = Buf(nc, "ps_tr", [128, 1024], BF16, psum=True)
    emit_const_tiles(P, nc, bufs)
    load_w_transposed(P, nc, bufs, "WkT", wk, D, D, ps_tr)
    load_w_transposed(P, nc, bufs, "WvT", wv, D, D, ps_tr)
    xr = Buf(nc, "xr", [128, NT, D], F32)
    for qq in range(4):
        P.dma("y", lambda e, qq=qq: e.dma_start(
            xr.t[:, qq * 4:(qq + 1) * 4, :],
            x.rearrange("(n p) d -> p n d", p=128)[:, qq * 4:(qq + 1) * 4, :]),
            w=[("xr", i) for i in range(qq * 4, qq * 4 + 4)])
    xnT = Buf(nc, "xnT", [128, DC, TPC], BF16)
    bufs["xnT"] = xnT
    emit_ln_xnT(P, nc, bufs, "xr", xr, xnT, None, ps_tr, "ln0")
    stot_sb = Buf(nc, "stot_sb", [64, H * 65], F32)
    snaps = Buf(nc, "snaps", [64, NT, H * 65], BF16) if dbg else None
    if dbg:
        bufs["dbg_phik0"] = Buf(nc, "dbg_phik0", [128, 1024], F32)
        bufs["dbg_vaug0"] = Buf(nc, "dbg_vaug0", [128, H * 65], F32)
        bufs["dbg_kps0"] = Buf(nc, "dbg_kps0", [128, 1024], F32)
    emit_l0_kv_pass(P, nc, bufs, snaps, stot_sb)
    P.dma("y", lambda e: e.dma_start(stot, stot_sb.t[0:64, :]), r=["stot_sb"], w=["stot_out"])
    if dbg:
        P.dma("g", lambda e: e.dma_start(snaps_o, snaps.t[0:64, :, :].rearrange("p n c -> p (n c)")),
              r=[("snaps", i) for i in range(NT)], w=["snaps_out"])
        P.dma("g", lambda e: e.dma_start(phik0_o, bufs["dbg_phik0"].ap()), r=["dphik0"], w=["o_p0"])
        P.dma("g", lambda e: e.dma_start(kps0_o, bufs["dbg_kps0"].ap()), r=["dkps0"], w=["o_k0"])
        for a in range(8):
            P.dma("g", lambda e, a=a: e.dma_start(xnT_o.rearrange("p (a t) -> p a t", a=8)[:, a, :],
                                                  xnT.t[:, a, 0:128]),
                  r=[("xnT", a, 0)], w=["o_xnT"])
        P.dma("g", lambda e: e.dma_start(wkT_o, bufs["WkT"].t[:, 0, :]), r=["WkT"], w=["o_wkT"])
        P.dma("g", lambda e: e.dma_start(vaug0_o, bufs["dbg_vaug0"].ap()), r=["dvaug0"], w=["o_v0"])
    P.barrier()
    P.finalize()
    P.close()
    return nc



def shard_inputs(inputs):
    x = np.ascontiguousarray(inputs["L0_attn_x"], dtype=np.float32)
    shards = []
    for c in range(8):
        b, q = c // 4, c % 4
        shards.append(np.ascontiguousarray(x[b, q * TPC:(q + 1) * TPC, :]))
    return shards



def emit_row_broadcast(P, nc, bufs, key, dram_row_ap, width, ps, dtype=F32):
    out = Buf(nc, key, [128, width], dtype)
    bufs[key] = out
    row = Buf(nc, key + "_row", [1, width], F32)
    P.dma("g", lambda e: e.dma_start(row.ap(), dram_row_ap), w=[(key, "row")])
    for c0 in range(0, width, 512):
        cw = min(512, width - c0)
        P.mm([lambda e, c0=c0, cw=cw: e.matmul(ps.t[:, 0:cw], bufs["ones"].t[0:1, :],
                                               row.t[:, c0:c0 + cw], start=True, stop=True)],
             r=[(key, "row"), "ones"], w=[ps.name])
        P.op("v", lambda e, c0=c0, cw=cw: e.tensor_copy(out.t[:, c0:c0 + cw], ps.t[:, 0:cw]),
             r=[ps.name], w=[key])
    P.barrier()
    row.free()
    return out


def emit_phi_block(P, nc, bufs, out_bf_ap, okey, psum_ap, pkey, t1, t2, prefix):
    emit_phi(P, nc, out_bf_ap, psum_ap, pkey, okey, t1, t2, prefix)


def emit_l0_attn_main(P, nc, bufs, snaps, offdup):
    xnT = bufs["xnT"]
    x_res = bufs["x_res"]
    proj_ps = Buf(nc, "proj_ps", [128, 1024], F32, psum=True)
    at_ps = Buf(nc, "at_ps", [128, 512], F32, psum=True)
    y_ps = Buf(nc, "y_ps", [128, H * 128], F32, psum=True)
    phiqT = Buf(nc, "phiqT", [128, DC, 128], BF16)
    phikT = Buf(nc, "phikT", [128, DC, 128], BF16)
    vaug = Buf(nc, "vaug2", [128, H * 65], BF16)
    comb = Buf(nc, "comb", [128, H * 65], BF16)
    at_sb = Buf(nc, "at_sb", [128, 128], BF16)
    y_nat = Buf(nc, "y_nat", [128, 1024], BF16)
    yT = Buf(nc, "yT", [128, DC, 128], BF16)
    den = Buf(nc, "den", [128, 3 * 16], F32)
    t1 = Buf(nc, "am_t1", [128, 128], F32)
    t2 = Buf(nc, "am_t2", [128, 128], F32)
    tri = Buf(nc, "tri", [128, 128], BF16)
    P.op("g", lambda e: e.memset(tri.ap(), 1.0), w=["tri"])
    P.op("g", lambda e: e.affine_select(tri.ap(), tri.ap(), [[1, 128]], ALU.is_ge,
                                        0.0, base=0, channel_multiplier=-1),
         r=["tri"], w=["tri"])
    P.op("g", lambda e: e.memset(vaug.t[:].rearrange("p (h c) -> p h c", h=H)[:, :, 64:65], 1.0),
         w=[("vaug2", "ones")])
    for i in range(NT):
        if i == 0:
            P.op("v", lambda e: e.tensor_copy(comb.ap(), offdup.ap()),
                 r=[offdup.name], w=["comb"])
        else:
            sb = ((i - 1) % 2) * 64
            blk = (i - 1) // 2
            P.mm([(lambda e, sb=sb, blk=blk, c0=c0: e.matmul(
                proj_ps.t[:, c0:c0 + 512], bufs["Jfull"].t[sb:sb + 64, :],
                snaps.t[sb:sb + 64, blk, c0:c0 + 512], start=True, stop=True))
                for c0 in (0, 512)] +
                [(lambda e, sb=sb, blk=blk: e.matmul(
                    at_ps.t[:, 0:16], bufs["Jfull"].t[sb:sb + 64, :],
                    snaps.t[sb:sb + 64, blk, 1024:1040], start=True, stop=True))],
                r=[("snaps", i - 1), "Jfull"],
                w=["proj_ps", "at_ps"])
            P.op("v", lambda e: e.tensor_tensor(comb.t[:, 0:1024], proj_ps.ap(),
                                                offdup.t[:, 0:1024], ALU.add),
                 r=["proj_ps"], w=["comb"])
            P.op("v", lambda e: e.tensor_tensor(comb.t[:, 1024:1040], at_ps.t[:, 0:16],
                                                offdup.t[:, 1024:1040], ALU.add),
                 r=["at_ps", offdup.name], w=["comb"])
        for which, wkey, outb, okey in (("q", "WqT", phiqT, "phiqT"), ("k", "WkT", phikT, "phikT")):
            for jt in range(DC):
                P.mm([(lambda e, a=a, jt=jt, i=i, wkey=wkey: e.matmul(
                    proj_ps.t[:, jt * 128:(jt + 1) * 128],
                    bufs[wkey].t[:, a, jt * 128:(jt + 1) * 128],
                    xnT.t[:, a, i * 128:(i + 1) * 128],
                    start=(a == 0), stop=(a == DC - 1)))
                    for a in range(DC)],
                    r=[("xnT", a, i) for a in range(DC)] + [wkey], w=["proj_ps"])
            for jt in range(DC):
                emit_phi(P, nc, outb.t[:, jt, :], proj_ps.t[:, jt * 128:(jt + 1) * 128],
                         "proj_ps", (okey, jt), t1, t2, "amphi" + which)
        P.mm([(lambda e, a=a, half=half, i=i: e.matmul(
            proj_ps.t[:, half * 512:(half + 1) * 512],
            xnT.t[:, a, i * 128:(i + 1) * 128],
            bufs["WvT"].t[:, a, half * 512:(half + 1) * 512],
            start=(a == 0), stop=(a == DC - 1)))
            for half in range(2) for a in range(DC)],
            r=[("xnT", a, i) for a in range(DC)] + ["WvT"],
            w=["proj_ps"])
        P.op("v", lambda e: e.tensor_copy(
            vaug.t[:].rearrange("p (h c) -> p h c", h=H)[:, :, 0:64],
            proj_ps.t[:].rearrange("p (h c) -> p h c", h=H)),
            r=["proj_ps"], w=[("vaug2", "data")])
        for h in range(H):
            jt, base = h // 2, (h % 2) * 64
            P.mm([lambda e, jt=jt, base=base: e.matmul(
                at_ps.t[:, 0:128],
                bufs_get(phikT, jt, base),
                bufs_get(phiqT, jt, base),
                start=True, stop=True)],
                r=[("phikT", jt), ("phiqT", jt)], w=["at_ps"])
            P.op("v", lambda e: e.tensor_tensor(at_sb.ap(), at_ps.t[:, 0:128], tri.ap(), ALU.mult),
                 r=["at_ps", "tri"], w=["at_sb"])
            P.mm([
                (lambda e, h=h: e.matmul(
                    y_ps.t[:, h * 128:h * 128 + 65],
                    at_sb.ap(),
                    vaug.t[:, h * 65:(h + 1) * 65],
                    start=True, stop=False)),
                (lambda e, h=h, jt=jt, base=base: e.matmul(
                    y_ps.t[:, h * 128:h * 128 + 65],
                    bufs_get(phiqT, jt, base),
                    comb.t[base:base + 64, h * 65:(h + 1) * 65],
                    start=False, stop=True)),
            ], r=["at_sb", ("vaug2", "data"), ("vaug2", "ones"), ("phiqT", jt), "comb"],
                w=[("y_ps", h)])
        P.op("v", lambda e: e.tensor_scalar(
            den.t[:, 0:16],
            y_ps.t[:].rearrange("p (h c) -> p h c", c=128)[:, :, 64],
            EPS_ATTN, None, ALU.max),
            r=[("y_ps", h) for h in range(H)], w=["den"])
        P.op("v", lambda e: e.reciprocal(den.t[:, 16:32], den.t[:, 0:16]),
             r=["den"], w=[("den", "rec")])
        P.op("v", lambda e: e.tensor_tensor(
            y_nat.t[:].rearrange("p (h c) -> p h c", c=64),
            y_ps.t[:].rearrange("p (h c) -> p h c", c=128)[:, :, 0:64],
            den.t[:, 16:32].unsqueeze(2).broadcast_to([128, 16, 64]),
            ALU.mult),
            r=[("y_ps", h) for h in range(H)] + [("den", "rec")], w=["y_nat"])
        for a in range(DC):
            P.mm([lambda e, a=a: e.transpose(bufs["ps_tr"].t[:, 0:128],
                                             y_nat.t[:, a * 128:(a + 1) * 128],
                                             bufs["id_b"].ap())],
                 r=["y_nat", "id_b"], w=["ps_tr"])
            P.op("v", lambda e, a=a: e.tensor_copy(yT.t[:, a, :], bufs["ps_tr"].t[:, 0:128]),
                 r=["ps_tr"], w=["yT"])
        P.mm([(lambda e, a=a, half=half: e.matmul(
            proj_ps.t[:, half * 512:(half + 1) * 512],
            yT.t[:, a, :],
            bufs["WoT"].t[:, a, half * 512:(half + 1) * 512],
            start=(a == 0), stop=(a == DC - 1)))
            for half in range(2) for a in range(DC)],
            r=["yT", "WoT"], w=["proj_ps"])
        P.op("v", lambda e, i=i: e.tensor_tensor(
            bufs["x_res"].t[:, i, :], bufs["x_res"].t[:, i, :], proj_ps.ap(), ALU.add),
            r=["proj_ps", ("x_res", i)], w=[("x_res", i)])
    P.barrier()
    tri.free(); t2.free(); t1.free(); den.free(); yT.free(); y_nat.free(); at_sb.free()
    comb.free(); vaug.free(); phikT.free(); phiqT.free()
    y_ps.free(); at_ps.free(); proj_ps.free()


def bufs_get(buf, jt, base):
    return buf.t[base:base + 64, jt, :]


def emit_gating(P, nc, bufs, c_store):
    xnT = bufs["xnT"]
    g_ps = Buf(nc, "g_ps", [128, 512], F32, psum=True)
    lg = Buf(nc, "lg", [128, 8], F32)
    eq1 = Buf(nc, "eq1", [128, 8], F32)
    t2b = Buf(nc, "t2b", [128, 8], F32)
    m = Buf(nc, "m12", [128, 8], F32)
    for i in range(NT):
        P.mm([(lambda e, a=a, i=i: e.matmul(
            g_ps.t[:, 0:8],
            xnT.t[:, a, i * 128:(i + 1) * 128],
            bufs["gate_sb"].t[:, a, :],
            start=(a == 0), stop=(a == DC - 1)))
            for a in range(DC)],
            r=[("xnT", a, i) for a in range(DC)] + ["gate_sb"], w=["g_ps"])
        P.op("v", lambda e: e.tensor_copy(lg.ap(), g_ps.t[:, 0:8]), r=["g_ps"], w=["lg"])
        P.op("v", lambda e: e.reduce_max(m.t[:, 0:1], lg.ap(), axis=AX), r=["lg"], w=[("m", 1)])
        P.op("v", lambda e: e.tensor_scalar(eq1.ap(), lg.ap(), m.t[:, 0:1], None, ALU.is_equal),
             r=["lg", ("m", 1)], w=["eq1"])
        P.op("v", lambda e: e.scalar_tensor_tensor(t2b.ap(), eq1.ap(), 1e30, lg.ap(),
                                                   ALU.mult, ALU.subtract),
             r=["eq1", "lg"], w=["t2b"])
        P.op("v", lambda e: e.tensor_reduce(m.t[:, 1:2], t2b.ap(), AX, ALU.min, negate=True),
             r=["t2b"], w=[("m", 2)])
        P.op("v", lambda e: e.tensor_tensor(m.t[:, 2:3], m.t[:, 1:2], m.t[:, 0:1], ALU.subtract),
             r=[("m", 1), ("m", 2)], w=[("m", 3)])
        P.op("s", lambda e: e.activation(m.t[:, 3:4], m.t[:, 2:3], AF.Sigmoid),
             r=[("m", 3)], w=[("m", "p1")])
        P.op("s", lambda e: e.activation(m.t[:, 4:5], m.t[:, 2:3], AF.Sigmoid, scale=-1.0),
             r=[("m", 3)], w=[("m", "p1b")])
        P.op("v", lambda e: e.tensor_scalar(m.t[:, 5:6], m.t[:, 4:5], -1.0, 1.0,
                                            ALU.mult, ALU.add),
             r=[("m", "p1b")], w=[("m", "p2")])
        P.op("v", lambda e: e.tensor_scalar(m.t[:, 6:7], lg.ap() if False else m.t[:, 1:2],
                                            0.0, None, ALU.add),
             r=[("m", 2)], w=[("m", "m2c")])
        P.op("v", lambda e: e.tensor_scalar(eq1.ap(), eq1.ap(), m.t[:, 4:5], None, ALU.mult),
             r=["eq1", ("m", "p1b")], w=["eq1"])
        P.op("v", lambda e: e.tensor_scalar(t2b.ap(), lg.ap(), m.t[:, 6:7], None, ALU.is_equal),
             r=["lg", ("m", "m2c")], w=["t2b"])
        P.op("v", lambda e, i=i: e.scalar_tensor_tensor(
            c_store.t[:, i * 8:(i + 1) * 8], t2b.ap(), m.t[:, 5:6], eq1.ap(),
            ALU.mult, ALU.add),
            r=["t2b", ("m", "p2"), "eq1"], w=[("c_store", i)])
    P.barrier()
    m.free(); t2b.free(); eq1.free(); lg.free(); g_ps.free()


def emit_moe(P, nc, bufs, w1_dram, w2_dram, c_store):
    xnT = bufs["xnT"]
    NE = 8
    GROUPS = [(0, 3), (3, 3), (6, 3), (9, 3), (12, 3), (15, 1)]
    w2_sb = Buf(nc, "w2_sb", [128, 32, 1024], BF16)
    w1_sb = Buf(nc, "w1_sb", [128, 4, DC, 128], BF16)
    hT_sb = Buf(nc, "hT_sb", [128, 2, 384], BF16)
    out_ps_l = [Buf(nc, "moe_out_ps%d" % j, [128, 1024], F32, psum=True) for j in range(3)]
    hT_ps_l = [Buf(nc, "hT_ps%d" % s, [128, 512], F32, psum=True) for s in range(2)]
    for e_i in range(NE):
        for q in range(4):
            P.dma("g", lambda e, e_i=e_i, q=q: e.dma_start(
                w2_sb.t[:, q * 8:(q + 1) * 8, :],
                w2_dram.rearrange("(ee n p) c -> ee p n c", ee=8, p=128)[e_i, :, q * 8:(q + 1) * 8, :]),
                w=[("w2_sb", q)])
        for gi, (g0, glen) in enumerate(GROUPS):
            for fc in range(32):
                slw = fc % 4
                sl = fc % 2
                P.dma("g", lambda e, e_i=e_i, fc=fc, slw=slw: e.dma_start(
                    w1_sb.t[:, slw, :, :],
                    w1_dram.rearrange("(ee a p) f -> ee p a f", ee=8, p=128)[e_i, :, :, fc * 128:(fc + 1) * 128]),
                    w=[("w1_sb", slw)])
                P.mm([(lambda e, a=a, sl=sl, slw=slw, g0=g0, glen=glen: e.matmul(
                    hT_ps_l[sl].t[:, 0:glen * 128],
                    w1_sb.t[:, slw, a, :],
                    xnT.t[:, a, g0 * 128:(g0 + glen) * 128],
                    start=(a == 0), stop=(a == DC - 1)))
                    for a in range(DC)],
                    r=[("w1_sb", slw)] + [("xnT", a, g0 + j) for a in range(DC) for j in range(glen)],
                    w=["hT_ps%d" % sl])
                P.op("s", lambda e, sl=sl, glen=glen: e.activation(
                    hT_sb.t[:, sl, 0:glen * 128], hT_ps_l[sl].t[:, 0:glen * 128], AF.Relu),
                    r=["hT_ps%d" % sl], w=[("hT_sb", sl)])
                P.mm([(lambda e, j=j, sl=sl, fc=fc, c0=c0: e.matmul(
                    out_ps_l[j].t[:, c0:c0 + 512],
                    hT_sb.t[:, sl, j * 128:(j + 1) * 128],
                    w2_sb.t[:, fc, c0:c0 + 512],
                    start=(fc == 0), stop=(fc == 31), skip_group_check=True))
                    for j in range(glen) for c0 in (0, 512)],
                    r=[("hT_sb", sl)] + [("w2_sb", q) for q in range(4)],
                    w=["moe_out_ps%d" % j for j in range(glen)])
            for j in range(glen):
                i = g0 + j
                P.op("v", lambda e, i=i, j=j, e_i=e_i: e.scalar_tensor_tensor(
                    bufs["x_res"].t[:, i, :],
                    out_ps_l[j].ap(),
                    c_store.t[:, i * 8 + e_i:i * 8 + e_i + 1],
                    bufs["x_res"].t[:, i, :],
                    ALU.mult, ALU.add),
                    r=["moe_out_ps%d" % j, ("c_store", i), ("x_res", i)], w=[("x_res", i)])
    P.barrier()
    for b_ in reversed(hT_ps_l):
        b_.free()
    for b_ in reversed(out_ps_l):
        b_.free()
    hT_sb.free(); w1_sb.free(); w2_sb.free()


def emit_softplus_col(P, nc, bufs, out_ap, okey, in_ap, ikey, rows, prefix):
    sp1 = bufs["sp_scratch"]
    P.op("s", lambda e: e.activation(sp1.t[0:rows, 0:1], in_ap, AF.Abs),
         r=[ikey], w=[("sp_sc", 1)])
    P.op("s", lambda e: e.activation(sp1.t[0:rows, 1:2], sp1.t[0:rows, 0:1], AF.Exp, scale=-1.0),
         r=[("sp_sc", 1)], w=[("sp_sc", 2)])
    P.op("s", lambda e: e.activation(sp1.t[0:rows, 2:3], sp1.t[0:rows, 1:2], AF.Ln,
                                     bias=bufs["cst"].t[0:rows, 1:2]),
         r=[("sp_sc", 2), "cst"], w=[("sp_sc", 3)])
    P.op("v", lambda e: e.tensor_scalar(sp1.t[0:rows, 3:4], in_ap, 0.0, None, ALU.max),
         r=[ikey], w=[("sp_sc", 4)])
    P.op("v", lambda e: e.tensor_tensor(out_ap, sp1.t[0:rows, 3:4], sp1.t[0:rows, 2:3], ALU.add),
         r=[("sp_sc", 3), ("sp_sc", 4)], w=[okey])


def emit_l1_preamble(P, nc, bufs, want, dram):
    x_res = bufs["x_res"]
    ln_ps = Buf(nc, "ln_ps", [128, 1024], F32, psum=True)
    z_ps = Buf(nc, "z_ps", [64, 1024], F32, psum=True)
    bufs["sp_scratch"] = Buf(nc, "sp_scratch", [64, 4], F32)
    tau = Buf(nc, "tau", [64, 4], F32)
    bufs["_tau"] = tau
    P.dma("g", lambda e: e.dma_start(tau.t[0:64, 0:1], dram["log_tau"].unsqueeze(1)),
         w=["ltau"])
    ldt = Buf(nc, "ldt", [1, 2], F32)
    bufs["_ldt"] = ldt
    P.dma("g", lambda e: e.dma_start(ldt.t[0:1, 0:1], dram["log_dt"].unsqueeze(1)),
         w=["ldt_in"])
    emit_softplus_col(P, nc, bufs, tau.t[0:64, 1:2], "tau_sp", tau.t[0:64, 0:1], "ltau", 64, "sp_t")
    emit_softplus_col(P, nc, bufs, ldt.t[0:1, 1:2], "dt_sp", ldt.t[0:1, 0:1], "ldt_in", 1, "sp_d")
    P.mm([lambda e: e.matmul(z_ps.t[0:64, 0:1], bufs["ones"].t[0:1, 0:64], ldt.t[0:1, 1:2],
                             start=True, stop=True)],
         r=["dt_sp", "ones"], w=["z_ps"])
    P.op("v", lambda e: e.tensor_tensor(tau.t[0:64, 2:3], tau.t[0:64, 1:2], z_ps.t[0:64, 0:1],
                                        ALU.mult),
         r=["tau_sp", "z_ps"], w=["a_col"])
    P.op("v", lambda e: e.tensor_scalar(tau.t[0:64, 3:4], tau.t[0:64, 2:3], -1.0, None, ALU.mult),
         r=["a_col"], w=["na_col"])
    E = Buf(nc, "E_small", [64, 256], F32)
    bufs["E_small"] = E
    P.op("g", lambda e: e.iota(E.t[0:64, :], [[1, 256]], base=0, channel_multiplier=0,
                               allow_small_or_imprecise_dtypes=True), w=["E_iota"])
    P.op("v", lambda e: e.tensor_scalar(E.t[0:64, :], E.t[0:64, :], tau.t[0:64, 3:4], None,
                                        ALU.mult),
         r=["E_iota", "na_col"], w=["E_lin"])
    P.op("s", lambda e: e.activation(E.t[0:64, :], E.t[0:64, :], AF.Exp),
         r=["E_lin"], w=["E_small"])
    Eb = Buf(nc, "E_small_b", [64, 256], BF16)
    bufs["E_small_b"] = Eb
    P.op("v", lambda e: e.tensor_copy(Eb.t[0:64, :], E.t[0:64, :]),
         r=["E_small"], w=["E_small_b"])
    bufs["na_col_ap"] = tau.t[0:64, 3:4]
    erev = Buf(nc, "erev", [64, 128], BF16)
    bufs["_erev"] = erev
    wsn = Buf(nc, "wsn", [128, 64], BF16)
    bufs["wsn"] = wsn
    P.op("g", lambda e: e.iota(erev.t[0:64, :], [[0, 4], [-1, 32]], base=31,
                               channel_multiplier=0, allow_small_or_imprecise_dtypes=True),
         w=["erev"])
    P.op("v", lambda e: e.tensor_scalar(erev.t[0:64, :], erev.t[0:64, :], tau.t[0:64, 3:4],
                                        None, ALU.mult),
         r=["erev", "na_col"], w=["erev"])
    P.op("s", lambda e: e.activation(erev.t[0:64, :], erev.t[0:64, :], AF.Exp),
         r=["erev"], w=["erev"])
    ps_trb = Buf(nc, "ps_trb", [128, 1024], BF16, psum=True)
    P.mm([lambda e: e.transpose(ps_trb.t[:, 0:64], erev.t[0:64, :], bufs["id_b"].t[0:64, 0:64])],
         r=["erev", "id_b"], w=["ps_trb"])
    P.op("v", lambda e: e.tensor_copy(wsn.ap(), ps_trb.t[:, 0:64]),
         r=["ps_trb"], w=["wsn"])
    P.barrier()
    ps_trb.free()
    lam = Buf(nc, "lam", [64, 2], F32)
    bufs["lam"] = lam
    P.op("v", lambda e: e.tensor_copy(lam.t[0:64, 0:1], E.t[0:64, 32:33]),
         r=["E_small"], w=["lam32"])
    n1w = emit_row_broadcast(P, nc, bufs, "n1w_rep", dram["norm1_w"].unsqueeze(0), 1024, ln_ps)
    n1b = emit_row_broadcast(P, nc, bufs, "n1b_rep", dram["norm1_b"].unsqueeze(0), 1024, ln_ps)
    rrep = emit_row_broadcast(P, nc, bufs, "r_rep", dram["routing"].unsqueeze(0), 1024, ln_ps)
    srun = Buf(nc, "srun", [64, 1024], F32)
    if want.get("snaps"):
        bufs["psnap_sb"] = Buf(nc, "psnap_sb", [64, 8, 1024], BF16)
    if want.get("store"):
        bufs["hn0b"] = Buf(nc, "hn0b", [128, 1024], BF16)
        bufs["hn0stg"] = Buf(nc, "hn0stg", [128, DC, 128], BF16)
    mu = Buf(nc, "p_mu", [128, 6], F32)
    xn1f = Buf(nc, "xn1f", [128, 1024], F32)
    uf = Buf(nc, "uf", [128, 1024], F32)
    ub = Buf(nc, "ub", [128, 1024], BF16)
    s_sb = bufs.get("s_sb")
    P.op("g", lambda e: e.memset(srun.t[0:64, :], 0.0), w=["srun"])
    store = want.get("store", False)
    for i in range(NT):
        P.op("v", lambda e, i=i: e.reduce_sum(mu.t[:, 0:1], x_res.t[:, i, :], axis=AX),
             r=[("x_res", i)], w=[("pmu", 1)])
        P.op("v", lambda e: e.tensor_scalar(mu.t[:, 1:2], mu.t[:, 0:1], 1.0 / D, None, ALU.mult),
             r=[("pmu", 1)], w=[("pmu", 2)])
        P.op("v", lambda e, i=i: e.tensor_scalar(xn1f.ap(), x_res.t[:, i, :], mu.t[:, 1:2],
                                                 None, ALU.subtract),
             r=[("x_res", i), ("pmu", 2)], w=["xn1f"])
        P.op("s", lambda e: e.activation(uf.ap(), xn1f.ap(), AF.Square, accum_out=mu.t[:, 2:3]),
             r=["xn1f"], w=[("pmu", 3), "uf"])
        P.op("s", lambda e: e.activation(mu.t[:, 3:4], mu.t[:, 2:3], AF.Ln,
                                         bias=bufs["cst"].t[:, 0:1], scale=1.0 / D),
             r=[("pmu", 3), "cst"], w=[("pmu", 4)])
        P.op("s", lambda e: e.activation(mu.t[:, 3:4], mu.t[:, 3:4], AF.Exp, scale=-0.5),
             r=[("pmu", 4)], w=[("pmu", 5)])
        P.op("v", lambda e: e.tensor_scalar(xn1f.ap(), xn1f.ap(), mu.t[:, 3:4], None, ALU.mult),
             r=["xn1f", ("pmu", 5)], w=["xn1f"])
        if store:
            P.op("s", lambda e, i=i: e.copy(bufs["xm"].t[:, i, :], xn1f.ap()),
                 r=["xn1f"], w=[("xm_init", i)])
            P.op("v", lambda e: e.tensor_tensor(mu.t[:, 4:5], mu.t[:, 3:4], mu.t[:, 3:4], ALU.mult),
                 r=[("pmu", 5)], w=[("pmu", 6)])
            P.op("v", lambda e: e.tensor_tensor(mu.t[:, 4:5], mu.t[:, 4:5], mu.t[:, 2:3], ALU.mult),
                 r=[("pmu", 6), ("pmu", 3)], w=[("pmu", 7)])
            P.op("s", lambda e: e.activation(mu.t[:, 5:6], mu.t[:, 4:5], AF.Ln),
                 r=[("pmu", 7)], w=[("pmu", 8)])
            P.op("s", lambda e: e.activation(mu.t[:, 5:6], mu.t[:, 5:6], AF.Exp, scale=0.5),
                 r=[("pmu", 8)], w=[("pmu", 9)])
            P.op("v", lambda e: e.tensor_scalar(mu.t[:, 5:6], mu.t[:, 5:6], 1e-12, None, ALU.max),
                 r=[("pmu", 9)], w=[("pmu", 10)])
            P.op("v", lambda e: e.reciprocal(mu.t[:, 4:5], mu.t[:, 5:6]),
                 r=[("pmu", 10)], w=[("pmu", 11)])
            P.op("v", lambda e, i=i: e.tensor_scalar(bufs["st_sb"].t[:, i:i + 1], mu.t[:, 4:5],
                                                     32.0, None, ALU.mult),
                 r=[("pmu", 11)], w=[("st_sb", i)])
            P.op("v", lambda e, i=i: e.tensor_scalar(bufs["hn0b"].ap(), xn1f.ap(),
                                                     bufs["st_sb"].t[:, i:i + 1], None, ALU.mult),
                 r=["xn1f", ("st_sb", i)], w=["hn0b"])
            for a in range(DC):
                P.mm([lambda e, a=a: e.transpose(bufs["ps_trb3"].t[:, 0:128],
                                                 bufs["hn0b"].t[:, a * 128:(a + 1) * 128],
                                                 bufs["id_b"].ap())],
                     r=["hn0b", "id_b"], w=["ps_trb3"])
                P.op("v", lambda e, a=a: e.tensor_copy(bufs["hn0stg"].t[:, a, :],
                                                       bufs["ps_trb3"].t[:, 0:128]),
                     r=["ps_trb3"], w=[("hn0stg", a)])
            P.dma("y", lambda e, i=i: e.dma_start(
                dram["hn0T"].rearrange("(a p) t -> p a t", p=128)[:, :, i * 128:(i + 1) * 128],
                bufs["hn0stg"].ap()),
                r=[("hn0stg", a) for a in range(DC)], w=[("hn0T_out", i)])
        P.op("v", lambda e: e.tensor_tensor(uf.ap(), xn1f.ap(), rrep.ap(), ALU.mult),
             r=["xn1f", "r_rep"], w=["uf"])
        P.op("v", lambda e, i=i: e.reduce_sum(s_sb.t[:, i:i + 1], uf.ap(), axis=AX),
             r=["uf"], w=[("s_sb", i)])
        P.op("v", lambda e, i=i: e.reduce_sum(mu.t[:, 0:1], xn1f.ap(), axis=AX),
             r=["xn1f"], w=[("pmu", 1)])
        P.op("v", lambda e: e.tensor_scalar(mu.t[:, 1:2], mu.t[:, 0:1], 1.0 / D, None, ALU.mult),
             r=[("pmu", 1)], w=[("pmu", 2)])
        P.op("v", lambda e: e.tensor_scalar(uf.ap(), xn1f.ap(), mu.t[:, 1:2], None, ALU.subtract),
             r=["xn1f", ("pmu", 2)], w=["uf"])
        P.op("s", lambda e: e.activation(ub.ap(), uf.ap(), AF.Square, accum_out=mu.t[:, 2:3]),
             r=["uf"], w=[("pmu", 3), ("ub", "scratch")])
        P.op("s", lambda e: e.activation(mu.t[:, 3:4], mu.t[:, 2:3], AF.Ln,
                                         bias=bufs["cst"].t[:, 0:1], scale=1.0 / D),
             r=[("pmu", 3), "cst"], w=[("pmu", 4)])
        P.op("s", lambda e: e.activation(mu.t[:, 3:4], mu.t[:, 3:4], AF.Exp, scale=-0.5),
             r=[("pmu", 4)], w=[("pmu", 5)])
        P.op("v", lambda e: e.tensor_scalar(uf.ap(), uf.ap(), mu.t[:, 3:4], None, ALU.mult),
             r=["uf", ("pmu", 5)], w=["uf"])
        P.op("v", lambda e: e.tensor_tensor(uf.ap(), uf.ap(), n1w.ap(), ALU.mult),
             r=["uf", "n1w_rep"], w=["uf"])
        P.op("v", lambda e: e.tensor_tensor(ub.ap(), uf.ap(), n1b.ap(), ALU.add),
             r=["uf", "n1b_rep"], w=[("ub", "data")])
        if store:
            P.op("s", lambda e, i=i: e.copy(bufs["u_bf"].t[:, i, :], ub.ap()),
                 r=[("ub", "data")], w=[("u_bf", i)])

        for j in range(NSUB):
            P.mm([(lambda e, j=j, c0=c0: e.matmul(
                z_ps.t[0:64, c0:c0 + 512],
                bufs["wsn"].t[j * 32:(j + 1) * 32, :],
                ub.t[j * 32:(j + 1) * 32, c0:c0 + 512],
                start=True, stop=True, tile_position=(j * 32, 0)))
                for c0 in (0, 512)],
                r=[("ub", "data"), "wsn"], w=["z_ps"])
            P.op("v", lambda e: e.scalar_tensor_tensor(
                srun.t[0:64, :], srun.t[0:64, :], lam.t[0:64, 0:1], z_ps.t[0:64, :],
                ALU.mult, ALU.add),
                r=["z_ps", "srun", "lam32"], w=["srun"])
            if want.get("snaps"):
                ss = i * NSUB + j
                P.op("s", lambda e, ss=ss: e.copy(
                    bufs["psnap_sb"].t[0:64, ss % 8, :], srun.t[0:64, :]),
                    r=["srun"], w=[("psnap_sb", ss % 8)])
                if ss % 8 == 7:
                    blk = ss // 8
                    P.dma("y", lambda e, blk=blk: e.dma_start(
                        dram["psnap"][0:64, blk * 8192:(blk + 1) * 8192],
                        bufs["psnap_sb"].t[0:64, :, :].rearrange("p k c -> p (k c)")),
                        r=[("psnap_sb", q) for q in range(8)], w=[("psnap_out", blk)])
    bufs["srun"] = srun
    P.barrier()
    ub.free(); uf.free(); xn1f.free(); mu.free()
    if want.get("free_tmp"):
        if want.get("store"):
            bufs["hn0stg"].free(); bufs["hn0b"].free()
        if want.get("snaps"):
            bufs["psnap_sb"].free()
        srun.free()
        rrep.free(); n1b.free(); n1w.free()
    z_ps.free(); ln_ps.free()



def build_B():
    nc = bass.Bass("TRN2", target_bir_lowering=False)
    x = nc.dram_tensor("x", [TPC, D], F32, kind="ExternalInput").ap()
    wq = nc.dram_tensor("Wq", [D, D], F32, kind="ExternalInput").ap()
    wk = nc.dram_tensor("Wk", [D, D], F32, kind="ExternalInput").ap()
    wv = nc.dram_tensor("Wv", [D, D], F32, kind="ExternalInput").ap()
    wo = nc.dram_tensor("Wo", [D, D], F32, kind="ExternalInput").ap()
    gate_w = nc.dram_tensor("gate_w", [D, 8], F32, kind="ExternalInput").ap()
    w1 = nc.dram_tensor("w1", [8 * D, 4096], F32, kind="ExternalInput").ap()
    w2 = nc.dram_tensor("w2", [8 * 4096, D], F32, kind="ExternalInput").ap()
    stot_all = nc.dram_tensor("stot_all", [8 * 64, H * 65], F32, kind="ExternalInput").ap()
    pred_mask = nc.dram_tensor("pred_mask", [1, 8], F32, kind="ExternalInput").ap()
    log_tau = nc.dram_tensor("log_tau", [NST], F32, kind="ExternalInput").ap()
    log_dt = nc.dram_tensor("log_dt", [1], F32, kind="ExternalInput").ap()
    norm1_w = nc.dram_tensor("norm1_w", [D], F32, kind="ExternalInput").ap()
    norm1_b = nc.dram_tensor("norm1_b", [D], F32, kind="ExternalInput").ap()
    routing = nc.dram_tensor("routing", [D], F32, kind="ExternalInput").ap()
    x1_out = nc.dram_tensor("x1_out", [TPC, D], F32, kind="ExternalOutput").ap()
    zpack = nc.dram_tensor("zpack", [66, 1024], F32, kind="ExternalOutput").ap()
    dram = {"log_tau": log_tau, "log_dt": log_dt, "norm1_w": norm1_w,
            "norm1_b": norm1_b, "routing": routing}

    P = Prog(nc)
    bufs = {}
    emit_const_tiles(P, nc, bufs)
    bufs["x_res"] = x_res = Buf(nc, "x_res", [128, NT, D], F32)
    bufs["xnT"] = xnT = Buf(nc, "xnT", [128, DC, TPC], BF16)
    bufs["s_sb"] = Buf(nc, "s_sb", [128, NT], F32)
    c_store = Buf(nc, "c_store", [128, NT * 8], F32)
    bufs["ps_tr"] = ps_tr = Buf(nc, "ps_tr", [128, 1024], BF16, psum=True)
    for qq in range(4):
        P.dma("y", lambda e, qq=qq: e.dma_start(
            x_res.t[:, qq * 4:(qq + 1) * 4, :],
            x.rearrange("(n p) d -> p n d", p=128)[:, qq * 4:(qq + 1) * 4, :]),
            w=[("x_res", i) for i in range(qq * 4, qq * 4 + 4)])
    bufs["gate_sb"] = gate_sb = Buf(nc, "gate_sb", [128, DC, 8], BF16)
    P.dma("g", lambda e: e.dma_start(gate_sb.ap(),
                                     gate_w.rearrange("(a p) e -> p a e", p=128)),
          w=["gate_sb"])
    load_w_transposed(P, nc, bufs, "WkT", wk, D, D, ps_tr)
    load_w_transposed(P, nc, bufs, "WvT", wv, D, D, ps_tr)
    emit_ln_xnT(P, nc, bufs, "x_res", x_res, xnT, None, ps_tr, "ln0")
    snaps = Buf(nc, "snaps", [128, NT // 2, H * 65], BF16)
    emit_l0_kv_pass(P, nc, bufs, snaps, None)
    bufs["Jfull"] = Jf = Buf(nc, "Jfull", [128, 128], BF16)
    for pb in (0, 64):
        for fb in (0, 64):
            P.op("v", lambda e, pb=pb, fb=fb: e.tensor_copy(
                Jf.t[pb:pb + 64, fb:fb + 64], bufs["id_b"].t[pb:pb + 64, pb:pb + 64]),
                r=["id_b"], w=["Jfull"])
    offdup = Buf(nc, "offdup", [128, H * 65], F32)
    stin = Buf(nc, "stin", [64, 8, H * 65], F32)
    msk = Buf(nc, "msk", [64, 8], F32)
    mrow = Buf(nc, "mrow", [1, 8], F32)
    offb = Buf(nc, "offb", [64, H * 65], F32)
    offc = Buf(nc, "offc", [64, H * 65], F32)
    ps_off = Buf(nc, "ps_off", [64, 512], F32, psum=True)
    ps_dup = Buf(nc, "ps_dup", [128, 1536], F32, psum=True)
    for r in range(8):
        P.dma("y", lambda e, r=r: e.dma_start(stin.t[0:64, r, :],
                                              stot_all[r * 64:(r + 1) * 64, :]),
              w=[("stin", r)])
    P.dma("g", lambda e: e.dma_start(mrow.ap(), pred_mask), w=["mrow"])
    P.mm([lambda e: e.matmul(ps_off.t[0:64, 0:8], bufs["ones"].t[0:1, 0:64], mrow.ap(),
                             start=True, stop=True)],
         r=["mrow", "ones"], w=["ps_off"])
    P.op("v", lambda e: e.tensor_copy(msk.t[0:64, :], ps_off.t[0:64, 0:8]),
         r=["ps_off"], w=["msk"])
    P.op("g", lambda e: e.memset(offb.t[0:64, :], 0.0), w=[("off", 0)])
    for r in range(8):
        src_b, dst_b = (offb, offc) if r % 2 == 0 else (offc, offb)
        P.op("v", lambda e, r=r, src_b=src_b, dst_b=dst_b: e.scalar_tensor_tensor(
            dst_b.t[0:64, :],
            stin.t[0:64, r, :],
            msk.t[0:64, r:r + 1],
            src_b.t[0:64, :],
            ALU.mult, ALU.add),
            r=[("stin", r), "msk", ("off", r)], w=[("off", r + 1)])
    offcb = offb
    ob16 = Buf(nc, "ob16", [64, H * 65], BF16)
    P.op("v", lambda e: e.tensor_copy(ob16.t[0:64, :], offc.t[0:64, :]),
         r=[("off", 8)], w=["ob16"])
    P.mm([(lambda e, c0=c0, cw=cw: e.matmul(ps_dup.t[:, c0:c0 + cw], Jf.t[0:64, :],
                                            ob16.t[0:64, c0:c0 + cw], start=True, stop=True))
          for c0, cw in ((0, 512), (512, 512), (1024, 16))],
         r=["ob16", "Jfull"], w=["ps_dup"])
    P.op("v", lambda e: e.tensor_copy(offdup.ap(), ps_dup.t[:, 0:1040]),
         r=["ps_dup"], w=["offdup"])
    P.barrier()
    ps_dup.free(); ps_off.free(); ob16.free(); offc.free(); offb.free(); mrow.free(); msk.free(); stin.free()
    load_w_transposed(P, nc, bufs, "WqT", wq, D, D, ps_tr)
    load_w_transposed(P, nc, bufs, "WoT", wo, D, D, ps_tr)
    emit_l0_attn_main(P, nc, bufs, snaps, offdup)
    P.barrier()
    bufs["WoT"].free(); bufs["WqT"].free()
    offdup.free(); bufs["Jfull"].free(); snaps.free()
    bufs["WvT"].free(); bufs["WkT"].free()
    emit_gating(P, nc, bufs, c_store)
    ps_tr.free()
    emit_moe(P, nc, bufs, w1, w2, c_store)
    emit_l1_preamble(P, nc, bufs, {"snaps": False, "store": False},
                     dict(dram, psnap=None))
    for qq in range(4):
        P.dma("y", lambda e, qq=qq: e.dma_start(
            x1_out.rearrange("(n p) d -> p n d", p=128)[:, qq * 4:(qq + 1) * 4, :],
            x_res.t[:, qq * 4:(qq + 1) * 4, :]),
            r=[("x_res", i) for i in range(qq * 4, qq * 4 + 4)], w=[("x1o", qq)])
    P.dma("y", lambda e: e.dma_start(zpack[0:64, :], bufs["srun"].t[0:64, :]),
          r=["srun"], w=["zp1"])
    P.dma("y", lambda e: e.dma_start(
        zpack[64:66, :].rearrange("r c -> (r c)").rearrange("(p f) -> p f", p=128),
        bufs["s_sb"].ap()),
        r=[("s_sb", i) for i in range(NT)], w=["zp2"])
    P.barrier()
    P.finalize()
    P.close()
    return nc




def emit_stream_transpose_w(P, nc, bufs, key, dram_in, scratch, R, C, ps_b, unused=None):
    cb = C // 128
    stg = Buf(nc, key + "_stg", [128, cb, 128], BF16)
    nat = Buf(nc, key + "_nat", [128, cb, 128], BF16)
    for rb in range(R // 128):
        P.dma("g", lambda e, rb=rb: e.dma_start(
            nat.ap(),
            dram_in.rearrange("(a p) f -> p a f", p=128)[:, rb, :].rearrange(
                "p (cb q) -> p cb q", q=128)),
            w=[(key, "nat")])
        for a in range(cb):
            P.mm([lambda e, a=a: e.transpose(ps_b.t[:, 0:128], nat.t[:, a, :], bufs["id_b"].ap())],
                 r=[(key, "nat"), "id_b"], w=[ps_b.name])
            P.op("v", lambda e, a=a: e.tensor_copy(stg.t[:, a, :], ps_b.t[:, 0:128]),
                 r=[ps_b.name], w=[(key, "stg")])
        P.dma("y", lambda e, rb=rb: e.dma_start(
            scratch.rearrange("(cb p) r -> p cb r", p=128)[:, :, rb * 128:(rb + 1) * 128],
            stg.ap()),
            r=[(key, "stg")], w=[(key, "out", rb)])
    P.barrier()
    nat.free(); stg.free()


def emit_l1_rest(P, nc, bufs, dram):
    x_res = bufs["x_res"]
    xm = bufs["xm"]
    ps_trb = Buf(nc, "ps_trb2", [128, 1024], BF16, psum=True)
    mixT = Buf(nc, "mixT", [64, 1024], BF16)
    mixn = Buf(nc, "mixn", [128, 8, 64], BF16)
    P.dma("g", lambda e: e.dma_start(mixn.ap(),
                                     dram["mix"].rearrange("(a p) n -> p a n", p=128)),
          w=["mixn"])
    for a in range(8):
        P.mm([lambda e, a=a: e.transpose(ps_trb.t[0:64, 0:128], mixn.t[:, a, :],
                                         bufs["id_b"].t[0:128, 0:128])],
             r=["mixn", "id_b"], w=["ps_trb2"])
        P.op("v", lambda e, a=a: e.tensor_copy(mixT.t[0:64, a * 128:(a + 1) * 128],
                                               ps_trb.t[0:64, 0:128]),
             r=["ps_trb2"], w=["mixT"])
    P.barrier()
    mixn.free()
    lampow = Buf(nc, "lampow", [64, 64], F32)
    P.op("g", lambda e: e.iota(lampow.t[0:64, :], [[32, 64]], base=32, channel_multiplier=0,
                               allow_small_or_imprecise_dtypes=True), w=["lampow"])
    P.op("v", lambda e: e.tensor_scalar(lampow.t[0:64, :], lampow.t[0:64, :],
                                        bufs["na_col_ap"], None, ALU.mult),
         r=["lampow", "na_col"], w=["lampow"])
    P.op("s", lambda e: e.activation(lampow.t[0:64, :], lampow.t[0:64, :], AF.Exp),
         r=["lampow"], w=["lampow"])
    dcol = Buf(nc, "dcol", [128, 8], F32)
    P.dma("y", lambda e: e.dma_start(dcol.ap(), dram["Dvec"].rearrange("(a p) -> p a", p=128), allow_slow_non_contiguous=True),
          w=["dcol"])
    ps_it = Buf(nc, "ps_it", [128, 512], F32, psum=True)
    ps_trf = Buf(nc, "ps_trf", [128, 512], F32, psum=True)
    uT = Buf(nc, "uT", [128, 4, TPC], BF16)
    yT = Buf(nc, "yT2", [128, 4, TPC], F32)
    K_sb = Buf(nc, "K_sb", [128, 4, 32], F32)
    dk0 = Buf(nc, "dk0", [128, 4], F32)
    psn = Buf(nc, "psn", [64, 8, 512], BF16)
    pcor = Buf(nc, "pcor", [64, 512], BF16)
    pb_ = Buf(nc, "pb", [64, 512], BF16)
    for hl in range(2):
        for i in range(NT):
            for cl in range(4):
                cc = hl * 4 + cl
                P.mm([lambda e, i=i, cc=cc: e.transpose(
                    ps_trb.t[:, 0:128], bufs["u_bf"].t[:, i, cc * 128:(cc + 1) * 128],
                    bufs["id_b"].ap())],
                    r=[("u_bf", i), "id_b"], w=["ps_trb2"])
                P.op("v", lambda e, i=i, cl=cl: e.tensor_copy(
                    uT.t[:, cl, i * 128:(i + 1) * 128], ps_trb.t[:, 0:128]),
                    r=["ps_trb2"], w=[("uT", cl)])
        for cl in range(4):
            cc = hl * 4 + cl
            P.mm([lambda e, cc=cc: e.matmul(ps_it.t[:, 0:32], mixT.t[0:64, cc * 128:(cc + 1) * 128],
                                            bufs["E_small_b"].t[0:64, 0:32],
                                            start=True, stop=True)],
                 r=["mixT", "E_small_b"], w=["ps_it"])
            P.op("v", lambda e, cl=cl: e.tensor_copy(K_sb.t[:, cl, :], ps_it.t[:, 0:32]),
                 r=["ps_it"], w=[("K_sb", cl)])
            P.op("v", lambda e, cl=cl, cc=cc: e.tensor_tensor(
                dk0.t[:, cl:cl + 1], dcol.t[:, cc:cc + 1], K_sb.t[:, cl, 0:1], ALU.add),
                r=[("K_sb", cl), "dcol"], w=[("dk0", cl)])
        for cl in range(4):
            P.op("v", lambda e, cl=cl: e.tensor_scalar(
                yT.t[:, cl, :], uT.t[:, cl, :], dk0.t[:, cl:cl + 1], None, ALU.mult),
                r=[("uT", cl), ("dk0", cl)], w=[("yT", cl)])
        for d in range(1, LSUB):
            for cl in range(4):
                P.op("v", lambda e, d=d, cl=cl: e.scalar_tensor_tensor(
                    yT.t[:, cl, :].rearrange("p (s q) -> p s q", q=LSUB)[:, :, d:LSUB],
                    uT.t[:, cl, :].rearrange("p (s q) -> p s q", q=LSUB)[:, :, 0:LSUB - d],
                    K_sb.t[:, cl, d:d + 1],
                    yT.t[:, cl, :].rearrange("p (s q) -> p s q", q=LSUB)[:, :, d:LSUB],
                    ALU.mult, ALU.add),
                    r=[("uT", cl), ("K_sb", cl), ("yT", cl)], w=[("yT", cl)])
        for i in range(NT):
            if i % 2 == 0:
                blk = i // 2
                P.dma("y", lambda e, blk=blk, hl=hl: e.dma_start(
                    psn.ap(),
                    dram["psnap"].rearrange("p (ss c) -> p ss c", c=1024)[0:64, blk * 8:(blk + 1) * 8, hl * 512:(hl + 1) * 512]),
                    w=["psn"])
            for j in range(NSUB):
                ss = i * NSUB + j
                P.op("v", lambda e, ss=ss, hl=hl: e.scalar_tensor_tensor(
                    pcor.t[0:64, :],
                    bufs["S_in"].t[0:64, hl * 512:(hl + 1) * 512],
                    lampow.t[0:64, ss:ss + 1],
                    psn.t[0:64, ss % 8, :],
                    ALU.mult, ALU.add),
                    r=["S_in", "lampow", "psn"], w=["pcor"])
                P.op("v", lambda e, hl=hl: e.tensor_tensor(
                    pb_.t[0:64, :], pcor.t[0:64, :],
                    mixT.t[0:64, hl * 512:(hl + 1) * 512], ALU.mult),
                    r=["pcor", "mixT"], w=["pb"])
                P.mm([(lambda e, cl=cl, j=j: e.matmul(
                    ps_it.t[:, cl * 128 + j * 32:cl * 128 + j * 32 + 32],
                    pb_.t[0:64, cl * 128:(cl + 1) * 128],
                    bufs["E_small_b"].t[0:64, 1:33],
                    start=True, stop=True))
                    for cl in range(4)],
                    r=["pb", "E_small_b"], w=["ps_it"])
            for cl in range(4):
                P.op("v", lambda e, cl=cl, i=i: e.tensor_tensor(
                    yT.t[:, cl, i * 128:(i + 1) * 128].rearrange("p (s q) -> p s q", q=32),
                    yT.t[:, cl, i * 128:(i + 1) * 128].rearrange("p (s q) -> p s q", q=32),
                    ps_it.t[:, cl * 128:(cl + 1) * 128].rearrange("p (s q) -> p s q", q=32),
                    ALU.add),
                    r=["ps_it", ("yT", cl)], w=[("yT", cl)])
        for i in range(NT):
            for cl in range(4):
                cc = hl * 4 + cl
                P.mm([lambda e, i=i, cl=cl: e.transpose(
                    ps_trf.t[:, 0:128], yT.t[:, cl, i * 128:(i + 1) * 128], bufs["id_f"].ap())],
                    r=[("yT", cl), "id_f"], w=["ps_trf"])
                P.op("v", lambda e, i=i, cc=cc: e.tensor_tensor(
                    xm.t[:, i, cc * 128:(cc + 1) * 128], ps_trf.t[:, 0:128],
                    xm.t[:, i, cc * 128:(cc + 1) * 128], ALU.add),
                    r=["ps_trf", ("xm_init", i)], w=[("xm", i, cc)])
    P.barrier()
    if DBG_XM and "xm_o" in dram:
        for qq in range(4):
            P.dma("g", lambda e, qq=qq: e.dma_start(
                dram["xm_o"].rearrange("(n p) d -> p n d", p=128)[:, qq * 4:(qq + 1) * 4, :],
                xm.t[:, qq * 4:(qq + 1) * 4, :]),
                r=[("xm", i, cc) for i in range(qq * 4, qq * 4 + 4) for cc in range(8)],
                w=[("xmo", qq)])
    pb_.free(); pcor.free(); psn.free(); dk0.free(); K_sb.free(); yT.free(); uT.free()
    ps_trf.free(); ps_it.free()

    ps_rb = Buf(nc, "ps_rb", [128, 512], F32, psum=True)
    n2w = emit_row_broadcast(P, nc, bufs, "n2w_rep", dram["norm2_w"].unsqueeze(0), 1024, ps_rb)
    n2b = emit_row_broadcast(P, nc, bufs, "n2b_rep", dram["norm2_b"].unsqueeze(0), 1024, ps_rb)
    b2rep = emit_row_broadcast(P, nc, bufs, "b2_rep", dram["fc2_b"].unsqueeze(0), 1024, ps_rb)
    P.barrier()
    ps_rb.free()
    b1col = Buf(nc, "b1col", [128, 32], F32)
    P.dma("y", lambda e: e.dma_start(b1col.ap(), dram["fc1_b"].rearrange("(a p) -> p a", p=128), allow_slow_non_contiguous=True),
          w=["b1col"])
    mu = Buf(nc, "c_mu", [128, 6], F32)
    cen = Buf(nc, "c_cen", [128, 1024], F32)
    gb = Buf(nc, "c_gb", [128, 1024], BF16)
    gT = Buf(nc, "gT", [128, DC, 512], BF16)
    h1T = Buf(nc, "h1T", [128, 32, 512], BF16)
    w1c = Buf(nc, "w1c", [128, 2, DC, 128], BF16)
    w2c = Buf(nc, "w2c", [128, 2, 1024], BF16)
    ps_h1 = [Buf(nc, "ps_h1%d" % s, [128, 512], F32, psum=True) for s in range(2)]
    ps_o = [Buf(nc, "ps_mo%d" % s, [128, 1024], F32, psum=True) for s in range(2)]
    for grp in range(4):
        for tl in range(4):
            i = grp * 4 + tl
            P.op("v", lambda e, i=i: e.reduce_sum(mu.t[:, 0:1], xm.t[:, i, :], axis=AX),
                 r=[("xm", i, cc) for cc in range(8)], w=[("cmu", 1)])
            P.op("v", lambda e: e.tensor_scalar(mu.t[:, 1:2], mu.t[:, 0:1], 1.0 / D, None, ALU.mult),
                 r=[("cmu", 1)], w=[("cmu", 2)])
            P.op("v", lambda e, i=i: e.tensor_scalar(cen.ap(), xm.t[:, i, :], mu.t[:, 1:2],
                                                     None, ALU.subtract),
                 r=[("xm", i, cc) for cc in range(8)] + [("cmu", 2)], w=["c_cen"])
            P.op("s", lambda e: e.activation(gb.ap(), cen.ap(), AF.Square, accum_out=mu.t[:, 2:3]),
                 r=["c_cen"], w=[("cmu", 3), ("c_gb", "x")])
            P.op("s", lambda e: e.activation(mu.t[:, 3:4], mu.t[:, 2:3], AF.Ln,
                                             bias=bufs["cst"].t[:, 0:1], scale=1.0 / D),
                 r=[("cmu", 3), "cst"], w=[("cmu", 4)])
            P.op("s", lambda e: e.activation(mu.t[:, 3:4], mu.t[:, 3:4], AF.Exp, scale=-0.5),
                 r=[("cmu", 4)], w=[("cmu", 5)])
            P.op("v", lambda e: e.tensor_scalar(cen.ap(), cen.ap(), mu.t[:, 3:4], None, ALU.mult),
                 r=["c_cen", ("cmu", 5)], w=["c_cen"])
            P.op("v", lambda e: e.tensor_tensor(cen.ap(), cen.ap(), n2w.ap(), ALU.mult),
                 r=["c_cen", "n2w_rep"], w=["c_cen"])
            P.op("v", lambda e: e.tensor_tensor(gb.ap(), cen.ap(), n2b.ap(), ALU.add),
                 r=["c_cen", "n2b_rep"], w=[("c_gb", "x")])
            for a in range(DC):
                P.mm([lambda e, a=a: e.transpose(ps_trb.t[:, 0:128],
                                                 gb.t[:, a * 128:(a + 1) * 128], bufs["id_b"].ap())],
                     r=[("c_gb", "x"), "id_b"], w=["ps_trb2"])
                P.op("v", lambda e, a=a, tl=tl: e.tensor_copy(
                    gT.t[:, a, tl * 128:(tl + 1) * 128], ps_trb.t[:, 0:128]),
                    r=["ps_trb2"], w=[("gT", a)])
        for ft in range(32):
            sl = ft % 2
            P.dma("g", lambda e, ft=ft, sl=sl: e.dma_start(
                w1c.t[:, sl, :, :],
                dram["fc1T"].rearrange("(a p) f -> p a f", p=128)[:, :, ft * 128:(ft + 1) * 128]),
                w=[("w1c", sl)])
            P.mm([(lambda e, a=a, sl=sl: e.matmul(
                ps_h1[sl].t[:, 0:512],
                w1c.t[:, sl, a, :],
                gT.t[:, a, :],
                start=(a == 0), stop=(a == DC - 1)))
                for a in range(DC)],
                r=[("w1c", sl)] + [("gT", a) for a in range(DC)],
                w=["ps_h1%d" % sl])
            P.op("s", lambda e, ft=ft, sl=sl: e.activation(
                h1T.t[:, ft, :], ps_h1[sl].t[:, 0:512], AF.Gelu, bias=b1col.t[:, ft:ft + 1]),
                r=["ps_h1%d" % sl, "b1col"], w=[("h1T", ft)])
        for pair in range(2):
            for ft in range(32):
                sl = ft % 2
                P.dma("y", lambda e, ft=ft, sl=sl: e.dma_start(
                    w2c.t[:, sl, :],
                    dram["fc2T"][ft * 128:(ft + 1) * 128, :]),
                    w=[("w2c", sl)])
                P.mm([(lambda e, tt=tt, ft=ft, sl=sl, c0=c0, pair=pair: e.matmul(
                    ps_o[tt].t[:, c0:c0 + 512],
                    h1T.t[:, ft, (pair * 2 + tt) * 128:(pair * 2 + tt) * 128 + 128],
                    w2c.t[:, sl, c0:c0 + 512],
                    start=(ft == 0), stop=(ft == 31), skip_group_check=True))
                    for tt in range(2) for c0 in (0, 512)],
                    r=[("w2c", sl)] + [("h1T", f_) for f_ in range(32)],
                    w=["ps_mo0", "ps_mo1"])
            for tt in range(2):
                i = grp * 4 + pair * 2 + tt
                P.op("v", lambda e, i=i, tt=tt: e.tensor_tensor(
                    cen.ap(), ps_o[tt].ap(), b2rep.ap(), ALU.add),
                    r=["ps_mo%d" % tt, "b2_rep"], w=["c_cen"])
                P.op("v", lambda e, i=i: e.tensor_tensor(
                    cen.ap(), cen.ap(), xm.t[:, i, :], ALU.add),
                    r=["c_cen"] + [("xm", i, cc) for cc in range(8)], w=["c_cen"])
                P.op("v", lambda e, i=i: e.tensor_tensor(
                    x_res.t[:, i, :], x_res.t[:, i, :], cen.ap(), ALU.add),
                    r=["c_cen", ("x_res", i)], w=[("x_res", i)])
    P.barrier()
    for b_ in reversed(ps_o):
        b_.free()
    for b_ in reversed(ps_h1):
        b_.free()
    w2c.free(); w1c.free(); h1T.free(); gT.free(); gb.free(); cen.free(); mu.free()
    b1col.free(); bufs["b2_rep"].free(); bufs["n2b_rep"].free(); bufs["n2w_rep"].free()
    dcol.free(); lampow.free(); mixT.free()
    ps_trb.free()



def emit_routing_select(P, nc, bufs, s_full, ps_small, tok_off_dram):
    EPSR = 0.1
    CONST = float(EPSR * np.log(8.0))
    sc = Buf(nc, "rt_sc", [1, 24], F32)
    col = Buf(nc, "rt_col", [128, 8], F32)
    wk = Buf(nc, "rt_wk", [128, 64], F32)
    wk2 = Buf(nc, "rt_wk2", [128, 64], F32)
    sat = Buf(nc, "rt_sat", [128, 64], F32)
    idxf = Buf(nc, "rt_idx", [128, 64], F32)
    trp = Buf(nc, "rt_trp", [1, 128], F32)
    toff = Buf(nc, "rt_toff", [1, 2], F32)
    P.dma("g", lambda e: e.dma_start(toff.t[0:1, 0:1], tok_off_dram), w=["toff"])
    P.op("g", lambda e: e.iota(idxf.ap(), [[128, 64]], base=0, channel_multiplier=1,
                               allow_small_or_imprecise_dtypes=True), w=["rt_idx"])
    P.op("g", lambda e: e.memset(col.t[:, 0:1], 0.0), w=[("rt", "arep")])

    def bcast(src_ap, dst_sl, rkeys, wkey):
        P.mm([lambda e: e.matmul(ps_small.t[:, 0:1], bufs["ones"].t[0:1, :], src_ap,
                                 start=True, stop=True)],
             r=rkeys + ["ones"], w=[ps_small.name])
        P.op("v", lambda e: e.tensor_copy(dst_sl, ps_small.t[:, 0:1]),
             r=[ps_small.name], w=[wkey])

    def colsum(src_sl, dst_sl, rkeys, wkey):
        P.mm([lambda e: e.matmul(ps_small.t[0:1, 0:1], src_sl, bufs["ones"].t[:, 0:1],
                                 start=True, stop=True)],
             r=rkeys + ["ones"], w=[ps_small.name])
        P.op("v", lambda e: e.tensor_copy(dst_sl, ps_small.t[0:1, 0:1]),
             r=[ps_small.name], w=[wkey])

    def fullmax(src_ap, dst_sl, rkeys, wkey, negate=False):
        P.op("v", lambda e: e.reduce_max(col.t[:, 1:2], src_ap, axis=AX, negate=negate),
             r=rkeys, w=[("rt", "mxc")])
        P.mm([lambda e: e.transpose(ps_small.t[0:1, 0:128], col.t[:, 1:2], bufs["id_f"].ap())],
             r=[("rt", "mxc"), "id_f"], w=[ps_small.name])
        P.op("v", lambda e: e.tensor_copy(trp.ap(), ps_small.t[0:1, 0:128]),
             r=[ps_small.name], w=[("rt", "trp")])
        P.op("v", lambda e: e.reduce_max(dst_sl, trp.ap(), axis=AX, negate=negate),
             r=[("rt", "trp")], w=[wkey])

    for it in range(50):
        P.op("v", lambda e: e.tensor_scalar(wk.ap(), s_full.ap(), col.t[:, 0:1], None, ALU.add),
             r=["s_full", ("rt", "arep")], w=[("rt", "wk")])
        P.op("v", lambda e: e.tensor_scalar(wk.ap(), wk.ap(), 0.0, -1.0, ALU.max, ALU.mult),
             r=[("rt", "wk")], w=[("rt", "wk")])
        P.op("v", lambda e: e.tensor_tensor(wk.ap(), s_full.ap(), wk.ap(), ALU.add),
             r=["s_full", ("rt", "wk")], w=[("rt", "wk")])
        fullmax(wk.ap(), sc.t[0:1, 0:1], [("rt", "wk")], ("rt", "m"))
        bcast(sc.t[0:1, 0:1], col.t[:, 2:3], [("rt", "m")], ("rt", "mrep"))
        P.op("v", lambda e: e.tensor_scalar(col.t[:, 3:4], col.t[:, 2:3], -10.0, None, ALU.mult),
             r=[("rt", "mrep")], w=[("rt", "mb")])
        P.op("s", lambda e: e.activation(wk2.ap(), wk.ap(), AF.Exp, scale=10.0,
                                         bias=col.t[:, 3:4]),
             r=[("rt", "wk"), ("rt", "mb")], w=[("rt", "ex")])
        P.op("v", lambda e: e.reduce_sum(col.t[:, 4:5], wk2.ap(), axis=AX),
             r=[("rt", "ex")], w=[("rt", "rs")])
        colsum(col.t[:, 4:5], sc.t[0:1, 1:2], [("rt", "rs")], ("rt", "S"))
        P.op("s", lambda e: e.activation(sc.t[0:1, 2:3], sc.t[0:1, 1:2], AF.Ln),
             r=[("rt", "S")], w=[("rt", "lnS")])
        P.op("v", lambda e: e.tensor_scalar(sc.t[0:1, 2:3], sc.t[0:1, 2:3], -EPSR, CONST,
                                            ALU.mult, ALU.add),
             r=[("rt", "lnS")], w=[("rt", "t")])
        P.op("v", lambda e: e.tensor_tensor(sc.t[0:1, 3:4], sc.t[0:1, 2:3], sc.t[0:1, 0:1],
                                            ALU.subtract),
             r=[("rt", "t"), ("rt", "m")], w=[("rt", "a")])
        bcast(sc.t[0:1, 3:4], col.t[:, 0:1], [("rt", "a")], ("rt", "arep"))
    P.op("v", lambda e: e.tensor_scalar(wk.ap(), s_full.ap(), col.t[:, 0:1], None, ALU.add),
         r=["s_full", ("rt", "arep")], w=[("rt", "wk")])
    P.op("v", lambda e: e.tensor_scalar(sat.ap(), wk.ap(), 0.0, None, ALU.is_ge),
         r=[("rt", "wk")], w=["rt_sat"])
    P.op("v", lambda e: e.reduce_sum(col.t[:, 4:5], sat.ap(), axis=AX),
         r=["rt_sat"], w=[("rt", "rs")])
    colsum(col.t[:, 4:5], sc.t[0:1, 4:5], [("rt", "rs")], ("rt", "csat"))
    P.op("v", lambda e: e.tensor_scalar(sc.t[0:1, 5:6], sc.t[0:1, 4:5], 1024.0, None, ALU.min),
         r=[("rt", "csat")], w=[("rt", "K1")])
    P.op("v", lambda e: e.tensor_scalar(sc.t[0:1, 6:7], sc.t[0:1, 5:6], -1.0, 1024.0,
                                        ALU.mult, ALU.add),
         r=[("rt", "K1")], w=[("rt", "K2")])

    def bisect(lo_init_sl, lo_init_keys, hi_init_sl, hi_init_keys, cnt_fn, K_sl, K_key,
               iters, lo_sl, hi_sl, lo_key, hi_key, increasing):
        if lo_init_sl is None:
            P.op("g", lambda e: e.memset(lo_sl, 0.0), w=[lo_key])
        else:
            P.op("v", lambda e: e.tensor_copy(lo_sl, lo_init_sl), r=lo_init_keys, w=[lo_key])
        P.op("v", lambda e: e.tensor_copy(hi_sl, hi_init_sl), r=hi_init_keys, w=[hi_key])
        for it2 in range(iters):
            P.op("v", lambda e: e.tensor_tensor(sc.t[0:1, 10:11], lo_sl, hi_sl, ALU.add),
                 r=[lo_key, hi_key], w=[("rt", "sum")])
            P.op("v", lambda e: e.tensor_scalar(sc.t[0:1, 10:11], sc.t[0:1, 10:11], 0.5,
                                                None, ALU.mult),
                 r=[("rt", "sum")], w=[("rt", "mid")])
            bcast(sc.t[0:1, 10:11], col.t[:, 5:6], [("rt", "mid")], ("rt", "midb"))
            cnt_fn()
            P.op("v", lambda e: e.reduce_sum(col.t[:, 6:7], wk2.ap(), axis=AX),
                 r=[("rt", "cmask")], w=[("rt", "rs2")])
            colsum(col.t[:, 6:7], sc.t[0:1, 11:12], [("rt", "rs2")], ("rt", "cnt"))
            P.op("v", lambda e: e.tensor_tensor(sc.t[0:1, 12:13], sc.t[0:1, 11:12], K_sl,
                                                ALU.is_gt),
                 r=[("rt", "cnt"), K_key], w=[("rt", "gt")])
            if increasing:
                P.op("v", lambda e: e.tensor_copy(sc.t[0:1, 18:19], sc.t[0:1, 12:13]),
                     r=[("rt", "gt")], w=[("rt", "selhi")])
            else:
                P.op("v", lambda e: e.tensor_scalar(sc.t[0:1, 18:19], sc.t[0:1, 12:13],
                                                    -1.0, 1.0, ALU.mult, ALU.add),
                     r=[("rt", "gt")], w=[("rt", "selhi")])
            P.op("v", lambda e: e.tensor_scalar(sc.t[0:1, 19:20], sc.t[0:1, 18:19],
                                                -1.0, 1.0, ALU.mult, ALU.add),
                 r=[("rt", "selhi")], w=[("rt", "sello")])
            P.op("v", lambda e: e.tensor_tensor(sc.t[0:1, 13:14], sc.t[0:1, 10:11], hi_sl,
                                                ALU.subtract),
                 r=[("rt", "mid"), hi_key], w=[("rt", "d1")])
            P.op("v", lambda e: e.scalar_tensor_tensor(hi_sl, sc.t[0:1, 13:14], sc.t[0:1, 18:19],
                                                       hi_sl, ALU.mult, ALU.add),
                 r=[("rt", "d1"), ("rt", "selhi"), hi_key], w=[hi_key])
            P.op("v", lambda e: e.tensor_tensor(sc.t[0:1, 16:17], sc.t[0:1, 10:11], lo_sl,
                                                ALU.subtract),
                 r=[("rt", "mid"), lo_key], w=[("rt", "d2")])
            P.op("v", lambda e: e.scalar_tensor_tensor(lo_sl, sc.t[0:1, 16:17], sc.t[0:1, 19:20],
                                                       lo_sl, ALU.mult, ALU.add),
                 r=[("rt", "d2"), ("rt", "sello"), lo_key], w=[lo_key])

    def cnt_idx():
        P.op("v", lambda e: e.tensor_scalar(wk2.ap(), idxf.ap(), col.t[:, 5:6], None, ALU.is_lt),
             r=["rt_idx", ("rt", "midb")], w=[("rt", "cmask")])
        P.op("v", lambda e: e.tensor_tensor(wk2.ap(), wk2.ap(), sat.ap(), ALU.mult),
             r=[("rt", "cmask"), "rt_sat"], w=[("rt", "cmask")])
    P.op("g", lambda e: e.memset(sc.t[0:1, 8:9], 8200.0), w=[("rt", "hi_init")])
    bisect(None, [], sc.t[0:1, 8:9], [("rt", "hi_init")], cnt_idx, sc.t[0:1, 5:6], ("rt", "K1"),
           20, sc.t[0:1, 7:8], sc.t[0:1, 8:9], ("rt", "ilo"), ("rt", "ihi"), increasing=True)
    P.op("v", lambda e: e.tensor_copy(sc.t[0:1, 9:10], sc.t[0:1, 7:8]),
         r=[("rt", "ilo")], w=[("rt", "theta_i")])
    fullmax(s_full.ap(), sc.t[0:1, 0:1], ["s_full"], ("rt", "smax"))
    P.op("v", lambda e: e.tensor_scalar(sc.t[0:1, 0:1], sc.t[0:1, 0:1], 1.0, None, ALU.add),
         r=[("rt", "smax")], w=[("rt", "smax1")])
    fullmax(s_full.ap(), sc.t[0:1, 1:2], ["s_full"], ("rt", "smin"), negate=True)
    P.op("v", lambda e: e.tensor_scalar(sc.t[0:1, 1:2], sc.t[0:1, 1:2], -1.0, -1.0,
                                        ALU.mult, ALU.add),
         r=[("rt", "smin")], w=[("rt", "smin1")])

    P.op("v", lambda e: e.tensor_scalar(wk.ap(), sat.ap(), -1.0, 1.0, ALU.mult, ALU.add),
         r=["rt_sat"], w=[("rt", "unsat")])

    def cnt_s2():
        P.op("v", lambda e: e.tensor_scalar(wk2.ap(), s_full.ap(), col.t[:, 5:6], None, ALU.is_gt),
             r=["s_full", ("rt", "midb")], w=[("rt", "cmask")])
        P.op("v", lambda e: e.tensor_tensor(wk2.ap(), wk2.ap(), wk.ap(), ALU.mult),
             r=[("rt", "cmask"), ("rt", "unsat")], w=[("rt", "cmask")])
    bisect(sc.t[0:1, 1:2], [("rt", "smin1")], sc.t[0:1, 0:1], [("rt", "smax1")], cnt_s2,
           sc.t[0:1, 6:7], ("rt", "K2"),
           42, sc.t[0:1, 14:15], sc.t[0:1, 15:16], ("rt", "slo"), ("rt", "shi"),
           increasing=False)
    bcast(toff.t[0:1, 0:1], col.t[:, 6:7], ["toff"], ("rt", "toffb"))
    bcast(sc.t[0:1, 9:10], col.t[:, 5:6], [("rt", "theta_i")], ("rt", "thib"))
    bcast(sc.t[0:1, 15:16], col.t[:, 7:8], [("rt", "shi")], ("rt", "thsb"))
    loc_i = Buf(nc, "rt_loci", [128, NT], F32)
    loc_m = Buf(nc, "rt_locm", [128, NT], F32)
    P.op("g", lambda e: e.iota(loc_i.ap(), [[128, NT]], base=0, channel_multiplier=1,
                               allow_small_or_imprecise_dtypes=True), w=[("rt", "loci")])
    P.op("v", lambda e: e.tensor_scalar(loc_i.ap(), loc_i.ap(), col.t[:, 6:7], None, ALU.add),
         r=[("rt", "loci"), ("rt", "toffb")], w=[("rt", "loci")])
    P.op("v", lambda e: e.tensor_scalar(loc_i.ap(), loc_i.ap(), col.t[:, 5:6], None, ALU.is_lt),
         r=[("rt", "loci"), ("rt", "thib")], w=[("rt", "ltheta")])
    P.op("v", lambda e: e.tensor_scalar(loc_m.ap(), bufs["s_sb"].ap(), col.t[:, 0:1], None,
                                        ALU.add),
         r=[("s_sb", i) for i in range(NT)] + [("rt", "arep")], w=[("rt", "sa_loc")])
    P.op("v", lambda e: e.tensor_scalar(loc_m.ap(), loc_m.ap(), 0.0, None, ALU.is_ge),
         r=[("rt", "sa_loc")], w=[("rt", "sat_loc")])
    P.op("v", lambda e: e.tensor_tensor(loc_i.ap(), loc_i.ap(), loc_m.ap(), ALU.mult),
         r=[("rt", "ltheta"), ("rt", "sat_loc")], w=[("rt", "sel1")])
    P.op("v", lambda e: e.tensor_scalar(loc_m.ap(), loc_m.ap(), -1.0, 1.0, ALU.mult, ALU.add),
         r=[("rt", "sat_loc")], w=[("rt", "unsat_loc")])
    P.op("v", lambda e: e.tensor_scalar(bufs["hmask"].ap(), bufs["s_sb"].ap(), col.t[:, 7:8],
                                        None, ALU.is_gt),
         r=[("s_sb", i) for i in range(NT)] + [("rt", "thsb")], w=["hmask"])
    P.op("v", lambda e: e.tensor_tensor(bufs["hmask"].ap(), bufs["hmask"].ap(), loc_m.ap(),
                                        ALU.mult),
         r=["hmask", ("rt", "unsat_loc")], w=["hmask"])
    P.op("v", lambda e: e.tensor_tensor(bufs["hmask"].ap(), bufs["hmask"].ap(), loc_i.ap(),
                                        ALU.add),
         r=["hmask", ("rt", "sel1")], w=["hmask"])
    if "rt_dbg" in bufs:
        for cidx in (0, 3, 4, 5, 6, 8, 9, 15):
            P.op("s", lambda e, cidx=cidx: e.copy(bufs["rt_dbg"].t[0:1, cidx:cidx + 1],
                                                  sc.t[0:1, cidx:cidx + 1]),
                 r=[("rt", k) for k in ("arep", "csat", "K1", "K2", "theta_i", "shi", "ihi")],
                 w=["rt_dbg"])
    P.barrier()
    loc_m.free(); loc_i.free(); toff.free(); trp.free(); idxf.free(); sat.free()
    wk2.free(); wk.free(); col.free(); sc.free()


def emit_light(P, nc, bufs, dram):
    x_res = bufs["x_res"]
    w1l = Buf(nc, "w1l", [128, DC, 512], BF16)
    w2l = Buf(nc, "w2l", [128, 4, 1024], BF16)
    glc = Buf(nc, "glc", [128, DC], F32)
    b1l = Buf(nc, "b1l", [128, 4], F32)
    ps_rb = Buf(nc, "ps_rbl", [128, 512], F32, psum=True)
    b2l = emit_row_broadcast(P, nc, bufs, "b2l_rep", dram["b2_light"].unsqueeze(0), 1024, ps_rb)
    P.barrier()
    ps_rb.free()
    lt_t = Buf(nc, "lt_t", [128, 1024], F32)
    bufs["lt_t"] = lt_t
    hnl = Buf(nc, "hnl", [128, 512], BF16)
    h1l = Buf(nc, "h1l", [128, 4, 512], BF16)
    hn0c = Buf(nc, "hn0c", [128, DC, 512], BF16)
    ps_l = [Buf(nc, "ps_l%d" % f_, [128, 512], F32, psum=True) for f_ in range(4)]
    ps_lo = Buf(nc, "ps_lo", [128, 1024], F32, psum=True)
    P.dma("g", lambda e: e.dma_start(w1l.ap(), dram["w1_light"].rearrange("(a p) f -> p a f", p=128)),
          w=["w1l"])
    P.dma("g", lambda e: e.dma_start(w2l.ap(), dram["w2_light"].rearrange("(a p) f -> p a f", p=128)),
          w=["w2l"])
    P.dma("g", lambda e: e.dma_start(glc.ap(), dram["gamma_light"].rearrange("(a p) -> p a", p=128), allow_slow_non_contiguous=True),
          w=["glc"])
    P.dma("g", lambda e: e.dma_start(b1l.ap(), dram["b1_light"].rearrange("(a p) -> p a", p=128), allow_slow_non_contiguous=True),
          w=["b1l"])
    for q in range(4):
        P.dma("y", lambda e, q=q: e.dma_start(
            hn0c.ap(),
            dram["hn0T"].rearrange("(a p) t -> p a t", p=128)[:, :, q * 512:(q + 1) * 512]),
            w=["hn0c"])
        for a in range(DC):
            P.op("v", lambda e, a=a: e.tensor_scalar(hnl.ap(), hn0c.t[:, a, :],
                                                     glc.t[:, a:a + 1], None, ALU.mult),
                 r=["hn0c", "glc"], w=["hnl"])
            P.mm([(lambda e, fb=fb, a=a: e.matmul(
                ps_l[fb].t[:, 0:512],
                w1l.t[:, a, fb * 128:(fb + 1) * 128],
                hnl.ap(),
                start=(a == 0), stop=(a == DC - 1), skip_group_check=True))
                for fb in range(4)],
                r=["w1l", "hnl"], w=["ps_l%d" % f_ for f_ in range(4)])
        for fb in range(4):
            P.op("s", lambda e, fb=fb: e.activation(h1l.t[:, fb, :], ps_l[fb].ap(), AF.Gelu,
                                                    bias=b1l.t[:, fb:fb + 1]),
                 r=["ps_l%d" % fb, "b1l"], w=[("h1l", fb)])
        for tt in range(4):
            i = q * 4 + tt
            P.mm([(lambda e, fb=fb, tt=tt, c0=c0: e.matmul(
                ps_lo.t[:, c0:c0 + 512],
                h1l.t[:, fb, tt * 128:(tt + 1) * 128],
                w2l.t[:, fb, c0:c0 + 512],
                start=(fb == 0), stop=(fb == 3), skip_group_check=True))
                for c0 in (0, 512) for fb in range(4)],
                r=[("h1l", f_) for f_ in range(4)] + ["w2l"], w=["ps_lo"])
            P.op("v", lambda e: e.tensor_tensor(bufs["lt_t"].ap(), ps_lo.ap(), b2l.ap(), ALU.add),
                 r=["ps_lo", "b2l_rep"], w=["lt_t"])
            P.op("v", lambda e, i=i: e.tensor_tensor(x_res.t[:, i, :], x_res.t[:, i, :],
                                                     bufs["lt_t"].ap(), ALU.add),
                 r=["lt_t", ("x_res", i)], w=[("x_res", i)])
    P.barrier()
    ps_lo.free()
    for b_ in reversed(ps_l):
        b_.free()
    hn0c.free(); h1l.free(); hnl.free(); lt_t.free()
    bufs["b2l_rep"].free()
    b1l.free(); glc.free(); w2l.free(); w1l.free()


def emit_heavy(P, nc, bufs, dram):
    x_res = bufs["x_res"]
    GROUPS = [(0, 3), (3, 3), (6, 3), (9, 3), (12, 3), (15, 1)]
    w2h = Buf(nc, "w2h", [128, 32, 1024], BF16)
    ghc = Buf(nc, "ghc", [128, DC], F32)
    b1h = Buf(nc, "b1h", [128, 32], F32)
    ps_rb = Buf(nc, "ps_rbh", [128, 512], F32, psum=True)
    b2h = emit_row_broadcast(P, nc, bufs, "b2h_rep", dram["b2_heavy"].unsqueeze(0), 1024, ps_rb)
    P.barrier()
    ps_rb.free()
    bufs["hv_t"] = hv_t = Buf(nc, "hv_t", [128, 1024], F32)
    hn0c = Buf(nc, "hn0ch", [128, DC, 384], BF16)
    hnh = Buf(nc, "hnh", [128, DC, 384], BF16)
    w1c = Buf(nc, "w1ch", [128, 2, DC, 128], BF16)
    hTs = Buf(nc, "hTs", [128, 2, 384], BF16)
    out_ps_l = [Buf(nc, "hv_out%d" % j, [128, 1024], F32, psum=True) for j in range(3)]
    hT_ps_l = [Buf(nc, "hv_h%d" % s, [128, 512], F32, psum=True) for s in range(2)]
    P.dma("g", lambda e: e.dma_start(ghc.ap(), dram["gamma_heavy"].rearrange("(a p) -> p a", p=128), allow_slow_non_contiguous=True),
          w=["ghc"])
    P.dma("g", lambda e: e.dma_start(b1h.ap(), dram["b1_heavy"].rearrange("(a p) -> p a", p=128), allow_slow_non_contiguous=True),
          w=["b1h"])
    for qq in range(4):
        P.dma("g", lambda e, qq=qq: e.dma_start(
            w2h.t[:, qq * 8:(qq + 1) * 8, :],
            dram["w2_heavy"].rearrange("(n p) c -> p n c", p=128)[:, qq * 8:(qq + 1) * 8, :]),
            w=[("w2h", qq)])
    for gi, (g0, glen) in enumerate(GROUPS):
        P.dma("y", lambda e, g0=g0, glen=glen: e.dma_start(
            hn0c.t[:, :, 0:glen * 128],
            dram["hn0T"].rearrange("(a p) t -> p a t", p=128)[:, :, g0 * 128:(g0 + glen) * 128]),
            w=["hn0ch"])
        for a in range(DC):
            P.op("v", lambda e, a=a, glen=glen: e.tensor_scalar(
                hnh.t[:, a, 0:glen * 128], hn0c.t[:, a, 0:glen * 128],
                ghc.t[:, a:a + 1], None, ALU.mult),
                r=["hn0ch", "ghc"], w=[("hnh", a)])
        for fc in range(32):
            sl = fc % 2
            P.dma("g", lambda e, fc=fc, sl=sl: e.dma_start(
                w1c.t[:, sl, :, :],
                dram["w1_heavy"].rearrange("(a p) f -> p a f", p=128)[:, :, fc * 128:(fc + 1) * 128]),
                w=[("w1ch", sl)])
            P.mm([(lambda e, a=a, sl=sl, glen=glen: e.matmul(
                hT_ps_l[sl].t[:, 0:glen * 128],
                w1c.t[:, sl, a, :],
                hnh.t[:, a, 0:glen * 128],
                start=(a == 0), stop=(a == DC - 1)))
                for a in range(DC)],
                r=[("w1ch", sl)] + [("hnh", a) for a in range(DC)],
                w=["hv_h%d" % sl])
            P.op("s", lambda e, sl=sl, fc=fc, glen=glen: e.activation(
                hTs.t[:, sl, 0:glen * 128], hT_ps_l[sl].t[:, 0:glen * 128], AF.Gelu,
                bias=b1h.t[:, fc:fc + 1]),
                r=["hv_h%d" % sl, "b1h"], w=[("hTs", sl)])
            P.mm([(lambda e, j=j, sl=sl, fc=fc, c0=c0: e.matmul(
                out_ps_l[j].t[:, c0:c0 + 512],
                hTs.t[:, sl, j * 128:(j + 1) * 128],
                w2h.t[:, fc, c0:c0 + 512],
                start=(fc == 0), stop=(fc == 31), skip_group_check=True))
                for j in range(glen) for c0 in (0, 512)],
                r=[("hTs", sl)] + [("w2h", q) for q in range(4)],
                w=["hv_out%d" % j for j in range(glen)])
        for j in range(glen):
            i = g0 + j
            P.op("v", lambda e, j=j: e.tensor_tensor(
                bufs["hv_t"].ap(), out_ps_l[j].ap(), bufs["b2h_rep"].ap(), ALU.add),
                r=["hv_out%d" % j, "b2h_rep"], w=["hv_t"])
            P.op("v", lambda e, i=i: e.scalar_tensor_tensor(
                x_res.t[:, i, :],
                bufs["hv_t"].ap(),
                bufs["hmask"].t[:, i:i + 1],
                x_res.t[:, i, :],
                ALU.mult, ALU.add),
                r=["hv_t", "hmask", ("x_res", i)], w=[("x_res", i)])
    P.barrier()
    for b_ in reversed(hT_ps_l):
        b_.free()
    for b_ in reversed(out_ps_l):
        b_.free()
    hTs.free(); w1c.free(); hnh.free(); hn0c.free(); hv_t.free()
    bufs["b2h_rep"].free()
    b1h.free(); ghc.free(); w2h.free()



def build_C():
    nc = bass.Bass("TRN2", target_bir_lowering=False)
    x1 = nc.dram_tensor("x1", [TPC, D], F32, kind="ExternalInput").ap()
    zall = nc.dram_tensor("zall", [8 * 66, 1024], F32, kind="ExternalInput").ap()
    dist1h = nc.dram_tensor("dist1h", [1, 24], F32, kind="ExternalInput").ap()
    tok_off = nc.dram_tensor("tok_off", [1, 1], F32, kind="ExternalInput").ap()
    smask_d = nc.dram_tensor("smask", [1, 8], F32, kind="ExternalInput").ap()
    names = {
        "mix": [D, 64], "Dvec": [D], "log_tau": [NST], "log_dt": [1],
        "norm1_w": [D], "norm1_b": [D], "norm2_w": [D], "norm2_b": [D],
        "fc1_w": [4096, D], "fc1_b": [4096], "fc2_w": [D, 4096], "fc2_b": [D],
        "routing": [D], "gamma_light": [D], "w1_light": [D, 512], "b1_light": [512],
        "w2_light": [512, D], "b2_light": [D], "gamma_heavy": [D],
        "w1_heavy": [D, 4096], "b1_heavy": [4096], "w2_heavy": [4096, D], "b2_heavy": [D],
    }
    dram = {k: nc.dram_tensor(k, shp, F32, kind="ExternalInput").ap() for k, shp in names.items()}
    out = nc.dram_tensor("out", [TPC, D], F32, kind="ExternalOutput").ap()
    if DBG_XM:
        dram["xm_o"] = nc.dram_tensor("xm_o", [TPC, D], F32, kind="ExternalOutput").ap()
        dram["st1_o"] = nc.dram_tensor("st1_o", [TPC, D], F32, kind="ExternalOutput").ap()
        dram["st2_o"] = nc.dram_tensor("st2_o", [TPC, D], F32, kind="ExternalOutput").ap()
    dram["psnap"] = nc.dram_tensor("psnap", [64, NT * NSUB * 1024], BF16).ap()
    dram["hn0T"] = nc.dram_tensor("hn0T_sc", [D, TPC], BF16).ap()
    dram["fc1T"] = nc.dram_tensor("fc1T_sc", [D, 4096], BF16).ap()
    dram["fc2T"] = nc.dram_tensor("fc2T_sc", [4096, D], BF16).ap()

    P = Prog(nc)
    bufs = {}
    emit_const_tiles(P, nc, bufs)
    bufs["x_res"] = x_res = Buf(nc, "x_res", [128, NT, D], F32)
    bufs["hmask"] = Buf(nc, "hmask", [128, NT], F32)
    bufs["xm"] = Buf(nc, "xm", [128, NT, 1024], BF16)
    bufs["u_bf"] = Buf(nc, "u_bf", [128, NT, 1024], BF16)
    bufs["st_sb"] = Buf(nc, "st_sb", [128, NT], F32)
    bufs["s_sb"] = Buf(nc, "s_sb", [128, NT], F32)
    bufs["ps_trb3"] = Buf(nc, "ps_trb3", [128, 1024], BF16, psum=True)
    for qq in range(4):
        P.dma("y", lambda e, qq=qq: e.dma_start(
            x_res.t[:, qq * 4:(qq + 1) * 4, :],
            x1.rearrange("(n p) d -> p n d", p=128)[:, qq * 4:(qq + 1) * 4, :]),
            w=[("x_res", i) for i in range(qq * 4, qq * 4 + 4)])
    emit_stream_transpose_w(P, nc, bufs, "fc1", dram["fc1_w"], dram["fc1T"], 4096, 1024,
                            bufs["ps_trb3"])
    emit_stream_transpose_w(P, nc, bufs, "fc2", dram["fc2_w"], dram["fc2T"], 1024, 4096,
                            bufs["ps_trb3"])
    emit_l1_preamble(P, nc, bufs, {"snaps": True, "store": True, "free_tmp": True}, dram)
    bufs["S_in"] = S_in = Buf(nc, "S_in", [64, 1024], F32)
    s_full = Buf(nc, "s_full", [128, 64], F32)
    bufs["s_full"] = s_full
    zin = Buf(nc, "zin", [64, 8, 1024], BF16)
    sfin = Buf(nc, "sfin", [128, 8, 16], F32)
    ohb = Buf(nc, "ohb", [64, 24], F32)
    sm = Buf(nc, "smb", [128, 8], F32)
    rowb = Buf(nc, "rowb", [1, 32], F32)
    dec = Buf(nc, "dec", [64, 12], F32)
    acc = Buf(nc, "z_acc", [64, 1024], F32)
    ps_sm = Buf(nc, "ps_sm", [128, 512], F32, psum=True)
    for r in range(8):
        P.dma("g", lambda e, r=r: e.dma_start(zin.t[0:64, r, :], zall[r * 66:r * 66 + 64, :]),
              w=[("zin", r)])
        P.dma("y", lambda e, r=r: e.dma_start(
            sfin.t[:, r, :],
            zall[r * 66 + 64:r * 66 + 66, :].rearrange("r c -> (r c)").rearrange("(p f) -> p f", p=128)),
            w=[("sfin", r)])
    P.dma("g", lambda e: e.dma_start(rowb.t[0:1, 0:24], dist1h), w=["rowb"])
    P.dma("g", lambda e: e.dma_start(rowb.t[0:1, 24:32], smask_d), w=[("rowb", "sm")])
    P.mm([lambda e: e.matmul(ps_sm.t[:, 0:32], bufs["ones"].t[0:1, 0:128], rowb.t[0:1, 0:32],
                             start=True, stop=True)],
         r=["rowb", ("rowb", "sm"), "ones"], w=["ps_sm"])
    P.op("v", lambda e: e.tensor_copy(ohb.t[0:64, 0:24], ps_sm.t[0:64, 0:24]),
         r=["ps_sm"], w=["ohb"])
    P.op("v", lambda e: e.tensor_copy(sm.ap(), ps_sm.t[:, 24:32]),
         r=["ps_sm"], w=["smb"])
    for d_ in range(3):
        P.op("v", lambda e, d_=d_: e.tensor_scalar(dec.t[0:64, d_:d_ + 1], bufs["na_col_ap"],
                                                   2048.0 * (d_ + 1), None, ALU.mult),
             r=["na_col"], w=[("dec", d_)])
        P.op("s", lambda e, d_=d_: e.activation(dec.t[0:64, d_:d_ + 1], dec.t[0:64, d_:d_ + 1], AF.Exp),
             r=[("dec", d_)], w=[("dec", d_)])
    for r in range(8):
        P.op("v", lambda e, r=r: e.tensor_tensor(dec.t[0:64, 3 + r:4 + r],
                                                 ohb.t[0:64, r * 3:r * 3 + 1],
                                                 dec.t[0:64, 0:1], ALU.mult),
             r=["ohb", ("dec", 0)], w=[("decr", r)])
        for d_ in (1, 2):
            P.op("v", lambda e, r=r, d_=d_: e.scalar_tensor_tensor(
                dec.t[0:64, 3 + r:4 + r], dec.t[0:64, d_:d_ + 1],
                ohb.t[0:64, r * 3 + d_:r * 3 + d_ + 1], dec.t[0:64, 3 + r:4 + r],
                ALU.mult, ALU.add),
                r=["ohb", ("dec", d_), ("decr", r)], w=[("decr", r)])
    P.op("g", lambda e: e.memset(acc.t[0:64, :], 0.0), w=[("zacc", 0)])
    for r in range(8):
        src_b, dst_b = (acc, S_in) if r % 2 == 0 else (S_in, acc)
        P.op("v", lambda e, r=r, src_b=src_b, dst_b=dst_b: e.scalar_tensor_tensor(
            dst_b.t[0:64, :], zin.t[0:64, r, :], dec.t[0:64, 3 + r:4 + r], src_b.t[0:64, :],
            ALU.mult, ALU.add),
            r=[("zin", r), ("decr", r), ("zacc", r)], w=[("zacc", r + 1)])
    for slot in range(4):
        P.op("v", lambda e, slot=slot: e.tensor_scalar(
            s_full.t[:, slot * 16:(slot + 1) * 16], sfin.t[:, slot, :],
            sm.t[:, slot:slot + 1],
            None, ALU.mult),
            r=[("sfin", slot), "smb"], w=[("s_full", slot)])
        P.op("v", lambda e, slot=slot: e.scalar_tensor_tensor(
            s_full.t[:, slot * 16:(slot + 1) * 16], sfin.t[:, slot + 4, :],
            sm.t[:, slot + 4:slot + 5],
            s_full.t[:, slot * 16:(slot + 1) * 16],
            ALU.mult, ALU.add),
            r=[("sfin", slot + 4), "smb", ("s_full", slot)], w=[("s_full", slot), "s_full"])
    P.barrier()
    emit_routing_select(P, nc, bufs, s_full, ps_sm, tok_off)
    P.barrier()
    ps_sm.free(); acc.free(); dec.free(); rowb.free(); sm.free(); ohb.free(); sfin.free(); zin.free()
    emit_l1_rest(P, nc, bufs, dram)
    s_full.free(); S_in.free()
    bufs["lam"].free(); bufs["wsn"].free(); bufs["_erev"].free(); bufs["E_small_b"].free()
    bufs["E_small"].free(); bufs["_ldt"].free(); bufs["_tau"].free(); bufs["sp_scratch"].free()
    bufs["s_sb"].free(); bufs["st_sb"].free(); bufs["u_bf"].free(); bufs["xm"].free()
    bufs["ps_trb3"].free()
    if DBG_XM:
        for qq in range(4):
            P.dma("y", lambda e, qq=qq: e.dma_start(
                dram["st1_o"].rearrange("(n p) d -> p n d", p=128)[:, qq * 4:(qq + 1) * 4, :],
                x_res.t[:, qq * 4:(qq + 1) * 4, :]),
                r=[("x_res", i) for i in range(qq * 4, qq * 4 + 4)], w=[("st1", qq)])
    emit_light(P, nc, bufs, dram)
    if DBG_XM:
        for qq in range(4):
            P.dma("y", lambda e, qq=qq: e.dma_start(
                dram["st2_o"].rearrange("(n p) d -> p n d", p=128)[:, qq * 4:(qq + 1) * 4, :],
                x_res.t[:, qq * 4:(qq + 1) * 4, :]),
                r=[("x_res", i) for i in range(qq * 4, qq * 4 + 4)], w=[("st2", qq)])
    emit_heavy(P, nc, bufs, dram)
    for qq in range(4):
        P.dma("y", lambda e, qq=qq: e.dma_start(
            out.rearrange("(n p) d -> p n d", p=128)[:, qq * 4:(qq + 1) * 4, :],
            x_res.t[:, qq * 4:(qq + 1) * 4, :]),
            r=[("x_res", i) for i in range(qq * 4, qq * 4 + 4)], w=[("out", qq)])
    P.barrier()
    P.finalize()
    P.close()
    return nc


def kernel_unfused(**inputs):
    shards = shard_inputs(inputs)
    f32 = lambda a: np.ascontiguousarray(np.asarray(a, dtype=np.float32))
    ncA = build_A()
    mapsA = [{"x": shards[c], "Wk": f32(inputs["L0_attn_W_k"]),
              "Wv": f32(inputs["L0_attn_W_v"])} for c in range(8)]
    resA = run_bass_kernel_spmd(ncA, mapsA, core_ids=list(range(8)))
    stot_all = np.concatenate([resA.results[c]["stot"] for c in range(8)], 0)
    ncB = build_B()
    mapsB = []
    for c in range(8):
        pm = np.zeros((1, 8), np.float32)
        for r in range(4 * (c // 4), c):
            pm[0, r] = 1.0
        mapsB.append({
            "x": shards[c], "Wq": f32(inputs["L0_attn_W_q"]), "Wk": f32(inputs["L0_attn_W_k"]),
            "Wv": f32(inputs["L0_attn_W_v"]), "Wo": f32(inputs["L0_attn_W_o"]),
            "gate_w": f32(inputs["L0_ffn_gate_w"]),
            "w1": f32(inputs["L0_ffn_w1"]).reshape(8 * 1024, 4096),
            "w2": f32(inputs["L0_ffn_w2"]).reshape(8 * 4096, 1024),
            "stot_all": stot_all, "pred_mask": pm,
            "log_tau": f32(inputs["L1_attn_log_tau"]),
            "log_dt": f32(inputs["L1_attn_log_dt"]).reshape(1),
            "norm1_w": f32(inputs["L1_attn_norm1_w"]), "norm1_b": f32(inputs["L1_attn_norm1_b"]),
            "routing": f32(inputs["L1_ffn_routing_token"]),
        })
    resB = run_bass_kernel_spmd(ncB, mapsB, core_ids=list(range(8)))
    zall = np.concatenate([resB.results[c]["zpack"] for c in range(8)], 0)
    ncC = build_C()
    mapsC = []
    for c in range(8):
        b, q = c // 4, c % 4
        d1h = np.zeros((1, 24), np.float32)
        sm = np.zeros((1, 8), np.float32)
        for r in range(8):
            if r // 4 == b:
                sm[0, r] = 1.0
                if r % 4 < q:
                    d = q - (r % 4) - 1
                    d1h[0, r * 3 + d] = 1.0
        mapsC.append({
            "x1": resB.results[c]["x1_out"], "zall": zall, "dist1h": d1h, "smask": sm,
            "tok_off": np.full((1, 1), q * TPC, np.float32),
            "mix": f32(inputs["L1_attn_mix"]), "Dvec": f32(inputs["L1_attn_D"]),
            "log_tau": f32(inputs["L1_attn_log_tau"]),
            "log_dt": f32(inputs["L1_attn_log_dt"]).reshape(1),
            "norm1_w": f32(inputs["L1_attn_norm1_w"]), "norm1_b": f32(inputs["L1_attn_norm1_b"]),
            "norm2_w": f32(inputs["L1_attn_norm2_w"]), "norm2_b": f32(inputs["L1_attn_norm2_b"]),
            "fc1_w": f32(inputs["L1_attn_fc1_w"]), "fc1_b": f32(inputs["L1_attn_fc1_b"]),
            "fc2_w": f32(inputs["L1_attn_fc2_w"]), "fc2_b": f32(inputs["L1_attn_fc2_b"]),
            "routing": f32(inputs["L1_ffn_routing_token"]),
            "gamma_light": f32(inputs["L1_ffn_gamma_light"]),
            "w1_light": f32(inputs["L1_ffn_w1_light"]), "b1_light": f32(inputs["L1_ffn_b1_light"]),
            "w2_light": f32(inputs["L1_ffn_w2_light"]), "b2_light": f32(inputs["L1_ffn_b2_light"]),
            "gamma_heavy": f32(inputs["L1_ffn_gamma_heavy"]),
            "w1_heavy": f32(inputs["L1_ffn_w1_heavy"]), "b1_heavy": f32(inputs["L1_ffn_b1_heavy"]),
            "w2_heavy": f32(inputs["L1_ffn_w2_heavy"]), "b2_heavy": f32(inputs["L1_ffn_b2_heavy"]),
        })
    resC = run_bass_kernel_spmd(ncC, mapsC, core_ids=list(range(8)))
    out = np.zeros((2, 8192, 1024), np.float32)
    for c in range(8):
        b, q = c // 4, c % 4
        out[b, q * TPC:(q + 1) * TPC] = resC.results[c]["out"]
    return out



def build_F():
    nc = bass.Bass("TRN2", target_bir_lowering=False)
    x = nc.dram_tensor("x", [TPC, D], F32, kind="ExternalInput").ap()
    wq = nc.dram_tensor("Wq", [D, D], F32, kind="ExternalInput").ap()
    wk = nc.dram_tensor("Wk", [D, D], F32, kind="ExternalInput").ap()
    wv = nc.dram_tensor("Wv", [D, D], F32, kind="ExternalInput").ap()
    wo = nc.dram_tensor("Wo", [D, D], F32, kind="ExternalInput").ap()
    gate_w = nc.dram_tensor("gate_w", [D, 8], F32, kind="ExternalInput").ap()
    w1 = nc.dram_tensor("w1", [8 * D, 4096], F32, kind="ExternalInput").ap()
    w2 = nc.dram_tensor("w2", [8 * 4096, D], F32, kind="ExternalInput").ap()
    pred_mask = nc.dram_tensor("pred_mask", [1, 8], F32, kind="ExternalInput").ap()
    dist1h = nc.dram_tensor("dist1h", [1, 24], F32, kind="ExternalInput").ap()
    smask_d = nc.dram_tensor("smask", [1, 8], F32, kind="ExternalInput").ap()
    tok_off = nc.dram_tensor("tok_off", [1, 1], F32, kind="ExternalInput").ap()
    names = {
        "mix": [D, 64], "Dvec": [D], "log_tau": [NST], "log_dt": [1],
        "norm1_w": [D], "norm1_b": [D], "norm2_w": [D], "norm2_b": [D],
        "fc1_w": [4096, D], "fc1_b": [4096], "fc2_w": [D, 4096], "fc2_b": [D],
        "routing": [D], "gamma_light": [D], "w1_light": [D, 512], "b1_light": [512],
        "w2_light": [512, D], "b2_light": [D], "gamma_heavy": [D],
        "w1_heavy": [D, 4096], "b1_heavy": [4096], "w2_heavy": [4096, D], "b2_heavy": [D],
    }
    dram = {k: nc.dram_tensor(k, shp, F32, kind="ExternalInput").ap() for k, shp in names.items()}
    out = nc.dram_tensor("out", [TPC, D], F32, kind="ExternalOutput").ap()
    dram["psnap"] = nc.dram_tensor("psnap", [64, NT * NSUB * 1024], BF16).ap()
    dram["hn0T"] = nc.dram_tensor("hn0T_sc", [D, TPC], BF16).ap()
    dram["fc1T"] = nc.dram_tensor("fc1T_sc", [D, 4096], BF16).ap()
    dram["fc2T"] = nc.dram_tensor("fc2T_sc", [4096, D], BF16).ap()
    ccin1 = nc.dram_tensor("ccin1", [64, H * 65], F32).ap()
    ccout1 = nc.dram_tensor("ccout1", [8 * 64, H * 65], F32).ap()
    ccin2 = nc.dram_tensor("ccin2", [66, 1024], F32).ap()
    ccout2 = nc.dram_tensor("ccout2", [8 * 66, 1024], F32).ap()

    P = Prog(nc)
    bufs = {}
    emit_const_tiles(P, nc, bufs)
    bufs["x_res"] = x_res = Buf(nc, "x_res", [128, NT, D], F32)
    bufs["xnT"] = xnT = Buf(nc, "xnT", [128, DC, TPC], BF16)
    c_store = Buf(nc, "c_store", [128, NT * 8], F32)
    bufs["ps_tr"] = ps_tr = Buf(nc, "ps_tr", [128, 1024], BF16, psum=True)
    for qq in range(4):
        P.dma("y", lambda e, qq=qq: e.dma_start(
            x_res.t[:, qq * 4:(qq + 1) * 4, :],
            x.rearrange("(n p) d -> p n d", p=128)[:, qq * 4:(qq + 1) * 4, :]),
            w=[("x_res", i) for i in range(qq * 4, qq * 4 + 4)])
    bufs["gate_sb"] = gate_sb = Buf(nc, "gate_sb", [128, DC, 8], BF16)
    P.dma("g", lambda e: e.dma_start(gate_sb.ap(),
                                     gate_w.rearrange("(a p) e -> p a e", p=128)),
          w=["gate_sb"])
    emit_stream_transpose_w(P, nc, bufs, "fc1", dram["fc1_w"], dram["fc1T"], 4096, 1024, ps_tr)
    emit_stream_transpose_w(P, nc, bufs, "fc2", dram["fc2_w"], dram["fc2T"], 1024, 4096, ps_tr)
    load_w_transposed(P, nc, bufs, "WkT", wk, D, D, ps_tr)
    load_w_transposed(P, nc, bufs, "WvT", wv, D, D, ps_tr)
    emit_ln_xnT(P, nc, bufs, "x_res", x_res, xnT, None, ps_tr, "ln0")
    snaps = Buf(nc, "snaps", [128, NT // 2, H * 65], BF16)
    stot_sb = Buf(nc, "stot_sb", [64, H * 65], F32)
    emit_l0_kv_pass(P, nc, bufs, snaps, stot_sb)
    P.dma("y", lambda e: e.dma_start(ccin1, stot_sb.t[0:64, :]), r=["stot_sb"], w=["ccin1"])
    P.cc(0, lambda e: e.collective_compute(
        "AllGather", ALU.bypass, replica_groups=[list(range(8))],
        ins=[ccin1.opt()], outs=[ccout1.opt()]),
        r=["ccin1"], w=["ccout1"])
    bufs["Jfull"] = Jf = Buf(nc, "Jfull", [128, 128], BF16)
    for pb in (0, 64):
        for fb in (0, 64):
            P.op("v", lambda e, pb=pb, fb=fb: e.tensor_copy(
                Jf.t[pb:pb + 64, fb:fb + 64], bufs["id_b"].t[pb:pb + 64, pb:pb + 64]),
                r=["id_b"], w=["Jfull"])
    offdup = Buf(nc, "offdup", [128, H * 65], F32)
    stin = Buf(nc, "stin", [64, 8, H * 65], F32)
    msk = Buf(nc, "msk", [64, 8], F32)
    mrow = Buf(nc, "mrow", [1, 8], F32)
    offb = Buf(nc, "offb", [64, H * 65], F32)
    offc = Buf(nc, "offc", [64, H * 65], F32)
    ps_off = Buf(nc, "ps_off", [64, 512], F32, psum=True)
    ps_dup = Buf(nc, "ps_dup", [128, 1536], F32, psum=True)
    for r in range(8):
        P.dma("y", lambda e, r=r: e.dma_start(stin.t[0:64, r, :],
                                              ccout1[r * 64:(r + 1) * 64, :]),
              r=["ccout1"], w=[("stin", r)])
    P.dma("g", lambda e: e.dma_start(mrow.ap(), pred_mask), w=["mrow"])
    P.mm([lambda e: e.matmul(ps_off.t[0:64, 0:8], bufs["ones"].t[0:1, 0:64], mrow.ap(),
                             start=True, stop=True)],
         r=["mrow", "ones"], w=["ps_off"])
    P.op("v", lambda e: e.tensor_copy(msk.t[0:64, :], ps_off.t[0:64, 0:8]),
         r=["ps_off"], w=["msk"])
    P.op("g", lambda e: e.memset(offb.t[0:64, :], 0.0), w=[("off", 0)])
    for r in range(8):
        src_b, dst_b = (offb, offc) if r % 2 == 0 else (offc, offb)
        P.op("v", lambda e, r=r, src_b=src_b, dst_b=dst_b: e.scalar_tensor_tensor(
            dst_b.t[0:64, :], stin.t[0:64, r, :], msk.t[0:64, r:r + 1], src_b.t[0:64, :],
            ALU.mult, ALU.add),
            r=[("stin", r), "msk", ("off", r)], w=[("off", r + 1)])
    ob16 = Buf(nc, "ob16", [64, H * 65], BF16)
    P.op("v", lambda e: e.tensor_copy(ob16.t[0:64, :], offc.t[0:64, :]),
         r=[("off", 8)], w=["ob16"])
    P.mm([(lambda e, c0=c0, cw=cw: e.matmul(ps_dup.t[:, c0:c0 + cw], Jf.t[0:64, :],
                                            ob16.t[0:64, c0:c0 + cw], start=True, stop=True))
          for c0, cw in ((0, 512), (512, 512), (1024, 16))],
         r=["ob16", "Jfull"], w=["ps_dup"])
    P.op("v", lambda e: e.tensor_copy(offdup.ap(), ps_dup.t[:, 0:1040]),
         r=["ps_dup"], w=["offdup"])
    P.barrier()
    ps_dup.free(); ps_off.free(); ob16.free(); offc.free(); offb.free(); mrow.free(); msk.free(); stin.free()
    load_w_transposed(P, nc, bufs, "WqT", wq, D, D, ps_tr)
    load_w_transposed(P, nc, bufs, "WoT", wo, D, D, ps_tr)
    emit_l0_attn_main(P, nc, bufs, snaps, offdup)
    P.barrier()
    bufs["WoT"].free(); bufs["WqT"].free()
    offdup.free(); bufs["Jfull"].free(); stot_sb.free(); snaps.free()
    bufs["WvT"].free(); bufs["WkT"].free()
    emit_gating(P, nc, bufs, c_store)
    ps_tr.free()
    emit_moe(P, nc, bufs, w1, w2, c_store)
    P.barrier()
    gate_sb.free(); c_store.free(); xnT.free()
    bufs["hmask"] = Buf(nc, "hmask", [128, NT], F32)
    bufs["xm"] = Buf(nc, "xm", [128, NT, 1024], BF16)
    bufs["u_bf"] = Buf(nc, "u_bf", [128, NT, 1024], BF16)
    bufs["st_sb"] = Buf(nc, "st_sb", [128, NT], F32)
    bufs["s_sb"] = Buf(nc, "s_sb", [128, NT], F32)
    bufs["ps_trb3"] = Buf(nc, "ps_trb3", [128, 1024], BF16, psum=True)
    emit_l1_preamble(P, nc, bufs, {"snaps": True, "store": True, "free_tmp": True}, dram)
    P.dma("y", lambda e: e.dma_start(ccin2[0:64, :], bufs["srun"].t[0:64, :]),
          r=["srun"], w=["ccin2"])
    P.dma("y", lambda e: e.dma_start(
        ccin2[64:66, :].rearrange("r c -> (r c)").rearrange("(p f) -> p f", p=128),
        bufs["s_sb"].ap()),
        r=[("s_sb", i) for i in range(NT)], w=[("ccin2", "s")])
    P.cc(1, lambda e: e.collective_compute(
        "AllGather", ALU.bypass, replica_groups=[list(range(8))],
        ins=[ccin2.opt()], outs=[ccout2.opt()]),
        r=["ccin2", ("ccin2", "s")], w=["ccout2"])
    zall = ccout2
    bufs["S_in"] = S_in = Buf(nc, "S_in", [64, 1024], F32)
    s_full = Buf(nc, "s_full", [128, 64], F32)
    bufs["s_full"] = s_full
    zin = Buf(nc, "zin", [64, 8, 1024], BF16)
    sfin = Buf(nc, "sfin", [128, 8, 16], F32)
    ohb = Buf(nc, "ohb", [64, 24], F32)
    sm = Buf(nc, "smb", [128, 8], F32)
    rowb = Buf(nc, "rowb", [1, 32], F32)
    dec = Buf(nc, "dec", [64, 12], F32)
    acc = Buf(nc, "z_acc", [64, 1024], F32)
    ps_sm = Buf(nc, "ps_sm", [128, 512], F32, psum=True)
    for r in range(8):
        P.dma("g", lambda e, r=r: e.dma_start(zin.t[0:64, r, :], zall[r * 66:r * 66 + 64, :]),
              r=["ccout2"], w=[("zin", r)])
        P.dma("y", lambda e, r=r: e.dma_start(
            sfin.t[:, r, :],
            zall[r * 66 + 64:r * 66 + 66, :].rearrange("r c -> (r c)").rearrange("(p f) -> p f", p=128)),
            r=["ccout2"], w=[("sfin", r)])
    P.dma("g", lambda e: e.dma_start(rowb.t[0:1, 0:24], dist1h), w=["rowb"])
    P.dma("g", lambda e: e.dma_start(rowb.t[0:1, 24:32], smask_d), w=[("rowb", "sm")])
    P.mm([lambda e: e.matmul(ps_sm.t[:, 0:32], bufs["ones"].t[0:1, 0:128], rowb.t[0:1, 0:32],
                             start=True, stop=True)],
         r=["rowb", ("rowb", "sm"), "ones"], w=["ps_sm"])
    P.op("v", lambda e: e.tensor_copy(ohb.t[0:64, 0:24], ps_sm.t[0:64, 0:24]),
         r=["ps_sm"], w=["ohb"])
    P.op("v", lambda e: e.tensor_copy(sm.ap(), ps_sm.t[:, 24:32]),
         r=["ps_sm"], w=["smb"])
    for d_ in range(3):
        P.op("v", lambda e, d_=d_: e.tensor_scalar(dec.t[0:64, d_:d_ + 1], bufs["na_col_ap"],
                                                   2048.0 * (d_ + 1), None, ALU.mult),
             r=["na_col"], w=[("dec", d_)])
        P.op("s", lambda e, d_=d_: e.activation(dec.t[0:64, d_:d_ + 1], dec.t[0:64, d_:d_ + 1], AF.Exp),
             r=[("dec", d_)], w=[("dec", d_)])
    for r in range(8):
        P.op("v", lambda e, r=r: e.tensor_tensor(dec.t[0:64, 3 + r:4 + r],
                                                 ohb.t[0:64, r * 3:r * 3 + 1],
                                                 dec.t[0:64, 0:1], ALU.mult),
             r=["ohb", ("dec", 0)], w=[("decr", r)])
        for d_ in (1, 2):
            P.op("v", lambda e, r=r, d_=d_: e.scalar_tensor_tensor(
                dec.t[0:64, 3 + r:4 + r], dec.t[0:64, d_:d_ + 1],
                ohb.t[0:64, r * 3 + d_:r * 3 + d_ + 1], dec.t[0:64, 3 + r:4 + r],
                ALU.mult, ALU.add),
                r=["ohb", ("dec", d_), ("decr", r)], w=[("decr", r)])
    P.op("g", lambda e: e.memset(acc.t[0:64, :], 0.0), w=[("zacc", 0)])
    for r in range(8):
        src_b, dst_b = (acc, S_in) if r % 2 == 0 else (S_in, acc)
        P.op("v", lambda e, r=r, src_b=src_b, dst_b=dst_b: e.scalar_tensor_tensor(
            dst_b.t[0:64, :], zin.t[0:64, r, :], dec.t[0:64, 3 + r:4 + r], src_b.t[0:64, :],
            ALU.mult, ALU.add),
            r=[("zin", r), ("decr", r), ("zacc", r)], w=[("zacc", r + 1)])
    for slot in range(4):
        P.op("v", lambda e, slot=slot: e.tensor_scalar(
            s_full.t[:, slot * 16:(slot + 1) * 16], sfin.t[:, slot, :],
            sm.t[:, slot:slot + 1], None, ALU.mult),
            r=[("sfin", slot), "smb"], w=[("s_full", slot)])
        P.op("v", lambda e, slot=slot: e.scalar_tensor_tensor(
            s_full.t[:, slot * 16:(slot + 1) * 16], sfin.t[:, slot + 4, :],
            sm.t[:, slot + 4:slot + 5],
            s_full.t[:, slot * 16:(slot + 1) * 16],
            ALU.mult, ALU.add),
            r=[("sfin", slot + 4), "smb", ("s_full", slot)], w=[("s_full", slot), "s_full"])
    P.barrier()
    emit_routing_select(P, nc, bufs, s_full, ps_sm, tok_off)
    P.barrier()
    ps_sm.free(); acc.free(); dec.free(); rowb.free(); sm.free(); ohb.free(); sfin.free(); zin.free()
    emit_l1_rest(P, nc, bufs, dram)
    s_full.free(); S_in.free()
    bufs["lam"].free(); bufs["wsn"].free(); bufs["_erev"].free(); bufs["E_small_b"].free()
    bufs["E_small"].free(); bufs["_ldt"].free(); bufs["_tau"].free(); bufs["sp_scratch"].free()
    bufs["s_sb"].free(); bufs["st_sb"].free(); bufs["u_bf"].free(); bufs["xm"].free()
    bufs["ps_trb3"].free()
    emit_light(P, nc, bufs, dram)
    emit_heavy(P, nc, bufs, dram)
    for qq in range(4):
        P.dma("y", lambda e, qq=qq: e.dma_start(
            out.rearrange("(n p) d -> p n d", p=128)[:, qq * 4:(qq + 1) * 4, :],
            x_res.t[:, qq * 4:(qq + 1) * 4, :]),
            r=[("x_res", i) for i in range(qq * 4, qq * 4 + 4)], w=[("out", qq)])
    P.barrier()
    P.finalize()
    P.close()
    return nc


def kernel(**inputs):
    shards = shard_inputs(inputs)
    f32 = lambda a: np.ascontiguousarray(np.asarray(a, dtype=np.float32))
    ncF = build_F()
    maps = []
    for c in range(8):
        b, q = c // 4, c % 4
        pm = np.zeros((1, 8), np.float32)
        for r in range(4 * b, c):
            pm[0, r] = 1.0
        d1h = np.zeros((1, 24), np.float32)
        sm = np.zeros((1, 8), np.float32)
        for r in range(8):
            if r // 4 == b:
                sm[0, r] = 1.0
                if r % 4 < q:
                    d1h[0, r * 3 + (q - (r % 4) - 1)] = 1.0
        maps.append({
            "x": shards[c], "Wq": f32(inputs["L0_attn_W_q"]), "Wk": f32(inputs["L0_attn_W_k"]),
            "Wv": f32(inputs["L0_attn_W_v"]), "Wo": f32(inputs["L0_attn_W_o"]),
            "gate_w": f32(inputs["L0_ffn_gate_w"]),
            "w1": f32(inputs["L0_ffn_w1"]).reshape(8 * 1024, 4096),
            "w2": f32(inputs["L0_ffn_w2"]).reshape(8 * 4096, 1024),
            "pred_mask": pm, "dist1h": d1h, "smask": sm,
            "tok_off": np.full((1, 1), q * TPC, np.float32),
            "mix": f32(inputs["L1_attn_mix"]), "Dvec": f32(inputs["L1_attn_D"]),
            "log_tau": f32(inputs["L1_attn_log_tau"]),
            "log_dt": f32(inputs["L1_attn_log_dt"]).reshape(1),
            "norm1_w": f32(inputs["L1_attn_norm1_w"]), "norm1_b": f32(inputs["L1_attn_norm1_b"]),
            "norm2_w": f32(inputs["L1_attn_norm2_w"]), "norm2_b": f32(inputs["L1_attn_norm2_b"]),
            "fc1_w": f32(inputs["L1_attn_fc1_w"]), "fc1_b": f32(inputs["L1_attn_fc1_b"]),
            "fc2_w": f32(inputs["L1_attn_fc2_w"]), "fc2_b": f32(inputs["L1_attn_fc2_b"]),
            "routing": f32(inputs["L1_ffn_routing_token"]),
            "gamma_light": f32(inputs["L1_ffn_gamma_light"]),
            "w1_light": f32(inputs["L1_ffn_w1_light"]), "b1_light": f32(inputs["L1_ffn_b1_light"]),
            "w2_light": f32(inputs["L1_ffn_w2_light"]), "b2_light": f32(inputs["L1_ffn_b2_light"]),
            "gamma_heavy": f32(inputs["L1_ffn_gamma_heavy"]),
            "w1_heavy": f32(inputs["L1_ffn_w1_heavy"]), "b1_heavy": f32(inputs["L1_ffn_b1_heavy"]),
            "w2_heavy": f32(inputs["L1_ffn_w2_heavy"]), "b2_heavy": f32(inputs["L1_ffn_b2_heavy"]),
        })
    res = run_bass_kernel_spmd(ncF, maps, core_ids=list(range(8)))
    out = np.zeros((2, 8192, 1024), np.float32)
    for c in range(8):
        b, q = c // 4, c % 4
        out[b, q * TPC:(q + 1) * TPC] = res.results[c]["out"]
    return out


# version 05
# speedup vs baseline: 1.2515x; 1.2515x over previous
import numpy as np
import concourse.bass as bass

DBG_XM = False
import concourse.mybir as mybir
from concourse.bass_utils import run_bass_kernel_spmd

F32 = mybir.dt.float32
BF16 = mybir.dt.bfloat16
AF = mybir.ActivationFunctionType
ALU = mybir.AluOpType
AX = mybir.AxisListType.X

TPC = 2048
NT = TPC // 128
D = 1024
DC = D // 128
H = 16
DK = 64
NST = 64
LSUB = 32
NSUB = 128 // LSUB
EPS_LN = 1e-5
EPS_ATTN = 1e-6

ENG = ("t", "v", "s", "g", "y")


class Prog:

    def __init__(self, nc):
        self.nc = nc
        self.q = {e: [] for e in ENG}
        self.sems = {}
        self.semval = {}
        self.last_w = {}
        self.readers = {}
        self.waited = {e: {} for e in ENG}
        self._stack = []
        names = ["done_t", "done_v", "done_s", "done_g", "done_y",
                 "cc0", "cc1", "cc2"]
        self.NDMA = 4
        for q in ("y", "g", "s"):
            for j in range(self.NDMA):
                names.append("dma_%s%d" % (q, j))
        for name in names:
            cm = nc.semaphore(name)
            handle = cm.__enter__()
            self._stack.append(cm)
            self.sems[name] = handle
            self.semval[name] = 0
        self._dma_rot = {"y": 0, "g": 0, "s": 0}

    def _wait(self, e, semname, val):
        if self.waited[e].get(semname, 0) >= val:
            return
        sem = self.sems[semname]
        self.q[e].append(lambda eng, s=sem, v=val: eng.wait_ge(s, v))
        self.waited[e][semname] = val

    def _deps(self, e, r, w):
        for k in r:
            lw = self.last_w.get(k)
            if lw:
                self._wait(e, *lw)
        for k in w:
            lw = self.last_w.get(k)
            if lw:
                self._wait(e, *lw)
            for sn, v in self.readers.get(k, {}).items():
                self._wait(e, sn, v)

    def _commit(self, e, semname, inc, r, w):
        self.semval[semname] += inc
        val = self.semval[semname]
        tok = (semname, val)
        for k in w:
            self.last_w[k] = tok
            self.readers[k] = {}
        for k in r:
            d = self.readers.setdefault(k, {})
            if d.get(semname, 0) < val:
                d[semname] = val
        return tok

    def op(self, e, fn, r=(), w=()):
        self._deps(e, r, w)
        semname = "done_" + e
        sem = self.sems[semname]
        self.q[e].append(lambda eng, f=fn, s=sem: f(eng).then_inc(s, 1))
        return self._commit(e, semname, 1, r, w)

    def mm(self, fns, r=(), w=()):
        self._deps("t", r, w)
        sem = self.sems["done_t"]
        for f in fns[:-1]:
            self.q["t"].append(lambda eng, f=f: f(eng))
        self.q["t"].append(lambda eng, f=fns[-1], s=sem: f(eng).then_inc(s, 1))
        return self._commit("t", "done_t", 1, r, w)

    def dma(self, e, fn, r=(), w=()):
        assert e in ("y", "g", "s")
        j = self._dma_rot[e]
        self._dma_rot[e] = (j + 1) % self.NDMA
        semname = "dma_%s%d" % (e, j)
        self._wait(e, semname, self.semval[semname])
        self._deps(e, r, w)
        sem = self.sems[semname]
        self.q[e].append(lambda eng, f=fn, s=sem: f(eng).then_inc(s, 16))
        return self._commit(e, semname, 16, r, w)

    def cc(self, n, fn, r=(), w=()):
        semname = "cc%d" % n
        self._deps("g", r, w)
        sem = self.sems[semname]
        self.q["g"].append(lambda eng, f=fn, s=sem: f(eng).then_inc(s))
        self.semval[semname] += 1
        tok = (semname, 1)
        for k in w:
            self.last_w[k] = tok
            self.readers[k] = {}
        for k in r:
            d = self.readers.setdefault(k, {})
            d[semname] = 1
        return tok

    def barrier(self):
        for e in ENG:
            for sn, v in self.semval.items():
                if v > 0:
                    self._wait(e, sn, v)

    def finalize(self):
        nc = self.nc
        with nc.Block() as block:
            @block.tensor
            def _(eng):
                for f in self.q["t"]:
                    f(eng)

            @block.vector
            def _(eng):
                for f in self.q["v"]:
                    f(eng)

            @block.scalar
            def _(eng):
                for f in self.q["s"]:
                    f(eng)

            @block.gpsimd
            def _(eng):
                for f in self.q["g"]:
                    f(eng)

            @block.sync
            def _(eng):
                for f in self.q["y"]:
                    f(eng)

    def close(self):
        for h in reversed(self._stack):
            h.__exit__(None, None, None)


class Buf:

    def __init__(self, nc, name, shape, dtype, psum=False):
        if psum:
            esz = 2 if dtype == BF16 else 4
            free = 1
            for s in shape[1:]:
                free *= s
            assert (free * esz) % 2048 == 0, (name, shape, dtype)
        cm = nc.psum_tensor(name, shape, dtype) if psum else nc.sbuf_tensor(name, shape, dtype)
        self.t = cm.__enter__()
        self._cm = cm
        self.name = name

    def __getitem__(self, idx):
        return self.t[idx]

    def ap(self):
        return self.t[:]

    def free(self):
        self._cm.__exit__(None, None, None)



CONSTS = {0: EPS_LN, 1: 1.0}


def emit_const_tiles(P, nc, bufs):
    bufs["id_f"] = idf = Buf(nc, "id_f", [128, 128], F32)
    bufs["id_b"] = idb = Buf(nc, "id_b", [128, 128], BF16)
    bufs["ones"] = ones = Buf(nc, "ones_t", [128, 128], F32)
    bufs["cst"] = cst = Buf(nc, "cst", [128, 8], F32)
    P.op("g", lambda e: e.memset(ones.ap(), 1.0), w=["ones"])
    for col, val in CONSTS.items():
        P.op("g", lambda e, col=col, val=val: e.memset(cst.t[:, col:col + 1], float(val)),
             w=["cst"])
    P.op("g", lambda e: e.memset(idf.ap(), 1.0), w=["id_f"])
    P.op("g", lambda e: e.affine_select(idf.ap(), idf.ap(), [[1, 128]], ALU.is_equal,
                                        0.0, base=0, channel_multiplier=-1),
         r=["id_f"], w=["id_f"])
    P.op("v", lambda e: e.tensor_copy(idb.ap(), idf.ap()), r=["id_f"], w=["id_b"])


def load_w_transposed(P, nc, bufs, key, dram_ap, rows, cols, ps):
    rc = rows // 128
    cc = cols // 128
    out = Buf(nc, key, [128, rc, cols], BF16)
    bufs[key] = out
    nat = Buf(nc, key + "_nat", [128, cc, rows], BF16)
    for half in range(2):
        sl = slice(half * (cc // 2), (half + 1) * (cc // 2)) if cc >= 2 else slice(0, cc)
        P.dma("g", lambda e, sl=sl: e.dma_start(
            nat.t[:, sl, :],
            dram_ap.rearrange("(jb p) d -> p jb d", p=128)[:, sl, :]),
            w=[(key, "nat", half)])
        if cc < 2:
            break
    for a in range(rc):
        for jb in range(cc):
            half = 0 if cc < 2 or jb < cc // 2 else 1
            P.mm([lambda e, a=a, jb=jb: e.transpose(
                ps.t[:, 0:128], nat.t[:, jb, a * 128:(a + 1) * 128], bufs["id_b"].ap())],
                r=[(key, "nat", half), "id_b"], w=[ps.name])
            P.op("v", lambda e, a=a, jb=jb: e.tensor_copy(
                out.t[:, a, jb * 128:(jb + 1) * 128], ps.t[:, 0:128]),
                r=[ps.name], w=[key])
    P.barrier()
    nat.free()
    return out


def emit_ln_xnT(P, nc, bufs, x_key, xbuf, out_xnT, out_xn_bf, ps_tr, prefix,
                w_rep=None, b_rep=None, store_xn=None):
    mu = Buf(nc, prefix + "_mu", [128, 4], F32)
    cen = Buf(nc, prefix + "_cen", [128, 1024], F32)
    sq = Buf(nc, prefix + "_sq", [128, 1024], F32)
    cb = Buf(nc, prefix + "_cb", [128, 1024], BF16)
    for i in range(NT):
        xk = (x_key, i)
        P.op("v", lambda e, i=i: e.reduce_sum(mu.t[:, 0:1], xbuf.t[:, i, :], axis=AX),
             r=[xk], w=[(prefix, "mu")])
        P.op("v", lambda e: e.tensor_scalar(mu.t[:, 1:2], mu.t[:, 0:1], 1.0 / D, None, ALU.mult),
             r=[(prefix, "mu")], w=[(prefix, "mu1")])
        P.op("v", lambda e, i=i: e.tensor_scalar(cen.ap(), xbuf.t[:, i, :], mu.t[:, 1:2], None, ALU.subtract),
             r=[xk, (prefix, "mu1")], w=[(prefix, "cen")])
        P.op("s", lambda e: e.activation(sq.ap(), cen.ap(), AF.Square, accum_out=mu.t[:, 2:3]),
             r=[(prefix, "cen")], w=[(prefix, "sq")])
        P.op("s", lambda e: e.activation(mu.t[:, 3:4], mu.t[:, 2:3], AF.Ln,
                                         bias=bufs["cst"].t[:, 0:1], scale=1.0 / D),
             r=[(prefix, "sq"), "cst"], w=[(prefix, "ln")])
        P.op("s", lambda e: e.activation(mu.t[:, 3:4], mu.t[:, 3:4], AF.Exp, scale=-0.5),
             r=[(prefix, "ln")], w=[(prefix, "rstd")])
        if w_rep is None:
            P.op("v", lambda e: e.tensor_scalar(cb.ap(), cen.ap(), mu.t[:, 3:4], None, ALU.mult),
                 r=[(prefix, "cen"), (prefix, "rstd")], w=[(prefix, "cb")])
        else:
            P.op("v", lambda e: e.tensor_scalar(cen.ap(), cen.ap(), mu.t[:, 3:4], None, ALU.mult),
                 r=[(prefix, "cen"), (prefix, "rstd")], w=[(prefix, "cen")])
            P.op("v", lambda e: e.tensor_tensor(cen.ap(), cen.ap(), w_rep.ap(), ALU.mult),
                 r=[(prefix, "cen"), w_rep.name], w=[(prefix, "cen")])
            P.op("v", lambda e: e.tensor_tensor(cb.ap(), cen.ap(), b_rep.ap(), ALU.add),
                 r=[(prefix, "cen"), b_rep.name], w=[(prefix, "cb")])
        if store_xn is not None:
            P.op("s", lambda e, i=i: e.copy(store_xn.t[:, i, :], cb.ap()),
                 r=[(prefix, "cb")], w=[(store_xn.name, i)])
        if out_xn_bf is not None:
            P.op("s", lambda e, i=i: e.copy(out_xn_bf.t[:, i, :], cb.ap()),
                 r=[(prefix, "cb")], w=[(out_xn_bf.name, i)])
        if out_xnT is not None:
            for a in range(DC):
                P.mm([lambda e, a=a: e.transpose(ps_tr.t[:, 0:128],
                                                 cb.t[:, a * 128:(a + 1) * 128], bufs["id_b"].ap())],
                     r=[(prefix, "cb"), "id_b"], w=[ps_tr.name])
                P.op("v", lambda e, a=a, i=i: e.tensor_copy(
                    out_xnT.t[:, a, i * 128:(i + 1) * 128], ps_tr.t[:, 0:128]),
                    r=[ps_tr.name], w=[(out_xnT.name, a, i)])
    P.barrier()
    cb.free(); sq.free(); cen.free(); mu.free()


def emit_phi(P, nc, out_bf, psum_ap, pkey, okey, tmp1, tmp2, prefix):
    P.op("v", lambda e: e.tensor_scalar(tmp1.ap(), psum_ap, 0.0, None, ALU.min),
         r=[pkey], w=[(prefix, "t1")])
    P.op("s", lambda e: e.activation(tmp2.ap(), tmp1.ap(), AF.Exp),
         r=[(prefix, "t1")], w=[(prefix, "t2")])
    P.op("v", lambda e: e.scalar_tensor_tensor(out_bf, psum_ap, 0.0, tmp2.ap(), ALU.max, ALU.add),
         r=[pkey, (prefix, "t2")], w=[okey])


def emit_l0_kv_pass(P, nc, bufs, snaps, stot_sb):
    xnT = bufs["xnT"]
    ps_k = Buf(nc, "ps_kv", [128, 1024], F32, psum=True)
    ps_v = ps_k
    ps_S = Buf(nc, "ps_S", [64, H * 128], F32, psum=True)
    scum = Buf(nc, "scum", [64, H * 65], F32)
    P.op("g", lambda e: e.memset(scum.t[0:64, :], 0.0), w=["scum"])
    phik = Buf(nc, "phik", [128, 1024], BF16)
    vaug = Buf(nc, "vaug", [128, H * 65], BF16)
    t1 = Buf(nc, "kv_t1", [128, 1024], F32)
    t2 = Buf(nc, "kv_t2", [128, 1024], F32)
    P.op("g", lambda e: e.memset(vaug.t[:].rearrange("p (h c) -> p h c", h=H)[:, :, 64:65], 1.0),
         w=[("vaug", "ones")])
    for i in range(NT):
        P.mm([(lambda e, a=a, half=half, i=i: e.matmul(
            ps_k.t[:, half * 512:(half + 1) * 512],
            xnT.t[:, a, i * 128:(i + 1) * 128],
            bufs["WkT"].t[:, a, half * 512:(half + 1) * 512],
            start=(a == 0), stop=(a == DC - 1)))
            for half in range(2) for a in range(DC)],
            r=[("xnT", a, i) for a in range(DC)] + ["WkT"], w=["ps_kv"])
        if i == 0 and "dbg_kps0" in bufs:
            P.op("s", lambda e: e.copy(bufs["dbg_kps0"].ap(), ps_k.ap()), r=["ps_kv"], w=["dkps0"])
        emit_phi(P, nc, phik.ap(), ps_k.ap(), "ps_kv", "phik", t1, t2, "kvphi")
        P.mm([(lambda e, a=a, half=half, i=i: e.matmul(
            ps_v.t[:, half * 512:(half + 1) * 512],
            xnT.t[:, a, i * 128:(i + 1) * 128],
            bufs["WvT"].t[:, a, half * 512:(half + 1) * 512],
            start=(a == 0), stop=(a == DC - 1)))
            for half in range(2) for a in range(DC)],
            r=[("xnT", a, i) for a in range(DC)] + ["WvT"], w=["ps_kv"])
        P.op("v", lambda e: e.tensor_copy(
            vaug.t[:].rearrange("p (h c) -> p h c", h=H)[:, :, 0:64],
            ps_v.t[:].rearrange("p (h c) -> p h c", h=H)),
            r=["ps_kv"], w=[("vaug", "data")])
        P.mm([(lambda e, h=h: e.matmul(
            ps_S.t[0:64, h * 128:h * 128 + 65],
            phik.t[:, h * 64:(h + 1) * 64],
            vaug.t[:, h * 65:(h + 1) * 65],
            start=True, stop=True))
            for h in range(H)],
            r=["phik", ("vaug", "data"), ("vaug", "ones")], w=["ps_S"])
        P.op("v", lambda e: e.tensor_tensor(
            scum.t[0:64, :].rearrange("p (h c) -> p h c", h=H),
            scum.t[0:64, :].rearrange("p (h c) -> p h c", h=H),
            ps_S.t[0:64, :].rearrange("p (h c) -> p h c", h=H)[:, :, 0:65],
            ALU.add),
            r=["ps_S", "scum"], w=["scum"])
        if snaps is not None and i == 0 and "dbg_phik0" in bufs:
            P.op("s", lambda e: e.copy(bufs["dbg_phik0"].ap(), phik.ap()), r=["phik"], w=["dphik0"])
            P.op("s", lambda e: e.copy(bufs["dbg_vaug0"].ap(), vaug.ap()),
                 r=[("vaug", "data"), ("vaug", "ones")], w=["dvaug0"])
        if snaps is not None:
            P.op("v", lambda e, i=i: e.tensor_copy(
                snaps.t[(i % 2) * 64:(i % 2) * 64 + 64, i // 2, :], scum.t[0:64, :]),
                 r=["scum"], w=[("snaps", i)])
    if stot_sb is not None:
        P.op("v", lambda e: e.tensor_copy(stot_sb.t[0:64, :], scum.t[0:64, :]),
             r=["scum"], w=["stot_sb"])
    P.barrier()
    t2.free(); t1.free(); vaug.free(); phik.free(); scum.free(); ps_S.free(); ps_k.free()



def build_A(dbg=False):
    nc = bass.Bass("TRN2", target_bir_lowering=False)
    x = nc.dram_tensor("x", [TPC, D], F32, kind="ExternalInput").ap()
    wk = nc.dram_tensor("Wk", [D, D], F32, kind="ExternalInput").ap()
    wv = nc.dram_tensor("Wv", [D, D], F32, kind="ExternalInput").ap()
    stot = nc.dram_tensor("stot", [64, H * 65], F32, kind="ExternalOutput").ap()
    snaps_o = nc.dram_tensor("snaps_o", [64, NT * H * 65], F32, kind="ExternalOutput").ap() if dbg else None
    phik0_o = nc.dram_tensor("phik0_o", [128, 1024], F32, kind="ExternalOutput").ap() if dbg else None
    vaug0_o = nc.dram_tensor("vaug0_o", [128, H * 65], F32, kind="ExternalOutput").ap() if dbg else None
    kps0_o = nc.dram_tensor("kps0_o", [128, 1024], F32, kind="ExternalOutput").ap() if dbg else None
    xnT_o = nc.dram_tensor("xnT_o", [128, 8 * 128], F32, kind="ExternalOutput").ap() if dbg else None
    wkT_o = nc.dram_tensor("wkT_o", [128, 1024], F32, kind="ExternalOutput").ap() if dbg else None
    P = Prog(nc)
    bufs = {}
    ps_tr = Buf(nc, "ps_tr", [128, 1024], BF16, psum=True)
    emit_const_tiles(P, nc, bufs)
    load_w_transposed(P, nc, bufs, "WkT", wk, D, D, ps_tr)
    load_w_transposed(P, nc, bufs, "WvT", wv, D, D, ps_tr)
    xr = Buf(nc, "xr", [128, NT, D], F32)
    for qq in range(4):
        P.dma("y", lambda e, qq=qq: e.dma_start(
            xr.t[:, qq * 4:(qq + 1) * 4, :],
            x.rearrange("(n p) d -> p n d", p=128)[:, qq * 4:(qq + 1) * 4, :]),
            w=[("xr", i) for i in range(qq * 4, qq * 4 + 4)])
    xnT = Buf(nc, "xnT", [128, DC, TPC], BF16)
    bufs["xnT"] = xnT
    emit_ln_xnT(P, nc, bufs, "xr", xr, xnT, None, ps_tr, "ln0")
    stot_sb = Buf(nc, "stot_sb", [64, H * 65], F32)
    snaps = Buf(nc, "snaps", [64, NT, H * 65], BF16) if dbg else None
    if dbg:
        bufs["dbg_phik0"] = Buf(nc, "dbg_phik0", [128, 1024], F32)
        bufs["dbg_vaug0"] = Buf(nc, "dbg_vaug0", [128, H * 65], F32)
        bufs["dbg_kps0"] = Buf(nc, "dbg_kps0", [128, 1024], F32)
    emit_l0_kv_pass(P, nc, bufs, snaps, stot_sb)
    P.dma("y", lambda e: e.dma_start(stot, stot_sb.t[0:64, :]), r=["stot_sb"], w=["stot_out"])
    if dbg:
        P.dma("g", lambda e: e.dma_start(snaps_o, snaps.t[0:64, :, :].rearrange("p n c -> p (n c)")),
              r=[("snaps", i) for i in range(NT)], w=["snaps_out"])
        P.dma("g", lambda e: e.dma_start(phik0_o, bufs["dbg_phik0"].ap()), r=["dphik0"], w=["o_p0"])
        P.dma("g", lambda e: e.dma_start(kps0_o, bufs["dbg_kps0"].ap()), r=["dkps0"], w=["o_k0"])
        for a in range(8):
            P.dma("g", lambda e, a=a: e.dma_start(xnT_o.rearrange("p (a t) -> p a t", a=8)[:, a, :],
                                                  xnT.t[:, a, 0:128]),
                  r=[("xnT", a, 0)], w=["o_xnT"])
        P.dma("g", lambda e: e.dma_start(wkT_o, bufs["WkT"].t[:, 0, :]), r=["WkT"], w=["o_wkT"])
        P.dma("g", lambda e: e.dma_start(vaug0_o, bufs["dbg_vaug0"].ap()), r=["dvaug0"], w=["o_v0"])
    P.barrier()
    P.finalize()
    P.close()
    return nc



def shard_inputs(inputs):
    x = np.ascontiguousarray(inputs["L0_attn_x"], dtype=np.float32)
    shards = []
    for c in range(8):
        b, q = c // 4, c % 4
        shards.append(np.ascontiguousarray(x[b, q * TPC:(q + 1) * TPC, :]))
    return shards



def emit_row_broadcast(P, nc, bufs, key, dram_row_ap, width, ps, dtype=F32):
    out = Buf(nc, key, [128, width], dtype)
    bufs[key] = out
    row = Buf(nc, key + "_row", [1, width], F32)
    P.dma("g", lambda e: e.dma_start(row.ap(), dram_row_ap), w=[(key, "row")])
    for c0 in range(0, width, 512):
        cw = min(512, width - c0)
        P.mm([lambda e, c0=c0, cw=cw: e.matmul(ps.t[:, 0:cw], bufs["ones"].t[0:1, :],
                                               row.t[:, c0:c0 + cw], start=True, stop=True)],
             r=[(key, "row"), "ones"], w=[ps.name])
        P.op("v", lambda e, c0=c0, cw=cw: e.tensor_copy(out.t[:, c0:c0 + cw], ps.t[:, 0:cw]),
             r=[ps.name], w=[key])
    P.barrier()
    row.free()
    return out


def emit_phi_block(P, nc, bufs, out_bf_ap, okey, psum_ap, pkey, t1, t2, prefix):
    emit_phi(P, nc, out_bf_ap, psum_ap, pkey, okey, t1, t2, prefix)


def emit_l0_attn_main(P, nc, bufs, snaps, offdup):
    xnT = bufs["xnT"]
    x_res = bufs["x_res"]
    proj_ps = Buf(nc, "proj_ps", [128, 1024], F32, psum=True)
    at_ps = Buf(nc, "at_ps", [128, 512], F32, psum=True)
    y_ps = Buf(nc, "y_ps", [128, H * 128], F32, psum=True)
    phiqT = Buf(nc, "phiqT", [128, DC, 128], BF16)
    phikT = Buf(nc, "phikT", [128, DC, 128], BF16)
    vaug = Buf(nc, "vaug2", [128, H * 65], BF16)
    comb = Buf(nc, "comb", [128, H * 65], BF16)
    at_sb = Buf(nc, "at_sb", [128, 128], BF16)
    y_nat = Buf(nc, "y_nat", [128, 1024], BF16)
    yT = Buf(nc, "yT", [128, DC, 128], BF16)
    den = Buf(nc, "den", [128, 3 * 16], F32)
    t1 = Buf(nc, "am_t1", [128, 128], F32)
    t2 = Buf(nc, "am_t2", [128, 128], F32)
    tri = Buf(nc, "tri", [128, 128], BF16)
    P.op("g", lambda e: e.memset(tri.ap(), 1.0), w=["tri"])
    P.op("g", lambda e: e.affine_select(tri.ap(), tri.ap(), [[1, 128]], ALU.is_ge,
                                        0.0, base=0, channel_multiplier=-1),
         r=["tri"], w=["tri"])
    P.op("g", lambda e: e.memset(vaug.t[:].rearrange("p (h c) -> p h c", h=H)[:, :, 64:65], 1.0),
         w=[("vaug2", "ones")])
    for i in range(NT):
        if i == 0:
            P.op("v", lambda e: e.tensor_copy(comb.ap(), offdup.ap()),
                 r=[offdup.name], w=["comb"])
        else:
            sb = ((i - 1) % 2) * 64
            blk = (i - 1) // 2
            P.mm([(lambda e, sb=sb, blk=blk, c0=c0: e.matmul(
                proj_ps.t[:, c0:c0 + 512], bufs["Jfull"].t[sb:sb + 64, :],
                snaps.t[sb:sb + 64, blk, c0:c0 + 512], start=True, stop=True))
                for c0 in (0, 512)] +
                [(lambda e, sb=sb, blk=blk: e.matmul(
                    at_ps.t[:, 0:16], bufs["Jfull"].t[sb:sb + 64, :],
                    snaps.t[sb:sb + 64, blk, 1024:1040], start=True, stop=True))],
                r=[("snaps", i - 1), "Jfull"],
                w=["proj_ps", "at_ps"])
            P.op("v", lambda e: e.tensor_tensor(comb.t[:, 0:1024], proj_ps.ap(),
                                                offdup.t[:, 0:1024], ALU.add),
                 r=["proj_ps"], w=["comb"])
            P.op("v", lambda e: e.tensor_tensor(comb.t[:, 1024:1040], at_ps.t[:, 0:16],
                                                offdup.t[:, 1024:1040], ALU.add),
                 r=["at_ps", offdup.name], w=["comb"])
        for which, wkey, outb, okey in (("q", "WqT", phiqT, "phiqT"), ("k", "WkT", phikT, "phikT")):
            for jt in range(DC):
                P.mm([(lambda e, a=a, jt=jt, i=i, wkey=wkey: e.matmul(
                    proj_ps.t[:, jt * 128:(jt + 1) * 128],
                    bufs[wkey].t[:, a, jt * 128:(jt + 1) * 128],
                    xnT.t[:, a, i * 128:(i + 1) * 128],
                    start=(a == 0), stop=(a == DC - 1)))
                    for a in range(DC)],
                    r=[("xnT", a, i) for a in range(DC)] + [wkey], w=["proj_ps"])
            for jt in range(DC):
                emit_phi(P, nc, outb.t[:, jt, :], proj_ps.t[:, jt * 128:(jt + 1) * 128],
                         "proj_ps", (okey, jt), t1, t2, "amphi" + which)
        P.mm([(lambda e, a=a, half=half, i=i: e.matmul(
            proj_ps.t[:, half * 512:(half + 1) * 512],
            xnT.t[:, a, i * 128:(i + 1) * 128],
            bufs["WvT"].t[:, a, half * 512:(half + 1) * 512],
            start=(a == 0), stop=(a == DC - 1)))
            for half in range(2) for a in range(DC)],
            r=[("xnT", a, i) for a in range(DC)] + ["WvT"],
            w=["proj_ps"])
        P.op("v", lambda e: e.tensor_copy(
            vaug.t[:].rearrange("p (h c) -> p h c", h=H)[:, :, 0:64],
            proj_ps.t[:].rearrange("p (h c) -> p h c", h=H)),
            r=["proj_ps"], w=[("vaug2", "data")])
        for h in range(H):
            jt, base = h // 2, (h % 2) * 64
            P.mm([lambda e, jt=jt, base=base: e.matmul(
                at_ps.t[:, 0:128],
                bufs_get(phikT, jt, base),
                bufs_get(phiqT, jt, base),
                start=True, stop=True)],
                r=[("phikT", jt), ("phiqT", jt)], w=["at_ps"])
            P.op("v", lambda e: e.tensor_tensor(at_sb.ap(), at_ps.t[:, 0:128], tri.ap(), ALU.mult),
                 r=["at_ps", "tri"], w=["at_sb"])
            P.mm([
                (lambda e, h=h: e.matmul(
                    y_ps.t[:, h * 128:h * 128 + 65],
                    at_sb.ap(),
                    vaug.t[:, h * 65:(h + 1) * 65],
                    start=True, stop=False)),
                (lambda e, h=h, jt=jt, base=base: e.matmul(
                    y_ps.t[:, h * 128:h * 128 + 65],
                    bufs_get(phiqT, jt, base),
                    comb.t[base:base + 64, h * 65:(h + 1) * 65],
                    start=False, stop=True)),
            ], r=["at_sb", ("vaug2", "data"), ("vaug2", "ones"), ("phiqT", jt), "comb"],
                w=[("y_ps", h)])
        P.op("v", lambda e: e.tensor_scalar(
            den.t[:, 0:16],
            y_ps.t[:].rearrange("p (h c) -> p h c", c=128)[:, :, 64],
            EPS_ATTN, None, ALU.max),
            r=[("y_ps", h) for h in range(H)], w=["den"])
        P.op("v", lambda e: e.reciprocal(den.t[:, 16:32], den.t[:, 0:16]),
             r=["den"], w=[("den", "rec")])
        P.op("v", lambda e: e.tensor_tensor(
            y_nat.t[:].rearrange("p (h c) -> p h c", c=64),
            y_ps.t[:].rearrange("p (h c) -> p h c", c=128)[:, :, 0:64],
            den.t[:, 16:32].unsqueeze(2).broadcast_to([128, 16, 64]),
            ALU.mult),
            r=[("y_ps", h) for h in range(H)] + [("den", "rec")], w=["y_nat"])
        for a in range(DC):
            P.mm([lambda e, a=a: e.transpose(bufs["ps_tr"].t[:, 0:128],
                                             y_nat.t[:, a * 128:(a + 1) * 128],
                                             bufs["id_b"].ap())],
                 r=["y_nat", "id_b"], w=["ps_tr"])
            P.op("v", lambda e, a=a: e.tensor_copy(yT.t[:, a, :], bufs["ps_tr"].t[:, 0:128]),
                 r=["ps_tr"], w=["yT"])
        P.mm([(lambda e, a=a, half=half: e.matmul(
            proj_ps.t[:, half * 512:(half + 1) * 512],
            yT.t[:, a, :],
            bufs["WoT"].t[:, a, half * 512:(half + 1) * 512],
            start=(a == 0), stop=(a == DC - 1)))
            for half in range(2) for a in range(DC)],
            r=["yT", "WoT"], w=["proj_ps"])
        P.op("v", lambda e, i=i: e.tensor_tensor(
            bufs["x_res"].t[:, i, :], bufs["x_res"].t[:, i, :], proj_ps.ap(), ALU.add),
            r=["proj_ps", ("x_res", i)], w=[("x_res", i)])
    P.barrier()
    tri.free(); t2.free(); t1.free(); den.free(); yT.free(); y_nat.free(); at_sb.free()
    comb.free(); vaug.free(); phikT.free(); phiqT.free()
    y_ps.free(); at_ps.free(); proj_ps.free()


def bufs_get(buf, jt, base):
    return buf.t[base:base + 64, jt, :]


def emit_gating(P, nc, bufs, c_store):
    xnT = bufs["xnT"]
    g_ps = Buf(nc, "g_ps", [128, 512], F32, psum=True)
    lg = Buf(nc, "lg", [128, 8], F32)
    eq1 = Buf(nc, "eq1", [128, 8], F32)
    t2b = Buf(nc, "t2b", [128, 8], F32)
    m = Buf(nc, "m12", [128, 8], F32)
    for i in range(NT):
        P.mm([(lambda e, a=a, i=i: e.matmul(
            g_ps.t[:, 0:8],
            xnT.t[:, a, i * 128:(i + 1) * 128],
            bufs["gate_sb"].t[:, a, :],
            start=(a == 0), stop=(a == DC - 1)))
            for a in range(DC)],
            r=[("xnT", a, i) for a in range(DC)] + ["gate_sb"], w=["g_ps"])
        P.op("v", lambda e: e.tensor_copy(lg.ap(), g_ps.t[:, 0:8]), r=["g_ps"], w=["lg"])
        P.op("v", lambda e: e.reduce_max(m.t[:, 0:1], lg.ap(), axis=AX), r=["lg"], w=[("m", 1)])
        P.op("v", lambda e: e.tensor_scalar(eq1.ap(), lg.ap(), m.t[:, 0:1], None, ALU.is_equal),
             r=["lg", ("m", 1)], w=["eq1"])
        P.op("v", lambda e: e.scalar_tensor_tensor(t2b.ap(), eq1.ap(), 1e30, lg.ap(),
                                                   ALU.mult, ALU.subtract),
             r=["eq1", "lg"], w=["t2b"])
        P.op("v", lambda e: e.tensor_reduce(m.t[:, 1:2], t2b.ap(), AX, ALU.min, negate=True),
             r=["t2b"], w=[("m", 2)])
        P.op("v", lambda e: e.tensor_tensor(m.t[:, 2:3], m.t[:, 1:2], m.t[:, 0:1], ALU.subtract),
             r=[("m", 1), ("m", 2)], w=[("m", 3)])
        P.op("s", lambda e: e.activation(m.t[:, 3:4], m.t[:, 2:3], AF.Sigmoid),
             r=[("m", 3)], w=[("m", "p1")])
        P.op("s", lambda e: e.activation(m.t[:, 4:5], m.t[:, 2:3], AF.Sigmoid, scale=-1.0),
             r=[("m", 3)], w=[("m", "p1b")])
        P.op("v", lambda e: e.tensor_scalar(m.t[:, 5:6], m.t[:, 4:5], -1.0, 1.0,
                                            ALU.mult, ALU.add),
             r=[("m", "p1b")], w=[("m", "p2")])
        P.op("v", lambda e: e.tensor_scalar(m.t[:, 6:7], lg.ap() if False else m.t[:, 1:2],
                                            0.0, None, ALU.add),
             r=[("m", 2)], w=[("m", "m2c")])
        P.op("v", lambda e: e.tensor_scalar(eq1.ap(), eq1.ap(), m.t[:, 4:5], None, ALU.mult),
             r=["eq1", ("m", "p1b")], w=["eq1"])
        P.op("v", lambda e: e.tensor_scalar(t2b.ap(), lg.ap(), m.t[:, 6:7], None, ALU.is_equal),
             r=["lg", ("m", "m2c")], w=["t2b"])
        P.op("v", lambda e, i=i: e.scalar_tensor_tensor(
            c_store.t[:, i * 8:(i + 1) * 8], t2b.ap(), m.t[:, 5:6], eq1.ap(),
            ALU.mult, ALU.add),
            r=["t2b", ("m", "p2"), "eq1"], w=[("c_store", i)])
    P.barrier()
    m.free(); t2b.free(); eq1.free(); lg.free(); g_ps.free()


def emit_moe(P, nc, bufs, w1_dram, w2_dram, c_store):
    xnT = bufs["xnT"]
    NE = 8
    GROUPS = [(0, 3), (3, 3), (6, 3), (9, 3), (12, 3), (15, 1)]
    w2_sb = Buf(nc, "w2_sb", [128, 32, 1024], BF16)
    w1_sb = Buf(nc, "w1_sb", [128, 4, DC, 128], BF16)
    hT_sb = Buf(nc, "hT_sb", [128, 2, 384], BF16)
    out_ps_l = [Buf(nc, "moe_out_ps%d" % j, [128, 1024], F32, psum=True) for j in range(3)]
    hT_ps_l = [Buf(nc, "hT_ps%d" % s, [128, 512], F32, psum=True) for s in range(2)]
    for e_i in range(NE):
        for q in range(4):
            P.dma("g", lambda e, e_i=e_i, q=q: e.dma_start(
                w2_sb.t[:, q * 8:(q + 1) * 8, :],
                w2_dram.rearrange("(ee n p) c -> ee p n c", ee=8, p=128)[e_i, :, q * 8:(q + 1) * 8, :]),
                w=[("w2_sb", q)])
        for gi, (g0, glen) in enumerate(GROUPS):
            def emit_out(fc, glen):
                sl = fc % 2
                P.mm([(lambda e, j=j, sl=sl, fc=fc, c0=c0: e.matmul(
                    out_ps_l[j].t[:, c0:c0 + 512],
                    hT_sb.t[:, sl, j * 128:(j + 1) * 128],
                    w2_sb.t[:, fc, c0:c0 + 512],
                    start=(fc == 0), stop=(fc == 31), skip_group_check=True))
                    for j in range(glen) for c0 in (0, 512)],
                    r=[("hT_sb", sl)] + [("w2_sb", q) for q in range(4)],
                    w=["moe_out_ps%d" % j for j in range(glen)])
            for fc in range(32):
                slw = fc % 4
                sl = fc % 2
                P.dma("g", lambda e, e_i=e_i, fc=fc, slw=slw: e.dma_start(
                    w1_sb.t[:, slw, :, :],
                    w1_dram.rearrange("(ee a p) f -> ee p a f", ee=8, p=128)[e_i, :, :, fc * 128:(fc + 1) * 128]),
                    w=[("w1_sb", slw)])
                P.mm([(lambda e, a=a, sl=sl, slw=slw, g0=g0, glen=glen: e.matmul(
                    hT_ps_l[sl].t[:, 0:glen * 128],
                    w1_sb.t[:, slw, a, :],
                    xnT.t[:, a, g0 * 128:(g0 + glen) * 128],
                    start=(a == 0), stop=(a == DC - 1)))
                    for a in range(DC)],
                    r=[("w1_sb", slw)] + [("xnT", a, g0 + j) for a in range(DC) for j in range(glen)],
                    w=["hT_ps%d" % sl])
                P.op("s", lambda e, sl=sl, glen=glen: e.activation(
                    hT_sb.t[:, sl, 0:glen * 128], hT_ps_l[sl].t[:, 0:glen * 128], AF.Relu),
                    r=["hT_ps%d" % sl], w=[("hT_sb", sl)])
                if fc > 0:
                    emit_out(fc - 1, glen)
            emit_out(31, glen)
            for j in range(glen):
                i = g0 + j
                P.op("v", lambda e, i=i, j=j, e_i=e_i: e.scalar_tensor_tensor(
                    bufs["x_res"].t[:, i, :],
                    out_ps_l[j].ap(),
                    c_store.t[:, i * 8 + e_i:i * 8 + e_i + 1],
                    bufs["x_res"].t[:, i, :],
                    ALU.mult, ALU.add),
                    r=["moe_out_ps%d" % j, ("c_store", i), ("x_res", i)], w=[("x_res", i)])
    P.barrier()
    for b_ in reversed(hT_ps_l):
        b_.free()
    for b_ in reversed(out_ps_l):
        b_.free()
    hT_sb.free(); w1_sb.free(); w2_sb.free()


def emit_softplus_col(P, nc, bufs, out_ap, okey, in_ap, ikey, rows, prefix):
    sp1 = bufs["sp_scratch"]
    P.op("s", lambda e: e.activation(sp1.t[0:rows, 0:1], in_ap, AF.Abs),
         r=[ikey], w=[("sp_sc", 1)])
    P.op("s", lambda e: e.activation(sp1.t[0:rows, 1:2], sp1.t[0:rows, 0:1], AF.Exp, scale=-1.0),
         r=[("sp_sc", 1)], w=[("sp_sc", 2)])
    P.op("s", lambda e: e.activation(sp1.t[0:rows, 2:3], sp1.t[0:rows, 1:2], AF.Ln,
                                     bias=bufs["cst"].t[0:rows, 1:2]),
         r=[("sp_sc", 2), "cst"], w=[("sp_sc", 3)])
    P.op("v", lambda e: e.tensor_scalar(sp1.t[0:rows, 3:4], in_ap, 0.0, None, ALU.max),
         r=[ikey], w=[("sp_sc", 4)])
    P.op("v", lambda e: e.tensor_tensor(out_ap, sp1.t[0:rows, 3:4], sp1.t[0:rows, 2:3], ALU.add),
         r=[("sp_sc", 3), ("sp_sc", 4)], w=[okey])


def emit_l1_preamble(P, nc, bufs, want, dram):
    x_res = bufs["x_res"]
    ln_ps = Buf(nc, "ln_ps", [128, 1024], F32, psum=True)
    z_ps = Buf(nc, "z_ps", [64, 1024], F32, psum=True)
    bufs["sp_scratch"] = Buf(nc, "sp_scratch", [64, 4], F32)
    tau = Buf(nc, "tau", [64, 4], F32)
    bufs["_tau"] = tau
    P.dma("g", lambda e: e.dma_start(tau.t[0:64, 0:1], dram["log_tau"].unsqueeze(1)),
         w=["ltau"])
    ldt = Buf(nc, "ldt", [1, 2], F32)
    bufs["_ldt"] = ldt
    P.dma("g", lambda e: e.dma_start(ldt.t[0:1, 0:1], dram["log_dt"].unsqueeze(1)),
         w=["ldt_in"])
    emit_softplus_col(P, nc, bufs, tau.t[0:64, 1:2], "tau_sp", tau.t[0:64, 0:1], "ltau", 64, "sp_t")
    emit_softplus_col(P, nc, bufs, ldt.t[0:1, 1:2], "dt_sp", ldt.t[0:1, 0:1], "ldt_in", 1, "sp_d")
    P.mm([lambda e: e.matmul(z_ps.t[0:64, 0:1], bufs["ones"].t[0:1, 0:64], ldt.t[0:1, 1:2],
                             start=True, stop=True)],
         r=["dt_sp", "ones"], w=["z_ps"])
    P.op("v", lambda e: e.tensor_tensor(tau.t[0:64, 2:3], tau.t[0:64, 1:2], z_ps.t[0:64, 0:1],
                                        ALU.mult),
         r=["tau_sp", "z_ps"], w=["a_col"])
    P.op("v", lambda e: e.tensor_scalar(tau.t[0:64, 3:4], tau.t[0:64, 2:3], -1.0, None, ALU.mult),
         r=["a_col"], w=["na_col"])
    E = Buf(nc, "E_small", [64, 256], F32)
    bufs["E_small"] = E
    P.op("g", lambda e: e.iota(E.t[0:64, :], [[1, 256]], base=0, channel_multiplier=0,
                               allow_small_or_imprecise_dtypes=True), w=["E_iota"])
    P.op("v", lambda e: e.tensor_scalar(E.t[0:64, :], E.t[0:64, :], tau.t[0:64, 3:4], None,
                                        ALU.mult),
         r=["E_iota", "na_col"], w=["E_lin"])
    P.op("s", lambda e: e.activation(E.t[0:64, :], E.t[0:64, :], AF.Exp),
         r=["E_lin"], w=["E_small"])
    Eb = Buf(nc, "E_small_b", [64, 256], BF16)
    bufs["E_small_b"] = Eb
    P.op("v", lambda e: e.tensor_copy(Eb.t[0:64, :], E.t[0:64, :]),
         r=["E_small"], w=["E_small_b"])
    bufs["na_col_ap"] = tau.t[0:64, 3:4]
    erev = Buf(nc, "erev", [64, 128], BF16)
    bufs["_erev"] = erev
    wsn = Buf(nc, "wsn", [128, 64], BF16)
    bufs["wsn"] = wsn
    P.op("g", lambda e: e.iota(erev.t[0:64, :], [[0, 4], [-1, 32]], base=31,
                               channel_multiplier=0, allow_small_or_imprecise_dtypes=True),
         w=["erev"])
    P.op("v", lambda e: e.tensor_scalar(erev.t[0:64, :], erev.t[0:64, :], tau.t[0:64, 3:4],
                                        None, ALU.mult),
         r=["erev", "na_col"], w=["erev"])
    P.op("s", lambda e: e.activation(erev.t[0:64, :], erev.t[0:64, :], AF.Exp),
         r=["erev"], w=["erev"])
    ps_trb = Buf(nc, "ps_trb", [128, 1024], BF16, psum=True)
    P.mm([lambda e: e.transpose(ps_trb.t[:, 0:64], erev.t[0:64, :], bufs["id_b"].t[0:64, 0:64])],
         r=["erev", "id_b"], w=["ps_trb"])
    P.op("v", lambda e: e.tensor_copy(wsn.ap(), ps_trb.t[:, 0:64]),
         r=["ps_trb"], w=["wsn"])
    P.barrier()
    ps_trb.free()
    lam = Buf(nc, "lam", [64, 2], F32)
    bufs["lam"] = lam
    P.op("v", lambda e: e.tensor_copy(lam.t[0:64, 0:1], E.t[0:64, 32:33]),
         r=["E_small"], w=["lam32"])
    n1w = emit_row_broadcast(P, nc, bufs, "n1w_rep", dram["norm1_w"].unsqueeze(0), 1024, ln_ps)
    n1b = emit_row_broadcast(P, nc, bufs, "n1b_rep", dram["norm1_b"].unsqueeze(0), 1024, ln_ps)
    rrep = emit_row_broadcast(P, nc, bufs, "r_rep", dram["routing"].unsqueeze(0), 1024, ln_ps)
    srun = Buf(nc, "srun", [64, 1024], F32)
    if want.get("snaps"):
        bufs["psnap_sb"] = Buf(nc, "psnap_sb", [64, 8, 1024], BF16)
    if want.get("store"):
        bufs["hn0b"] = Buf(nc, "hn0b", [128, 1024], BF16)
        bufs["hn0stg"] = Buf(nc, "hn0stg", [128, DC, 128], BF16)
    mu = Buf(nc, "p_mu", [128, 6], F32)
    xn1f = Buf(nc, "xn1f", [128, 1024], F32)
    uf = Buf(nc, "uf", [128, 1024], F32)
    ub = Buf(nc, "ub", [128, 1024], BF16)
    s_sb = bufs.get("s_sb")
    P.op("g", lambda e: e.memset(srun.t[0:64, :], 0.0), w=["srun"])
    store = want.get("store", False)
    for i in range(NT):
        P.op("v", lambda e, i=i: e.reduce_sum(mu.t[:, 0:1], x_res.t[:, i, :], axis=AX),
             r=[("x_res", i)], w=[("pmu", 1)])
        P.op("v", lambda e: e.tensor_scalar(mu.t[:, 1:2], mu.t[:, 0:1], 1.0 / D, None, ALU.mult),
             r=[("pmu", 1)], w=[("pmu", 2)])
        P.op("v", lambda e, i=i: e.tensor_scalar(xn1f.ap(), x_res.t[:, i, :], mu.t[:, 1:2],
                                                 None, ALU.subtract),
             r=[("x_res", i), ("pmu", 2)], w=["xn1f"])
        P.op("s", lambda e: e.activation(uf.ap(), xn1f.ap(), AF.Square, accum_out=mu.t[:, 2:3]),
             r=["xn1f"], w=[("pmu", 3), "uf"])
        P.op("s", lambda e: e.activation(mu.t[:, 3:4], mu.t[:, 2:3], AF.Ln,
                                         bias=bufs["cst"].t[:, 0:1], scale=1.0 / D),
             r=[("pmu", 3), "cst"], w=[("pmu", 4)])
        P.op("s", lambda e: e.activation(mu.t[:, 3:4], mu.t[:, 3:4], AF.Exp, scale=-0.5),
             r=[("pmu", 4)], w=[("pmu", 5)])
        P.op("v", lambda e: e.tensor_scalar(xn1f.ap(), xn1f.ap(), mu.t[:, 3:4], None, ALU.mult),
             r=["xn1f", ("pmu", 5)], w=["xn1f"])
        if store:
            P.op("s", lambda e, i=i: e.copy(bufs["xm"].t[:, i, :], xn1f.ap()),
                 r=["xn1f"], w=[("xm_init", i)])
            P.op("v", lambda e: e.tensor_tensor(mu.t[:, 4:5], mu.t[:, 3:4], mu.t[:, 3:4], ALU.mult),
                 r=[("pmu", 5)], w=[("pmu", 6)])
            P.op("v", lambda e: e.tensor_tensor(mu.t[:, 4:5], mu.t[:, 4:5], mu.t[:, 2:3], ALU.mult),
                 r=[("pmu", 6), ("pmu", 3)], w=[("pmu", 7)])
            P.op("s", lambda e: e.activation(mu.t[:, 5:6], mu.t[:, 4:5], AF.Ln),
                 r=[("pmu", 7)], w=[("pmu", 8)])
            P.op("s", lambda e: e.activation(mu.t[:, 5:6], mu.t[:, 5:6], AF.Exp, scale=0.5),
                 r=[("pmu", 8)], w=[("pmu", 9)])
            P.op("v", lambda e: e.tensor_scalar(mu.t[:, 5:6], mu.t[:, 5:6], 1e-12, None, ALU.max),
                 r=[("pmu", 9)], w=[("pmu", 10)])
            P.op("v", lambda e: e.reciprocal(mu.t[:, 4:5], mu.t[:, 5:6]),
                 r=[("pmu", 10)], w=[("pmu", 11)])
            P.op("v", lambda e, i=i: e.tensor_scalar(bufs["st_sb"].t[:, i:i + 1], mu.t[:, 4:5],
                                                     32.0, None, ALU.mult),
                 r=[("pmu", 11)], w=[("st_sb", i)])
            P.op("v", lambda e, i=i: e.tensor_scalar(bufs["hn0b"].ap(), xn1f.ap(),
                                                     bufs["st_sb"].t[:, i:i + 1], None, ALU.mult),
                 r=["xn1f", ("st_sb", i)], w=["hn0b"])
            for a in range(DC):
                P.mm([lambda e, a=a: e.transpose(bufs["ps_trb3"].t[:, 0:128],
                                                 bufs["hn0b"].t[:, a * 128:(a + 1) * 128],
                                                 bufs["id_b"].ap())],
                     r=["hn0b", "id_b"], w=["ps_trb3"])
                P.op("v", lambda e, a=a: e.tensor_copy(bufs["hn0stg"].t[:, a, :],
                                                       bufs["ps_trb3"].t[:, 0:128]),
                     r=["ps_trb3"], w=[("hn0stg", a)])
            P.dma("y", lambda e, i=i: e.dma_start(
                dram["hn0T"].rearrange("(a p) t -> p a t", p=128)[:, :, i * 128:(i + 1) * 128],
                bufs["hn0stg"].ap()),
                r=[("hn0stg", a) for a in range(DC)], w=[("hn0T_out", i)])
        P.op("v", lambda e: e.tensor_tensor(uf.ap(), xn1f.ap(), rrep.ap(), ALU.mult),
             r=["xn1f", "r_rep"], w=["uf"])
        P.op("v", lambda e, i=i: e.reduce_sum(s_sb.t[:, i:i + 1], uf.ap(), axis=AX),
             r=["uf"], w=[("s_sb", i)])
        P.op("v", lambda e, i=i: e.reduce_sum(mu.t[:, 0:1], xn1f.ap(), axis=AX),
             r=["xn1f"], w=[("pmu", 1)])
        P.op("v", lambda e: e.tensor_scalar(mu.t[:, 1:2], mu.t[:, 0:1], 1.0 / D, None, ALU.mult),
             r=[("pmu", 1)], w=[("pmu", 2)])
        P.op("v", lambda e: e.tensor_scalar(uf.ap(), xn1f.ap(), mu.t[:, 1:2], None, ALU.subtract),
             r=["xn1f", ("pmu", 2)], w=["uf"])
        P.op("s", lambda e: e.activation(ub.ap(), uf.ap(), AF.Square, accum_out=mu.t[:, 2:3]),
             r=["uf"], w=[("pmu", 3), ("ub", "scratch")])
        P.op("s", lambda e: e.activation(mu.t[:, 3:4], mu.t[:, 2:3], AF.Ln,
                                         bias=bufs["cst"].t[:, 0:1], scale=1.0 / D),
             r=[("pmu", 3), "cst"], w=[("pmu", 4)])
        P.op("s", lambda e: e.activation(mu.t[:, 3:4], mu.t[:, 3:4], AF.Exp, scale=-0.5),
             r=[("pmu", 4)], w=[("pmu", 5)])
        P.op("v", lambda e: e.tensor_scalar(uf.ap(), uf.ap(), mu.t[:, 3:4], None, ALU.mult),
             r=["uf", ("pmu", 5)], w=["uf"])
        P.op("v", lambda e: e.tensor_tensor(uf.ap(), uf.ap(), n1w.ap(), ALU.mult),
             r=["uf", "n1w_rep"], w=["uf"])
        P.op("v", lambda e: e.tensor_tensor(ub.ap(), uf.ap(), n1b.ap(), ALU.add),
             r=["uf", "n1b_rep"], w=[("ub", "data")])
        if store:
            P.op("s", lambda e, i=i: e.copy(bufs["u_bf"].t[:, i, :], ub.ap()),
                 r=[("ub", "data")], w=[("u_bf", i)])

        for j in range(NSUB):
            P.mm([(lambda e, j=j, c0=c0: e.matmul(
                z_ps.t[0:64, c0:c0 + 512],
                bufs["wsn"].t[j * 32:(j + 1) * 32, :],
                ub.t[j * 32:(j + 1) * 32, c0:c0 + 512],
                start=True, stop=True, tile_position=(j * 32, 0)))
                for c0 in (0, 512)],
                r=[("ub", "data"), "wsn"], w=["z_ps"])
            P.op("v", lambda e: e.scalar_tensor_tensor(
                srun.t[0:64, :], srun.t[0:64, :], lam.t[0:64, 0:1], z_ps.t[0:64, :],
                ALU.mult, ALU.add),
                r=["z_ps", "srun", "lam32"], w=["srun"])
            if want.get("snaps"):
                ss = i * NSUB + j
                P.op("s", lambda e, ss=ss: e.copy(
                    bufs["psnap_sb"].t[0:64, ss % 8, :], srun.t[0:64, :]),
                    r=["srun"], w=[("psnap_sb", ss % 8)])
                if ss % 8 == 7:
                    blk = ss // 8
                    P.dma("y", lambda e, blk=blk: e.dma_start(
                        dram["psnap"][0:64, blk * 8192:(blk + 1) * 8192],
                        bufs["psnap_sb"].t[0:64, :, :].rearrange("p k c -> p (k c)")),
                        r=[("psnap_sb", q) for q in range(8)], w=[("psnap_out", blk)])
    bufs["srun"] = srun
    P.barrier()
    ub.free(); uf.free(); xn1f.free(); mu.free()
    if want.get("free_tmp"):
        if want.get("store"):
            bufs["hn0stg"].free(); bufs["hn0b"].free()
        if want.get("snaps"):
            bufs["psnap_sb"].free()
        srun.free()
        rrep.free(); n1b.free(); n1w.free()
    z_ps.free(); ln_ps.free()



def build_B():
    nc = bass.Bass("TRN2", target_bir_lowering=False)
    x = nc.dram_tensor("x", [TPC, D], F32, kind="ExternalInput").ap()
    wq = nc.dram_tensor("Wq", [D, D], F32, kind="ExternalInput").ap()
    wk = nc.dram_tensor("Wk", [D, D], F32, kind="ExternalInput").ap()
    wv = nc.dram_tensor("Wv", [D, D], F32, kind="ExternalInput").ap()
    wo = nc.dram_tensor("Wo", [D, D], F32, kind="ExternalInput").ap()
    gate_w = nc.dram_tensor("gate_w", [D, 8], F32, kind="ExternalInput").ap()
    w1 = nc.dram_tensor("w1", [8 * D, 4096], F32, kind="ExternalInput").ap()
    w2 = nc.dram_tensor("w2", [8 * 4096, D], F32, kind="ExternalInput").ap()
    stot_all = nc.dram_tensor("stot_all", [8 * 64, H * 65], F32, kind="ExternalInput").ap()
    pred_mask = nc.dram_tensor("pred_mask", [1, 8], F32, kind="ExternalInput").ap()
    log_tau = nc.dram_tensor("log_tau", [NST], F32, kind="ExternalInput").ap()
    log_dt = nc.dram_tensor("log_dt", [1], F32, kind="ExternalInput").ap()
    norm1_w = nc.dram_tensor("norm1_w", [D], F32, kind="ExternalInput").ap()
    norm1_b = nc.dram_tensor("norm1_b", [D], F32, kind="ExternalInput").ap()
    routing = nc.dram_tensor("routing", [D], F32, kind="ExternalInput").ap()
    x1_out = nc.dram_tensor("x1_out", [TPC, D], F32, kind="ExternalOutput").ap()
    zpack = nc.dram_tensor("zpack", [66, 1024], F32, kind="ExternalOutput").ap()
    dram = {"log_tau": log_tau, "log_dt": log_dt, "norm1_w": norm1_w,
            "norm1_b": norm1_b, "routing": routing}

    P = Prog(nc)
    bufs = {}
    emit_const_tiles(P, nc, bufs)
    bufs["x_res"] = x_res = Buf(nc, "x_res", [128, NT, D], F32)
    bufs["xnT"] = xnT = Buf(nc, "xnT", [128, DC, TPC], BF16)
    bufs["s_sb"] = Buf(nc, "s_sb", [128, NT], F32)
    c_store = Buf(nc, "c_store", [128, NT * 8], F32)
    bufs["ps_tr"] = ps_tr = Buf(nc, "ps_tr", [128, 1024], BF16, psum=True)
    for qq in range(4):
        P.dma("y", lambda e, qq=qq: e.dma_start(
            x_res.t[:, qq * 4:(qq + 1) * 4, :],
            x.rearrange("(n p) d -> p n d", p=128)[:, qq * 4:(qq + 1) * 4, :]),
            w=[("x_res", i) for i in range(qq * 4, qq * 4 + 4)])
    bufs["gate_sb"] = gate_sb = Buf(nc, "gate_sb", [128, DC, 8], BF16)
    P.dma("g", lambda e: e.dma_start(gate_sb.ap(),
                                     gate_w.rearrange("(a p) e -> p a e", p=128)),
          w=["gate_sb"])
    load_w_transposed(P, nc, bufs, "WkT", wk, D, D, ps_tr)
    load_w_transposed(P, nc, bufs, "WvT", wv, D, D, ps_tr)
    emit_ln_xnT(P, nc, bufs, "x_res", x_res, xnT, None, ps_tr, "ln0")
    snaps = Buf(nc, "snaps", [128, NT // 2, H * 65], BF16)
    emit_l0_kv_pass(P, nc, bufs, snaps, None)
    bufs["Jfull"] = Jf = Buf(nc, "Jfull", [128, 128], BF16)
    for pb in (0, 64):
        for fb in (0, 64):
            P.op("v", lambda e, pb=pb, fb=fb: e.tensor_copy(
                Jf.t[pb:pb + 64, fb:fb + 64], bufs["id_b"].t[pb:pb + 64, pb:pb + 64]),
                r=["id_b"], w=["Jfull"])
    offdup = Buf(nc, "offdup", [128, H * 65], F32)
    stin = Buf(nc, "stin", [64, 8, H * 65], F32)
    msk = Buf(nc, "msk", [64, 8], F32)
    mrow = Buf(nc, "mrow", [1, 8], F32)
    offb = Buf(nc, "offb", [64, H * 65], F32)
    offc = Buf(nc, "offc", [64, H * 65], F32)
    ps_off = Buf(nc, "ps_off", [64, 512], F32, psum=True)
    ps_dup = Buf(nc, "ps_dup", [128, 1536], F32, psum=True)
    for r in range(8):
        P.dma("y", lambda e, r=r: e.dma_start(stin.t[0:64, r, :],
                                              stot_all[r * 64:(r + 1) * 64, :]),
              w=[("stin", r)])
    P.dma("g", lambda e: e.dma_start(mrow.ap(), pred_mask), w=["mrow"])
    P.mm([lambda e: e.matmul(ps_off.t[0:64, 0:8], bufs["ones"].t[0:1, 0:64], mrow.ap(),
                             start=True, stop=True)],
         r=["mrow", "ones"], w=["ps_off"])
    P.op("v", lambda e: e.tensor_copy(msk.t[0:64, :], ps_off.t[0:64, 0:8]),
         r=["ps_off"], w=["msk"])
    P.op("g", lambda e: e.memset(offb.t[0:64, :], 0.0), w=[("off", 0)])
    for r in range(8):
        src_b, dst_b = (offb, offc) if r % 2 == 0 else (offc, offb)
        P.op("v", lambda e, r=r, src_b=src_b, dst_b=dst_b: e.scalar_tensor_tensor(
            dst_b.t[0:64, :],
            stin.t[0:64, r, :],
            msk.t[0:64, r:r + 1],
            src_b.t[0:64, :],
            ALU.mult, ALU.add),
            r=[("stin", r), "msk", ("off", r)], w=[("off", r + 1)])
    offcb = offb
    ob16 = Buf(nc, "ob16", [64, H * 65], BF16)
    P.op("v", lambda e: e.tensor_copy(ob16.t[0:64, :], offc.t[0:64, :]),
         r=[("off", 8)], w=["ob16"])
    P.mm([(lambda e, c0=c0, cw=cw: e.matmul(ps_dup.t[:, c0:c0 + cw], Jf.t[0:64, :],
                                            ob16.t[0:64, c0:c0 + cw], start=True, stop=True))
          for c0, cw in ((0, 512), (512, 512), (1024, 16))],
         r=["ob16", "Jfull"], w=["ps_dup"])
    P.op("v", lambda e: e.tensor_copy(offdup.ap(), ps_dup.t[:, 0:1040]),
         r=["ps_dup"], w=["offdup"])
    P.barrier()
    ps_dup.free(); ps_off.free(); ob16.free(); offc.free(); offb.free(); mrow.free(); msk.free(); stin.free()
    load_w_transposed(P, nc, bufs, "WqT", wq, D, D, ps_tr)
    load_w_transposed(P, nc, bufs, "WoT", wo, D, D, ps_tr)
    emit_l0_attn_main(P, nc, bufs, snaps, offdup)
    P.barrier()
    bufs["WoT"].free(); bufs["WqT"].free()
    offdup.free(); bufs["Jfull"].free(); snaps.free()
    bufs["WvT"].free(); bufs["WkT"].free()
    emit_gating(P, nc, bufs, c_store)
    ps_tr.free()
    emit_moe(P, nc, bufs, w1, w2, c_store)
    emit_l1_preamble(P, nc, bufs, {"snaps": False, "store": False},
                     dict(dram, psnap=None))
    for qq in range(4):
        P.dma("y", lambda e, qq=qq: e.dma_start(
            x1_out.rearrange("(n p) d -> p n d", p=128)[:, qq * 4:(qq + 1) * 4, :],
            x_res.t[:, qq * 4:(qq + 1) * 4, :]),
            r=[("x_res", i) for i in range(qq * 4, qq * 4 + 4)], w=[("x1o", qq)])
    P.dma("y", lambda e: e.dma_start(zpack[0:64, :], bufs["srun"].t[0:64, :]),
          r=["srun"], w=["zp1"])
    P.dma("y", lambda e: e.dma_start(
        zpack[64:66, :].rearrange("r c -> (r c)").rearrange("(p f) -> p f", p=128),
        bufs["s_sb"].ap()),
        r=[("s_sb", i) for i in range(NT)], w=["zp2"])
    P.barrier()
    P.finalize()
    P.close()
    return nc




def emit_stream_transpose_w(P, nc, bufs, key, dram_in, scratch, R, C, ps_b, unused=None):
    cb = C // 128
    stg = Buf(nc, key + "_stg", [128, cb, 128], BF16)
    nat = Buf(nc, key + "_nat", [128, cb, 128], BF16)
    for rb in range(R // 128):
        P.dma("g", lambda e, rb=rb: e.dma_start(
            nat.ap(),
            dram_in.rearrange("(a p) f -> p a f", p=128)[:, rb, :].rearrange(
                "p (cb q) -> p cb q", q=128)),
            w=[(key, "nat")])
        for a in range(cb):
            P.mm([lambda e, a=a: e.transpose(ps_b.t[:, 0:128], nat.t[:, a, :], bufs["id_b"].ap())],
                 r=[(key, "nat"), "id_b"], w=[ps_b.name])
            P.op("v", lambda e, a=a: e.tensor_copy(stg.t[:, a, :], ps_b.t[:, 0:128]),
                 r=[ps_b.name], w=[(key, "stg")])
        P.dma("y", lambda e, rb=rb: e.dma_start(
            scratch.rearrange("(cb p) r -> p cb r", p=128)[:, :, rb * 128:(rb + 1) * 128],
            stg.ap()),
            r=[(key, "stg")], w=[(key, "out", rb)])
    P.barrier()
    nat.free(); stg.free()


def emit_l1_rest(P, nc, bufs, dram):
    x_res = bufs["x_res"]
    xm = bufs["xm"]
    ps_trb = Buf(nc, "ps_trb2", [128, 1024], BF16, psum=True)
    mixT = Buf(nc, "mixT", [64, 1024], BF16)
    mixn = Buf(nc, "mixn", [128, 8, 64], BF16)
    P.dma("g", lambda e: e.dma_start(mixn.ap(),
                                     dram["mix"].rearrange("(a p) n -> p a n", p=128)),
          w=["mixn"])
    for a in range(8):
        P.mm([lambda e, a=a: e.transpose(ps_trb.t[0:64, 0:128], mixn.t[:, a, :],
                                         bufs["id_b"].t[0:128, 0:128])],
             r=["mixn", "id_b"], w=["ps_trb2"])
        P.op("v", lambda e, a=a: e.tensor_copy(mixT.t[0:64, a * 128:(a + 1) * 128],
                                               ps_trb.t[0:64, 0:128]),
             r=["ps_trb2"], w=["mixT"])
    P.barrier()
    mixn.free()
    lampow = Buf(nc, "lampow", [64, 64], F32)
    P.op("g", lambda e: e.iota(lampow.t[0:64, :], [[32, 64]], base=32, channel_multiplier=0,
                               allow_small_or_imprecise_dtypes=True), w=["lampow"])
    P.op("v", lambda e: e.tensor_scalar(lampow.t[0:64, :], lampow.t[0:64, :],
                                        bufs["na_col_ap"], None, ALU.mult),
         r=["lampow", "na_col"], w=["lampow"])
    P.op("s", lambda e: e.activation(lampow.t[0:64, :], lampow.t[0:64, :], AF.Exp),
         r=["lampow"], w=["lampow"])
    dcol = Buf(nc, "dcol", [128, 8], F32)
    P.dma("y", lambda e: e.dma_start(dcol.ap(), dram["Dvec"].rearrange("(a p) -> p a", p=128), allow_slow_non_contiguous=True),
          w=["dcol"])
    ps_it = Buf(nc, "ps_it", [128, 512], F32, psum=True)
    ps_trf = Buf(nc, "ps_trf", [128, 512], F32, psum=True)
    uT = Buf(nc, "uT", [128, 4, TPC], BF16)
    yT = Buf(nc, "yT2", [128, 4, TPC], F32)
    K_sb = Buf(nc, "K_sb", [128, 4, 32], F32)
    dk0 = Buf(nc, "dk0", [128, 4], F32)
    psn = Buf(nc, "psn", [64, 8, 512], BF16)
    pcor = Buf(nc, "pcor", [64, 512], BF16)
    pb_ = Buf(nc, "pb", [64, 512], BF16)
    for hl in range(2):
        for i in range(NT):
            for cl in range(4):
                cc = hl * 4 + cl
                P.mm([lambda e, i=i, cc=cc: e.transpose(
                    ps_trb.t[:, 0:128], bufs["u_bf"].t[:, i, cc * 128:(cc + 1) * 128],
                    bufs["id_b"].ap())],
                    r=[("u_bf", i), "id_b"], w=["ps_trb2"])
                P.op("v", lambda e, i=i, cl=cl: e.tensor_copy(
                    uT.t[:, cl, i * 128:(i + 1) * 128], ps_trb.t[:, 0:128]),
                    r=["ps_trb2"], w=[("uT", cl)])
        for cl in range(4):
            cc = hl * 4 + cl
            P.mm([lambda e, cc=cc: e.matmul(ps_it.t[:, 0:32], mixT.t[0:64, cc * 128:(cc + 1) * 128],
                                            bufs["E_small_b"].t[0:64, 0:32],
                                            start=True, stop=True)],
                 r=["mixT", "E_small_b"], w=["ps_it"])
            P.op("v", lambda e, cl=cl: e.tensor_copy(K_sb.t[:, cl, :], ps_it.t[:, 0:32]),
                 r=["ps_it"], w=[("K_sb", cl)])
            P.op("v", lambda e, cl=cl, cc=cc: e.tensor_tensor(
                dk0.t[:, cl:cl + 1], dcol.t[:, cc:cc + 1], K_sb.t[:, cl, 0:1], ALU.add),
                r=[("K_sb", cl), "dcol"], w=[("dk0", cl)])
        for cl in range(4):
            P.op("v", lambda e, cl=cl: e.tensor_scalar(
                yT.t[:, cl, :], uT.t[:, cl, :], dk0.t[:, cl:cl + 1], None, ALU.mult),
                r=[("uT", cl), ("dk0", cl)], w=[("yT", cl)])
        for d in range(1, LSUB):
            for cl in range(4):
                P.op("v", lambda e, d=d, cl=cl: e.scalar_tensor_tensor(
                    yT.t[:, cl, :].rearrange("p (s q) -> p s q", q=LSUB)[:, :, d:LSUB],
                    uT.t[:, cl, :].rearrange("p (s q) -> p s q", q=LSUB)[:, :, 0:LSUB - d],
                    K_sb.t[:, cl, d:d + 1],
                    yT.t[:, cl, :].rearrange("p (s q) -> p s q", q=LSUB)[:, :, d:LSUB],
                    ALU.mult, ALU.add),
                    r=[("uT", cl), ("K_sb", cl), ("yT", cl)], w=[("yT", cl)])
        for i in range(NT):
            if i % 2 == 0:
                blk = i // 2
                P.dma("y", lambda e, blk=blk, hl=hl: e.dma_start(
                    psn.ap(),
                    dram["psnap"].rearrange("p (ss c) -> p ss c", c=1024)[0:64, blk * 8:(blk + 1) * 8, hl * 512:(hl + 1) * 512]),
                    w=["psn"])
            for j in range(NSUB):
                ss = i * NSUB + j
                P.op("v", lambda e, ss=ss, hl=hl: e.scalar_tensor_tensor(
                    pcor.t[0:64, :],
                    bufs["S_in"].t[0:64, hl * 512:(hl + 1) * 512],
                    lampow.t[0:64, ss:ss + 1],
                    psn.t[0:64, ss % 8, :],
                    ALU.mult, ALU.add),
                    r=["S_in", "lampow", "psn"], w=["pcor"])
                P.op("v", lambda e, hl=hl: e.tensor_tensor(
                    pb_.t[0:64, :], pcor.t[0:64, :],
                    mixT.t[0:64, hl * 512:(hl + 1) * 512], ALU.mult),
                    r=["pcor", "mixT"], w=["pb"])
                P.mm([(lambda e, cl=cl, j=j: e.matmul(
                    ps_it.t[:, cl * 128 + j * 32:cl * 128 + j * 32 + 32],
                    pb_.t[0:64, cl * 128:(cl + 1) * 128],
                    bufs["E_small_b"].t[0:64, 1:33],
                    start=True, stop=True))
                    for cl in range(4)],
                    r=["pb", "E_small_b"], w=["ps_it"])
            for cl in range(4):
                P.op("v", lambda e, cl=cl, i=i: e.tensor_tensor(
                    yT.t[:, cl, i * 128:(i + 1) * 128].rearrange("p (s q) -> p s q", q=32),
                    yT.t[:, cl, i * 128:(i + 1) * 128].rearrange("p (s q) -> p s q", q=32),
                    ps_it.t[:, cl * 128:(cl + 1) * 128].rearrange("p (s q) -> p s q", q=32),
                    ALU.add),
                    r=["ps_it", ("yT", cl)], w=[("yT", cl)])
        for i in range(NT):
            for cl in range(4):
                cc = hl * 4 + cl
                P.mm([lambda e, i=i, cl=cl: e.transpose(
                    ps_trf.t[:, 0:128], yT.t[:, cl, i * 128:(i + 1) * 128], bufs["id_f"].ap())],
                    r=[("yT", cl), "id_f"], w=["ps_trf"])
                P.op("v", lambda e, i=i, cc=cc: e.tensor_tensor(
                    xm.t[:, i, cc * 128:(cc + 1) * 128], ps_trf.t[:, 0:128],
                    xm.t[:, i, cc * 128:(cc + 1) * 128], ALU.add),
                    r=["ps_trf", ("xm_init", i)], w=[("xm", i, cc)])
    P.barrier()
    if DBG_XM and "xm_o" in dram:
        for qq in range(4):
            P.dma("g", lambda e, qq=qq: e.dma_start(
                dram["xm_o"].rearrange("(n p) d -> p n d", p=128)[:, qq * 4:(qq + 1) * 4, :],
                xm.t[:, qq * 4:(qq + 1) * 4, :]),
                r=[("xm", i, cc) for i in range(qq * 4, qq * 4 + 4) for cc in range(8)],
                w=[("xmo", qq)])
    pb_.free(); pcor.free(); psn.free(); dk0.free(); K_sb.free(); yT.free(); uT.free()
    ps_trf.free(); ps_it.free()

    ps_rb = Buf(nc, "ps_rb", [128, 512], F32, psum=True)
    n2w = emit_row_broadcast(P, nc, bufs, "n2w_rep", dram["norm2_w"].unsqueeze(0), 1024, ps_rb)
    n2b = emit_row_broadcast(P, nc, bufs, "n2b_rep", dram["norm2_b"].unsqueeze(0), 1024, ps_rb)
    b2rep = emit_row_broadcast(P, nc, bufs, "b2_rep", dram["fc2_b"].unsqueeze(0), 1024, ps_rb)
    P.barrier()
    ps_rb.free()
    b1col = Buf(nc, "b1col", [128, 32], F32)
    P.dma("y", lambda e: e.dma_start(b1col.ap(), dram["fc1_b"].rearrange("(a p) -> p a", p=128), allow_slow_non_contiguous=True),
          w=["b1col"])
    mu = Buf(nc, "c_mu", [128, 6], F32)
    cen = Buf(nc, "c_cen", [128, 1024], F32)
    gb = Buf(nc, "c_gb", [128, 1024], BF16)
    gT = Buf(nc, "gT", [128, DC, 512], BF16)
    h1T = Buf(nc, "h1T", [128, 32, 512], BF16)
    w1c = Buf(nc, "w1c", [128, 2, DC, 128], BF16)
    w2c = Buf(nc, "w2c", [128, 2, 1024], BF16)
    ps_h1 = [Buf(nc, "ps_h1%d" % s, [128, 512], F32, psum=True) for s in range(2)]
    ps_o = [Buf(nc, "ps_mo%d" % s, [128, 1024], F32, psum=True) for s in range(2)]
    for grp in range(4):
        for tl in range(4):
            i = grp * 4 + tl
            P.op("v", lambda e, i=i: e.reduce_sum(mu.t[:, 0:1], xm.t[:, i, :], axis=AX),
                 r=[("xm", i, cc) for cc in range(8)], w=[("cmu", 1)])
            P.op("v", lambda e: e.tensor_scalar(mu.t[:, 1:2], mu.t[:, 0:1], 1.0 / D, None, ALU.mult),
                 r=[("cmu", 1)], w=[("cmu", 2)])
            P.op("v", lambda e, i=i: e.tensor_scalar(cen.ap(), xm.t[:, i, :], mu.t[:, 1:2],
                                                     None, ALU.subtract),
                 r=[("xm", i, cc) for cc in range(8)] + [("cmu", 2)], w=["c_cen"])
            P.op("s", lambda e: e.activation(gb.ap(), cen.ap(), AF.Square, accum_out=mu.t[:, 2:3]),
                 r=["c_cen"], w=[("cmu", 3), ("c_gb", "x")])
            P.op("s", lambda e: e.activation(mu.t[:, 3:4], mu.t[:, 2:3], AF.Ln,
                                             bias=bufs["cst"].t[:, 0:1], scale=1.0 / D),
                 r=[("cmu", 3), "cst"], w=[("cmu", 4)])
            P.op("s", lambda e: e.activation(mu.t[:, 3:4], mu.t[:, 3:4], AF.Exp, scale=-0.5),
                 r=[("cmu", 4)], w=[("cmu", 5)])
            P.op("v", lambda e: e.tensor_scalar(cen.ap(), cen.ap(), mu.t[:, 3:4], None, ALU.mult),
                 r=["c_cen", ("cmu", 5)], w=["c_cen"])
            P.op("v", lambda e: e.tensor_tensor(cen.ap(), cen.ap(), n2w.ap(), ALU.mult),
                 r=["c_cen", "n2w_rep"], w=["c_cen"])
            P.op("v", lambda e: e.tensor_tensor(gb.ap(), cen.ap(), n2b.ap(), ALU.add),
                 r=["c_cen", "n2b_rep"], w=[("c_gb", "x")])
            for a in range(DC):
                P.mm([lambda e, a=a: e.transpose(ps_trb.t[:, 0:128],
                                                 gb.t[:, a * 128:(a + 1) * 128], bufs["id_b"].ap())],
                     r=[("c_gb", "x"), "id_b"], w=["ps_trb2"])
                P.op("v", lambda e, a=a, tl=tl: e.tensor_copy(
                    gT.t[:, a, tl * 128:(tl + 1) * 128], ps_trb.t[:, 0:128]),
                    r=["ps_trb2"], w=[("gT", a)])
        for ft in range(32):
            sl = ft % 2
            P.dma("g", lambda e, ft=ft, sl=sl: e.dma_start(
                w1c.t[:, sl, :, :],
                dram["fc1T"].rearrange("(a p) f -> p a f", p=128)[:, :, ft * 128:(ft + 1) * 128]),
                w=[("w1c", sl)])
            P.mm([(lambda e, a=a, sl=sl: e.matmul(
                ps_h1[sl].t[:, 0:512],
                w1c.t[:, sl, a, :],
                gT.t[:, a, :],
                start=(a == 0), stop=(a == DC - 1)))
                for a in range(DC)],
                r=[("w1c", sl)] + [("gT", a) for a in range(DC)],
                w=["ps_h1%d" % sl])
            P.op("s", lambda e, ft=ft, sl=sl: e.activation(
                h1T.t[:, ft, :], ps_h1[sl].t[:, 0:512], AF.Gelu, bias=b1col.t[:, ft:ft + 1]),
                r=["ps_h1%d" % sl, "b1col"], w=[("h1T", ft)])
        for pair in range(2):
            for ft in range(32):
                sl = ft % 2
                P.dma("y", lambda e, ft=ft, sl=sl: e.dma_start(
                    w2c.t[:, sl, :],
                    dram["fc2T"][ft * 128:(ft + 1) * 128, :]),
                    w=[("w2c", sl)])
                P.mm([(lambda e, tt=tt, ft=ft, sl=sl, c0=c0, pair=pair: e.matmul(
                    ps_o[tt].t[:, c0:c0 + 512],
                    h1T.t[:, ft, (pair * 2 + tt) * 128:(pair * 2 + tt) * 128 + 128],
                    w2c.t[:, sl, c0:c0 + 512],
                    start=(ft == 0), stop=(ft == 31), skip_group_check=True))
                    for tt in range(2) for c0 in (0, 512)],
                    r=[("w2c", sl)] + [("h1T", f_) for f_ in range(32)],
                    w=["ps_mo0", "ps_mo1"])
            for tt in range(2):
                i = grp * 4 + pair * 2 + tt
                P.op("v", lambda e, i=i, tt=tt: e.tensor_tensor(
                    cen.ap(), ps_o[tt].ap(), b2rep.ap(), ALU.add),
                    r=["ps_mo%d" % tt, "b2_rep"], w=["c_cen"])
                P.op("v", lambda e, i=i: e.tensor_tensor(
                    cen.ap(), cen.ap(), xm.t[:, i, :], ALU.add),
                    r=["c_cen"] + [("xm", i, cc) for cc in range(8)], w=["c_cen"])
                P.op("v", lambda e, i=i: e.tensor_tensor(
                    x_res.t[:, i, :], x_res.t[:, i, :], cen.ap(), ALU.add),
                    r=["c_cen", ("x_res", i)], w=[("x_res", i)])
    P.barrier()
    for b_ in reversed(ps_o):
        b_.free()
    for b_ in reversed(ps_h1):
        b_.free()
    w2c.free(); w1c.free(); h1T.free(); gT.free(); gb.free(); cen.free(); mu.free()
    b1col.free(); bufs["b2_rep"].free(); bufs["n2b_rep"].free(); bufs["n2w_rep"].free()
    dcol.free(); lampow.free(); mixT.free()
    ps_trb.free()



def emit_routing_select(P, nc, bufs, s_full, ps_small, tok_off_dram):
    EPSR = 0.1
    CONST = float(EPSR * np.log(8.0))
    sc = Buf(nc, "rt_sc", [1, 24], F32)
    col = Buf(nc, "rt_col", [128, 8], F32)
    wk = Buf(nc, "rt_wk", [128, 64], F32)
    wk2 = Buf(nc, "rt_wk2", [128, 64], F32)
    sat = Buf(nc, "rt_sat", [128, 64], F32)
    idxf = Buf(nc, "rt_idx", [128, 64], F32)
    trp = Buf(nc, "rt_trp", [1, 128], F32)
    toff = Buf(nc, "rt_toff", [1, 2], F32)
    P.dma("g", lambda e: e.dma_start(toff.t[0:1, 0:1], tok_off_dram), w=["toff"])
    P.op("g", lambda e: e.iota(idxf.ap(), [[128, 64]], base=0, channel_multiplier=1,
                               allow_small_or_imprecise_dtypes=True), w=["rt_idx"])
    P.op("g", lambda e: e.memset(col.t[:, 0:1], 0.0), w=[("rt", "arep")])

    def bcast(src_ap, dst_sl, rkeys, wkey):
        P.mm([lambda e: e.matmul(ps_small.t[:, 0:1], bufs["ones"].t[0:1, :], src_ap,
                                 start=True, stop=True)],
             r=rkeys + ["ones"], w=[ps_small.name])
        P.op("v", lambda e: e.tensor_copy(dst_sl, ps_small.t[:, 0:1]),
             r=[ps_small.name], w=[wkey])

    def colsum(src_sl, dst_sl, rkeys, wkey):
        P.mm([lambda e: e.matmul(ps_small.t[0:1, 0:1], src_sl, bufs["ones"].t[:, 0:1],
                                 start=True, stop=True)],
             r=rkeys + ["ones"], w=[ps_small.name])
        P.op("v", lambda e: e.tensor_copy(dst_sl, ps_small.t[0:1, 0:1]),
             r=[ps_small.name], w=[wkey])

    def fullmax(src_ap, dst_sl, rkeys, wkey, negate=False):
        P.op("v", lambda e: e.reduce_max(col.t[:, 1:2], src_ap, axis=AX, negate=negate),
             r=rkeys, w=[("rt", "mxc")])
        P.mm([lambda e: e.transpose(ps_small.t[0:1, 0:128], col.t[:, 1:2], bufs["id_f"].ap())],
             r=[("rt", "mxc"), "id_f"], w=[ps_small.name])
        P.op("v", lambda e: e.tensor_copy(trp.ap(), ps_small.t[0:1, 0:128]),
             r=[ps_small.name], w=[("rt", "trp")])
        P.op("v", lambda e: e.reduce_max(dst_sl, trp.ap(), axis=AX, negate=negate),
             r=[("rt", "trp")], w=[wkey])

    for it in range(50):
        P.op("v", lambda e: e.tensor_scalar(wk.ap(), s_full.ap(), col.t[:, 0:1], None, ALU.add),
             r=["s_full", ("rt", "arep")], w=[("rt", "wk")])
        P.op("v", lambda e: e.tensor_scalar(wk.ap(), wk.ap(), 0.0, -1.0, ALU.max, ALU.mult),
             r=[("rt", "wk")], w=[("rt", "wk")])
        P.op("v", lambda e: e.tensor_tensor(wk.ap(), s_full.ap(), wk.ap(), ALU.add),
             r=["s_full", ("rt", "wk")], w=[("rt", "wk")])
        fullmax(wk.ap(), sc.t[0:1, 0:1], [("rt", "wk")], ("rt", "m"))
        bcast(sc.t[0:1, 0:1], col.t[:, 2:3], [("rt", "m")], ("rt", "mrep"))
        P.op("v", lambda e: e.tensor_scalar(col.t[:, 3:4], col.t[:, 2:3], -10.0, None, ALU.mult),
             r=[("rt", "mrep")], w=[("rt", "mb")])
        P.op("s", lambda e: e.activation(wk2.ap(), wk.ap(), AF.Exp, scale=10.0,
                                         bias=col.t[:, 3:4]),
             r=[("rt", "wk"), ("rt", "mb")], w=[("rt", "ex")])
        P.op("v", lambda e: e.reduce_sum(col.t[:, 4:5], wk2.ap(), axis=AX),
             r=[("rt", "ex")], w=[("rt", "rs")])
        colsum(col.t[:, 4:5], sc.t[0:1, 1:2], [("rt", "rs")], ("rt", "S"))
        P.op("s", lambda e: e.activation(sc.t[0:1, 2:3], sc.t[0:1, 1:2], AF.Ln),
             r=[("rt", "S")], w=[("rt", "lnS")])
        P.op("v", lambda e: e.tensor_scalar(sc.t[0:1, 2:3], sc.t[0:1, 2:3], -EPSR, CONST,
                                            ALU.mult, ALU.add),
             r=[("rt", "lnS")], w=[("rt", "t")])
        P.op("v", lambda e: e.tensor_tensor(sc.t[0:1, 3:4], sc.t[0:1, 2:3], sc.t[0:1, 0:1],
                                            ALU.subtract),
             r=[("rt", "t"), ("rt", "m")], w=[("rt", "a")])
        bcast(sc.t[0:1, 3:4], col.t[:, 0:1], [("rt", "a")], ("rt", "arep"))
    P.op("v", lambda e: e.tensor_scalar(wk.ap(), s_full.ap(), col.t[:, 0:1], None, ALU.add),
         r=["s_full", ("rt", "arep")], w=[("rt", "wk")])
    P.op("v", lambda e: e.tensor_scalar(sat.ap(), wk.ap(), 0.0, None, ALU.is_ge),
         r=[("rt", "wk")], w=["rt_sat"])
    P.op("v", lambda e: e.reduce_sum(col.t[:, 4:5], sat.ap(), axis=AX),
         r=["rt_sat"], w=[("rt", "rs")])
    colsum(col.t[:, 4:5], sc.t[0:1, 4:5], [("rt", "rs")], ("rt", "csat"))
    P.op("v", lambda e: e.tensor_scalar(sc.t[0:1, 5:6], sc.t[0:1, 4:5], 1024.0, None, ALU.min),
         r=[("rt", "csat")], w=[("rt", "K1")])
    P.op("v", lambda e: e.tensor_scalar(sc.t[0:1, 6:7], sc.t[0:1, 5:6], -1.0, 1024.0,
                                        ALU.mult, ALU.add),
         r=[("rt", "K1")], w=[("rt", "K2")])

    def bisect(lo_init_sl, lo_init_keys, hi_init_sl, hi_init_keys, cnt_fn, K_sl, K_key,
               iters, lo_sl, hi_sl, lo_key, hi_key, increasing):
        if lo_init_sl is None:
            P.op("g", lambda e: e.memset(lo_sl, 0.0), w=[lo_key])
        else:
            P.op("v", lambda e: e.tensor_copy(lo_sl, lo_init_sl), r=lo_init_keys, w=[lo_key])
        P.op("v", lambda e: e.tensor_copy(hi_sl, hi_init_sl), r=hi_init_keys, w=[hi_key])
        for it2 in range(iters):
            P.op("v", lambda e: e.tensor_tensor(sc.t[0:1, 10:11], lo_sl, hi_sl, ALU.add),
                 r=[lo_key, hi_key], w=[("rt", "sum")])
            P.op("v", lambda e: e.tensor_scalar(sc.t[0:1, 10:11], sc.t[0:1, 10:11], 0.5,
                                                None, ALU.mult),
                 r=[("rt", "sum")], w=[("rt", "mid")])
            bcast(sc.t[0:1, 10:11], col.t[:, 5:6], [("rt", "mid")], ("rt", "midb"))
            cnt_fn()
            P.op("v", lambda e: e.reduce_sum(col.t[:, 6:7], wk2.ap(), axis=AX),
                 r=[("rt", "cmask")], w=[("rt", "rs2")])
            colsum(col.t[:, 6:7], sc.t[0:1, 11:12], [("rt", "rs2")], ("rt", "cnt"))
            P.op("v", lambda e: e.tensor_tensor(sc.t[0:1, 12:13], sc.t[0:1, 11:12], K_sl,
                                                ALU.is_gt),
                 r=[("rt", "cnt"), K_key], w=[("rt", "gt")])
            if increasing:
                P.op("v", lambda e: e.tensor_copy(sc.t[0:1, 18:19], sc.t[0:1, 12:13]),
                     r=[("rt", "gt")], w=[("rt", "selhi")])
            else:
                P.op("v", lambda e: e.tensor_scalar(sc.t[0:1, 18:19], sc.t[0:1, 12:13],
                                                    -1.0, 1.0, ALU.mult, ALU.add),
                     r=[("rt", "gt")], w=[("rt", "selhi")])
            P.op("v", lambda e: e.tensor_scalar(sc.t[0:1, 19:20], sc.t[0:1, 18:19],
                                                -1.0, 1.0, ALU.mult, ALU.add),
                 r=[("rt", "selhi")], w=[("rt", "sello")])
            P.op("v", lambda e: e.tensor_tensor(sc.t[0:1, 13:14], sc.t[0:1, 10:11], hi_sl,
                                                ALU.subtract),
                 r=[("rt", "mid"), hi_key], w=[("rt", "d1")])
            P.op("v", lambda e: e.scalar_tensor_tensor(hi_sl, sc.t[0:1, 13:14], sc.t[0:1, 18:19],
                                                       hi_sl, ALU.mult, ALU.add),
                 r=[("rt", "d1"), ("rt", "selhi"), hi_key], w=[hi_key])
            P.op("v", lambda e: e.tensor_tensor(sc.t[0:1, 16:17], sc.t[0:1, 10:11], lo_sl,
                                                ALU.subtract),
                 r=[("rt", "mid"), lo_key], w=[("rt", "d2")])
            P.op("v", lambda e: e.scalar_tensor_tensor(lo_sl, sc.t[0:1, 16:17], sc.t[0:1, 19:20],
                                                       lo_sl, ALU.mult, ALU.add),
                 r=[("rt", "d2"), ("rt", "sello"), lo_key], w=[lo_key])

    def cnt_idx():
        P.op("v", lambda e: e.tensor_scalar(wk2.ap(), idxf.ap(), col.t[:, 5:6], None, ALU.is_lt),
             r=["rt_idx", ("rt", "midb")], w=[("rt", "cmask")])
        P.op("v", lambda e: e.tensor_tensor(wk2.ap(), wk2.ap(), sat.ap(), ALU.mult),
             r=[("rt", "cmask"), "rt_sat"], w=[("rt", "cmask")])
    P.op("g", lambda e: e.memset(sc.t[0:1, 8:9], 8200.0), w=[("rt", "hi_init")])
    bisect(None, [], sc.t[0:1, 8:9], [("rt", "hi_init")], cnt_idx, sc.t[0:1, 5:6], ("rt", "K1"),
           20, sc.t[0:1, 7:8], sc.t[0:1, 8:9], ("rt", "ilo"), ("rt", "ihi"), increasing=True)
    P.op("v", lambda e: e.tensor_copy(sc.t[0:1, 9:10], sc.t[0:1, 7:8]),
         r=[("rt", "ilo")], w=[("rt", "theta_i")])
    fullmax(s_full.ap(), sc.t[0:1, 0:1], ["s_full"], ("rt", "smax"))
    P.op("v", lambda e: e.tensor_scalar(sc.t[0:1, 0:1], sc.t[0:1, 0:1], 1.0, None, ALU.add),
         r=[("rt", "smax")], w=[("rt", "smax1")])
    fullmax(s_full.ap(), sc.t[0:1, 1:2], ["s_full"], ("rt", "smin"), negate=True)
    P.op("v", lambda e: e.tensor_scalar(sc.t[0:1, 1:2], sc.t[0:1, 1:2], -1.0, -1.0,
                                        ALU.mult, ALU.add),
         r=[("rt", "smin")], w=[("rt", "smin1")])

    P.op("v", lambda e: e.tensor_scalar(wk.ap(), sat.ap(), -1.0, 1.0, ALU.mult, ALU.add),
         r=["rt_sat"], w=[("rt", "unsat")])

    def cnt_s2():
        P.op("v", lambda e: e.tensor_scalar(wk2.ap(), s_full.ap(), col.t[:, 5:6], None, ALU.is_gt),
             r=["s_full", ("rt", "midb")], w=[("rt", "cmask")])
        P.op("v", lambda e: e.tensor_tensor(wk2.ap(), wk2.ap(), wk.ap(), ALU.mult),
             r=[("rt", "cmask"), ("rt", "unsat")], w=[("rt", "cmask")])
    bisect(sc.t[0:1, 1:2], [("rt", "smin1")], sc.t[0:1, 0:1], [("rt", "smax1")], cnt_s2,
           sc.t[0:1, 6:7], ("rt", "K2"),
           42, sc.t[0:1, 14:15], sc.t[0:1, 15:16], ("rt", "slo"), ("rt", "shi"),
           increasing=False)
    bcast(toff.t[0:1, 0:1], col.t[:, 6:7], ["toff"], ("rt", "toffb"))
    bcast(sc.t[0:1, 9:10], col.t[:, 5:6], [("rt", "theta_i")], ("rt", "thib"))
    bcast(sc.t[0:1, 15:16], col.t[:, 7:8], [("rt", "shi")], ("rt", "thsb"))
    loc_i = Buf(nc, "rt_loci", [128, NT], F32)
    loc_m = Buf(nc, "rt_locm", [128, NT], F32)
    P.op("g", lambda e: e.iota(loc_i.ap(), [[128, NT]], base=0, channel_multiplier=1,
                               allow_small_or_imprecise_dtypes=True), w=[("rt", "loci")])
    P.op("v", lambda e: e.tensor_scalar(loc_i.ap(), loc_i.ap(), col.t[:, 6:7], None, ALU.add),
         r=[("rt", "loci"), ("rt", "toffb")], w=[("rt", "loci")])
    P.op("v", lambda e: e.tensor_scalar(loc_i.ap(), loc_i.ap(), col.t[:, 5:6], None, ALU.is_lt),
         r=[("rt", "loci"), ("rt", "thib")], w=[("rt", "ltheta")])
    P.op("v", lambda e: e.tensor_scalar(loc_m.ap(), bufs["s_sb"].ap(), col.t[:, 0:1], None,
                                        ALU.add),
         r=[("s_sb", i) for i in range(NT)] + [("rt", "arep")], w=[("rt", "sa_loc")])
    P.op("v", lambda e: e.tensor_scalar(loc_m.ap(), loc_m.ap(), 0.0, None, ALU.is_ge),
         r=[("rt", "sa_loc")], w=[("rt", "sat_loc")])
    P.op("v", lambda e: e.tensor_tensor(loc_i.ap(), loc_i.ap(), loc_m.ap(), ALU.mult),
         r=[("rt", "ltheta"), ("rt", "sat_loc")], w=[("rt", "sel1")])
    P.op("v", lambda e: e.tensor_scalar(loc_m.ap(), loc_m.ap(), -1.0, 1.0, ALU.mult, ALU.add),
         r=[("rt", "sat_loc")], w=[("rt", "unsat_loc")])
    P.op("v", lambda e: e.tensor_scalar(bufs["hmask"].ap(), bufs["s_sb"].ap(), col.t[:, 7:8],
                                        None, ALU.is_gt),
         r=[("s_sb", i) for i in range(NT)] + [("rt", "thsb")], w=["hmask"])
    P.op("v", lambda e: e.tensor_tensor(bufs["hmask"].ap(), bufs["hmask"].ap(), loc_m.ap(),
                                        ALU.mult),
         r=["hmask", ("rt", "unsat_loc")], w=["hmask"])
    P.op("v", lambda e: e.tensor_tensor(bufs["hmask"].ap(), bufs["hmask"].ap(), loc_i.ap(),
                                        ALU.add),
         r=["hmask", ("rt", "sel1")], w=["hmask"])
    if "rt_dbg" in bufs:
        for cidx in (0, 3, 4, 5, 6, 8, 9, 15):
            P.op("s", lambda e, cidx=cidx: e.copy(bufs["rt_dbg"].t[0:1, cidx:cidx + 1],
                                                  sc.t[0:1, cidx:cidx + 1]),
                 r=[("rt", k) for k in ("arep", "csat", "K1", "K2", "theta_i", "shi", "ihi")],
                 w=["rt_dbg"])
    P.barrier()
    loc_m.free(); loc_i.free(); toff.free(); trp.free(); idxf.free(); sat.free()
    wk2.free(); wk.free(); col.free(); sc.free()


def emit_light(P, nc, bufs, dram):
    x_res = bufs["x_res"]
    w1l = Buf(nc, "w1l", [128, DC, 512], BF16)
    w2l = Buf(nc, "w2l", [128, 4, 1024], BF16)
    glc = Buf(nc, "glc", [128, DC], F32)
    b1l = Buf(nc, "b1l", [128, 4], F32)
    ps_rb = Buf(nc, "ps_rbl", [128, 512], F32, psum=True)
    b2l = emit_row_broadcast(P, nc, bufs, "b2l_rep", dram["b2_light"].unsqueeze(0), 1024, ps_rb)
    P.barrier()
    ps_rb.free()
    lt_t = Buf(nc, "lt_t", [128, 1024], F32)
    bufs["lt_t"] = lt_t
    hnl = Buf(nc, "hnl", [128, 512], BF16)
    h1l = Buf(nc, "h1l", [128, 4, 512], BF16)
    hn0c = Buf(nc, "hn0c", [128, DC, 512], BF16)
    ps_l = [Buf(nc, "ps_l%d" % f_, [128, 512], F32, psum=True) for f_ in range(4)]
    ps_lo = Buf(nc, "ps_lo", [128, 1024], F32, psum=True)
    P.dma("g", lambda e: e.dma_start(w1l.ap(), dram["w1_light"].rearrange("(a p) f -> p a f", p=128)),
          w=["w1l"])
    P.dma("g", lambda e: e.dma_start(w2l.ap(), dram["w2_light"].rearrange("(a p) f -> p a f", p=128)),
          w=["w2l"])
    P.dma("g", lambda e: e.dma_start(glc.ap(), dram["gamma_light"].rearrange("(a p) -> p a", p=128), allow_slow_non_contiguous=True),
          w=["glc"])
    P.dma("g", lambda e: e.dma_start(b1l.ap(), dram["b1_light"].rearrange("(a p) -> p a", p=128), allow_slow_non_contiguous=True),
          w=["b1l"])
    for q in range(4):
        P.dma("y", lambda e, q=q: e.dma_start(
            hn0c.ap(),
            dram["hn0T"].rearrange("(a p) t -> p a t", p=128)[:, :, q * 512:(q + 1) * 512]),
            w=["hn0c"])
        for a in range(DC):
            P.op("v", lambda e, a=a: e.tensor_scalar(hnl.ap(), hn0c.t[:, a, :],
                                                     glc.t[:, a:a + 1], None, ALU.mult),
                 r=["hn0c", "glc"], w=["hnl"])
            P.mm([(lambda e, fb=fb, a=a: e.matmul(
                ps_l[fb].t[:, 0:512],
                w1l.t[:, a, fb * 128:(fb + 1) * 128],
                hnl.ap(),
                start=(a == 0), stop=(a == DC - 1), skip_group_check=True))
                for fb in range(4)],
                r=["w1l", "hnl"], w=["ps_l%d" % f_ for f_ in range(4)])
        for fb in range(4):
            P.op("s", lambda e, fb=fb: e.activation(h1l.t[:, fb, :], ps_l[fb].ap(), AF.Gelu,
                                                    bias=b1l.t[:, fb:fb + 1]),
                 r=["ps_l%d" % fb, "b1l"], w=[("h1l", fb)])
        for tt in range(4):
            i = q * 4 + tt
            P.mm([(lambda e, fb=fb, tt=tt, c0=c0: e.matmul(
                ps_lo.t[:, c0:c0 + 512],
                h1l.t[:, fb, tt * 128:(tt + 1) * 128],
                w2l.t[:, fb, c0:c0 + 512],
                start=(fb == 0), stop=(fb == 3), skip_group_check=True))
                for c0 in (0, 512) for fb in range(4)],
                r=[("h1l", f_) for f_ in range(4)] + ["w2l"], w=["ps_lo"])
            P.op("v", lambda e: e.tensor_tensor(bufs["lt_t"].ap(), ps_lo.ap(), b2l.ap(), ALU.add),
                 r=["ps_lo", "b2l_rep"], w=["lt_t"])
            P.op("v", lambda e, i=i: e.tensor_tensor(x_res.t[:, i, :], x_res.t[:, i, :],
                                                     bufs["lt_t"].ap(), ALU.add),
                 r=["lt_t", ("x_res", i)], w=[("x_res", i)])
    P.barrier()
    ps_lo.free()
    for b_ in reversed(ps_l):
        b_.free()
    hn0c.free(); h1l.free(); hnl.free(); lt_t.free()
    bufs["b2l_rep"].free()
    b1l.free(); glc.free(); w2l.free(); w1l.free()


def emit_heavy(P, nc, bufs, dram):
    x_res = bufs["x_res"]
    GROUPS = [(0, 3), (3, 3), (6, 3), (9, 3), (12, 3), (15, 1)]
    w2h = Buf(nc, "w2h", [128, 32, 1024], BF16)
    ghc = Buf(nc, "ghc", [128, DC], F32)
    b1h = Buf(nc, "b1h", [128, 32], F32)
    ps_rb = Buf(nc, "ps_rbh", [128, 512], F32, psum=True)
    b2h = emit_row_broadcast(P, nc, bufs, "b2h_rep", dram["b2_heavy"].unsqueeze(0), 1024, ps_rb)
    P.barrier()
    ps_rb.free()
    bufs["hv_t"] = hv_t = Buf(nc, "hv_t", [128, 1024], F32)
    hn0c = Buf(nc, "hn0ch", [128, DC, 384], BF16)
    hnh = Buf(nc, "hnh", [128, DC, 384], BF16)
    w1c = Buf(nc, "w1ch", [128, 2, DC, 128], BF16)
    hTs = Buf(nc, "hTs", [128, 2, 384], BF16)
    out_ps_l = [Buf(nc, "hv_out%d" % j, [128, 1024], F32, psum=True) for j in range(3)]
    hT_ps_l = [Buf(nc, "hv_h%d" % s, [128, 512], F32, psum=True) for s in range(2)]
    P.dma("g", lambda e: e.dma_start(ghc.ap(), dram["gamma_heavy"].rearrange("(a p) -> p a", p=128), allow_slow_non_contiguous=True),
          w=["ghc"])
    P.dma("g", lambda e: e.dma_start(b1h.ap(), dram["b1_heavy"].rearrange("(a p) -> p a", p=128), allow_slow_non_contiguous=True),
          w=["b1h"])
    for qq in range(4):
        P.dma("g", lambda e, qq=qq: e.dma_start(
            w2h.t[:, qq * 8:(qq + 1) * 8, :],
            dram["w2_heavy"].rearrange("(n p) c -> p n c", p=128)[:, qq * 8:(qq + 1) * 8, :]),
            w=[("w2h", qq)])
    for gi, (g0, glen) in enumerate(GROUPS):
        P.dma("y", lambda e, g0=g0, glen=glen: e.dma_start(
            hn0c.t[:, :, 0:glen * 128],
            dram["hn0T"].rearrange("(a p) t -> p a t", p=128)[:, :, g0 * 128:(g0 + glen) * 128]),
            w=["hn0ch"])
        for a in range(DC):
            P.op("v", lambda e, a=a, glen=glen: e.tensor_scalar(
                hnh.t[:, a, 0:glen * 128], hn0c.t[:, a, 0:glen * 128],
                ghc.t[:, a:a + 1], None, ALU.mult),
                r=["hn0ch", "ghc"], w=[("hnh", a)])
        def emit_outh(fc, glen):
            sl = fc % 2
            P.mm([(lambda e, j=j, sl=sl, fc=fc, c0=c0: e.matmul(
                out_ps_l[j].t[:, c0:c0 + 512],
                hTs.t[:, sl, j * 128:(j + 1) * 128],
                w2h.t[:, fc, c0:c0 + 512],
                start=(fc == 0), stop=(fc == 31), skip_group_check=True))
                for j in range(glen) for c0 in (0, 512)],
                r=[("hTs", sl)] + [("w2h", q) for q in range(4)],
                w=["hv_out%d" % j for j in range(glen)])
        for fc in range(32):
            sl = fc % 2
            P.dma("g", lambda e, fc=fc, sl=sl: e.dma_start(
                w1c.t[:, sl, :, :],
                dram["w1_heavy"].rearrange("(a p) f -> p a f", p=128)[:, :, fc * 128:(fc + 1) * 128]),
                w=[("w1ch", sl)])
            P.mm([(lambda e, a=a, sl=sl, glen=glen: e.matmul(
                hT_ps_l[sl].t[:, 0:glen * 128],
                w1c.t[:, sl, a, :],
                hnh.t[:, a, 0:glen * 128],
                start=(a == 0), stop=(a == DC - 1)))
                for a in range(DC)],
                r=[("w1ch", sl)] + [("hnh", a) for a in range(DC)],
                w=["hv_h%d" % sl])
            P.op("s", lambda e, sl=sl, fc=fc, glen=glen: e.activation(
                hTs.t[:, sl, 0:glen * 128], hT_ps_l[sl].t[:, 0:glen * 128], AF.Gelu,
                bias=b1h.t[:, fc:fc + 1]),
                r=["hv_h%d" % sl, "b1h"], w=[("hTs", sl)])
            if fc > 0:
                emit_outh(fc - 1, glen)
        emit_outh(31, glen)
        for j in range(glen):
            i = g0 + j
            P.op("v", lambda e, j=j: e.tensor_tensor(
                bufs["hv_t"].ap(), out_ps_l[j].ap(), bufs["b2h_rep"].ap(), ALU.add),
                r=["hv_out%d" % j, "b2h_rep"], w=["hv_t"])
            P.op("v", lambda e, i=i: e.scalar_tensor_tensor(
                x_res.t[:, i, :],
                bufs["hv_t"].ap(),
                bufs["hmask"].t[:, i:i + 1],
                x_res.t[:, i, :],
                ALU.mult, ALU.add),
                r=["hv_t", "hmask", ("x_res", i)], w=[("x_res", i)])
    P.barrier()
    for b_ in reversed(hT_ps_l):
        b_.free()
    for b_ in reversed(out_ps_l):
        b_.free()
    hTs.free(); w1c.free(); hnh.free(); hn0c.free(); hv_t.free()
    bufs["b2h_rep"].free()
    b1h.free(); ghc.free(); w2h.free()



def build_C():
    nc = bass.Bass("TRN2", target_bir_lowering=False)
    x1 = nc.dram_tensor("x1", [TPC, D], F32, kind="ExternalInput").ap()
    zall = nc.dram_tensor("zall", [8 * 66, 1024], F32, kind="ExternalInput").ap()
    dist1h = nc.dram_tensor("dist1h", [1, 24], F32, kind="ExternalInput").ap()
    tok_off = nc.dram_tensor("tok_off", [1, 1], F32, kind="ExternalInput").ap()
    smask_d = nc.dram_tensor("smask", [1, 8], F32, kind="ExternalInput").ap()
    names = {
        "mix": [D, 64], "Dvec": [D], "log_tau": [NST], "log_dt": [1],
        "norm1_w": [D], "norm1_b": [D], "norm2_w": [D], "norm2_b": [D],
        "fc1_w": [4096, D], "fc1_b": [4096], "fc2_w": [D, 4096], "fc2_b": [D],
        "routing": [D], "gamma_light": [D], "w1_light": [D, 512], "b1_light": [512],
        "w2_light": [512, D], "b2_light": [D], "gamma_heavy": [D],
        "w1_heavy": [D, 4096], "b1_heavy": [4096], "w2_heavy": [4096, D], "b2_heavy": [D],
    }
    dram = {k: nc.dram_tensor(k, shp, F32, kind="ExternalInput").ap() for k, shp in names.items()}
    out = nc.dram_tensor("out", [TPC, D], F32, kind="ExternalOutput").ap()
    if DBG_XM:
        dram["xm_o"] = nc.dram_tensor("xm_o", [TPC, D], F32, kind="ExternalOutput").ap()
        dram["st1_o"] = nc.dram_tensor("st1_o", [TPC, D], F32, kind="ExternalOutput").ap()
        dram["st2_o"] = nc.dram_tensor("st2_o", [TPC, D], F32, kind="ExternalOutput").ap()
    dram["psnap"] = nc.dram_tensor("psnap", [64, NT * NSUB * 1024], BF16).ap()
    dram["hn0T"] = nc.dram_tensor("hn0T_sc", [D, TPC], BF16).ap()
    dram["fc1T"] = nc.dram_tensor("fc1T_sc", [D, 4096], BF16).ap()
    dram["fc2T"] = nc.dram_tensor("fc2T_sc", [4096, D], BF16).ap()

    P = Prog(nc)
    bufs = {}
    emit_const_tiles(P, nc, bufs)
    bufs["x_res"] = x_res = Buf(nc, "x_res", [128, NT, D], F32)
    bufs["hmask"] = Buf(nc, "hmask", [128, NT], F32)
    bufs["xm"] = Buf(nc, "xm", [128, NT, 1024], BF16)
    bufs["u_bf"] = Buf(nc, "u_bf", [128, NT, 1024], BF16)
    bufs["st_sb"] = Buf(nc, "st_sb", [128, NT], F32)
    bufs["s_sb"] = Buf(nc, "s_sb", [128, NT], F32)
    bufs["ps_trb3"] = Buf(nc, "ps_trb3", [128, 1024], BF16, psum=True)
    for qq in range(4):
        P.dma("y", lambda e, qq=qq: e.dma_start(
            x_res.t[:, qq * 4:(qq + 1) * 4, :],
            x1.rearrange("(n p) d -> p n d", p=128)[:, qq * 4:(qq + 1) * 4, :]),
            w=[("x_res", i) for i in range(qq * 4, qq * 4 + 4)])
    emit_stream_transpose_w(P, nc, bufs, "fc1", dram["fc1_w"], dram["fc1T"], 4096, 1024,
                            bufs["ps_trb3"])
    emit_stream_transpose_w(P, nc, bufs, "fc2", dram["fc2_w"], dram["fc2T"], 1024, 4096,
                            bufs["ps_trb3"])
    emit_l1_preamble(P, nc, bufs, {"snaps": True, "store": True, "free_tmp": True}, dram)
    bufs["S_in"] = S_in = Buf(nc, "S_in", [64, 1024], F32)
    s_full = Buf(nc, "s_full", [128, 64], F32)
    bufs["s_full"] = s_full
    zin = Buf(nc, "zin", [64, 8, 1024], BF16)
    sfin = Buf(nc, "sfin", [128, 8, 16], F32)
    ohb = Buf(nc, "ohb", [64, 24], F32)
    sm = Buf(nc, "smb", [128, 8], F32)
    rowb = Buf(nc, "rowb", [1, 32], F32)
    dec = Buf(nc, "dec", [64, 12], F32)
    acc = Buf(nc, "z_acc", [64, 1024], F32)
    ps_sm = Buf(nc, "ps_sm", [128, 512], F32, psum=True)
    for r in range(8):
        P.dma("g", lambda e, r=r: e.dma_start(zin.t[0:64, r, :], zall[r * 66:r * 66 + 64, :]),
              w=[("zin", r)])
        P.dma("y", lambda e, r=r: e.dma_start(
            sfin.t[:, r, :],
            zall[r * 66 + 64:r * 66 + 66, :].rearrange("r c -> (r c)").rearrange("(p f) -> p f", p=128)),
            w=[("sfin", r)])
    P.dma("g", lambda e: e.dma_start(rowb.t[0:1, 0:24], dist1h), w=["rowb"])
    P.dma("g", lambda e: e.dma_start(rowb.t[0:1, 24:32], smask_d), w=[("rowb", "sm")])
    P.mm([lambda e: e.matmul(ps_sm.t[:, 0:32], bufs["ones"].t[0:1, 0:128], rowb.t[0:1, 0:32],
                             start=True, stop=True)],
         r=["rowb", ("rowb", "sm"), "ones"], w=["ps_sm"])
    P.op("v", lambda e: e.tensor_copy(ohb.t[0:64, 0:24], ps_sm.t[0:64, 0:24]),
         r=["ps_sm"], w=["ohb"])
    P.op("v", lambda e: e.tensor_copy(sm.ap(), ps_sm.t[:, 24:32]),
         r=["ps_sm"], w=["smb"])
    for d_ in range(3):
        P.op("v", lambda e, d_=d_: e.tensor_scalar(dec.t[0:64, d_:d_ + 1], bufs["na_col_ap"],
                                                   2048.0 * (d_ + 1), None, ALU.mult),
             r=["na_col"], w=[("dec", d_)])
        P.op("s", lambda e, d_=d_: e.activation(dec.t[0:64, d_:d_ + 1], dec.t[0:64, d_:d_ + 1], AF.Exp),
             r=[("dec", d_)], w=[("dec", d_)])
    for r in range(8):
        P.op("v", lambda e, r=r: e.tensor_tensor(dec.t[0:64, 3 + r:4 + r],
                                                 ohb.t[0:64, r * 3:r * 3 + 1],
                                                 dec.t[0:64, 0:1], ALU.mult),
             r=["ohb", ("dec", 0)], w=[("decr", r)])
        for d_ in (1, 2):
            P.op("v", lambda e, r=r, d_=d_: e.scalar_tensor_tensor(
                dec.t[0:64, 3 + r:4 + r], dec.t[0:64, d_:d_ + 1],
                ohb.t[0:64, r * 3 + d_:r * 3 + d_ + 1], dec.t[0:64, 3 + r:4 + r],
                ALU.mult, ALU.add),
                r=["ohb", ("dec", d_), ("decr", r)], w=[("decr", r)])
    P.op("g", lambda e: e.memset(acc.t[0:64, :], 0.0), w=[("zacc", 0)])
    for r in range(8):
        src_b, dst_b = (acc, S_in) if r % 2 == 0 else (S_in, acc)
        P.op("v", lambda e, r=r, src_b=src_b, dst_b=dst_b: e.scalar_tensor_tensor(
            dst_b.t[0:64, :], zin.t[0:64, r, :], dec.t[0:64, 3 + r:4 + r], src_b.t[0:64, :],
            ALU.mult, ALU.add),
            r=[("zin", r), ("decr", r), ("zacc", r)], w=[("zacc", r + 1)])
    for slot in range(4):
        P.op("v", lambda e, slot=slot: e.tensor_scalar(
            s_full.t[:, slot * 16:(slot + 1) * 16], sfin.t[:, slot, :],
            sm.t[:, slot:slot + 1],
            None, ALU.mult),
            r=[("sfin", slot), "smb"], w=[("s_full", slot)])
        P.op("v", lambda e, slot=slot: e.scalar_tensor_tensor(
            s_full.t[:, slot * 16:(slot + 1) * 16], sfin.t[:, slot + 4, :],
            sm.t[:, slot + 4:slot + 5],
            s_full.t[:, slot * 16:(slot + 1) * 16],
            ALU.mult, ALU.add),
            r=[("sfin", slot + 4), "smb", ("s_full", slot)], w=[("s_full", slot), "s_full"])
    P.barrier()
    emit_routing_select(P, nc, bufs, s_full, ps_sm, tok_off)
    P.barrier()
    ps_sm.free(); acc.free(); dec.free(); rowb.free(); sm.free(); ohb.free(); sfin.free(); zin.free()
    emit_l1_rest(P, nc, bufs, dram)
    s_full.free(); S_in.free()
    bufs["lam"].free(); bufs["wsn"].free(); bufs["_erev"].free(); bufs["E_small_b"].free()
    bufs["E_small"].free(); bufs["_ldt"].free(); bufs["_tau"].free(); bufs["sp_scratch"].free()
    bufs["s_sb"].free(); bufs["st_sb"].free(); bufs["u_bf"].free(); bufs["xm"].free()
    bufs["ps_trb3"].free()
    if DBG_XM:
        for qq in range(4):
            P.dma("y", lambda e, qq=qq: e.dma_start(
                dram["st1_o"].rearrange("(n p) d -> p n d", p=128)[:, qq * 4:(qq + 1) * 4, :],
                x_res.t[:, qq * 4:(qq + 1) * 4, :]),
                r=[("x_res", i) for i in range(qq * 4, qq * 4 + 4)], w=[("st1", qq)])
    emit_light(P, nc, bufs, dram)
    if DBG_XM:
        for qq in range(4):
            P.dma("y", lambda e, qq=qq: e.dma_start(
                dram["st2_o"].rearrange("(n p) d -> p n d", p=128)[:, qq * 4:(qq + 1) * 4, :],
                x_res.t[:, qq * 4:(qq + 1) * 4, :]),
                r=[("x_res", i) for i in range(qq * 4, qq * 4 + 4)], w=[("st2", qq)])
    emit_heavy(P, nc, bufs, dram)
    for qq in range(4):
        P.dma("y", lambda e, qq=qq: e.dma_start(
            out.rearrange("(n p) d -> p n d", p=128)[:, qq * 4:(qq + 1) * 4, :],
            x_res.t[:, qq * 4:(qq + 1) * 4, :]),
            r=[("x_res", i) for i in range(qq * 4, qq * 4 + 4)], w=[("out", qq)])
    P.barrier()
    P.finalize()
    P.close()
    return nc


def kernel_unfused(**inputs):
    shards = shard_inputs(inputs)
    f32 = lambda a: np.ascontiguousarray(np.asarray(a, dtype=np.float32))
    ncA = build_A()
    mapsA = [{"x": shards[c], "Wk": f32(inputs["L0_attn_W_k"]),
              "Wv": f32(inputs["L0_attn_W_v"])} for c in range(8)]
    resA = run_bass_kernel_spmd(ncA, mapsA, core_ids=list(range(8)))
    stot_all = np.concatenate([resA.results[c]["stot"] for c in range(8)], 0)
    ncB = build_B()
    mapsB = []
    for c in range(8):
        pm = np.zeros((1, 8), np.float32)
        for r in range(4 * (c // 4), c):
            pm[0, r] = 1.0
        mapsB.append({
            "x": shards[c], "Wq": f32(inputs["L0_attn_W_q"]), "Wk": f32(inputs["L0_attn_W_k"]),
            "Wv": f32(inputs["L0_attn_W_v"]), "Wo": f32(inputs["L0_attn_W_o"]),
            "gate_w": f32(inputs["L0_ffn_gate_w"]),
            "w1": f32(inputs["L0_ffn_w1"]).reshape(8 * 1024, 4096),
            "w2": f32(inputs["L0_ffn_w2"]).reshape(8 * 4096, 1024),
            "stot_all": stot_all, "pred_mask": pm,
            "log_tau": f32(inputs["L1_attn_log_tau"]),
            "log_dt": f32(inputs["L1_attn_log_dt"]).reshape(1),
            "norm1_w": f32(inputs["L1_attn_norm1_w"]), "norm1_b": f32(inputs["L1_attn_norm1_b"]),
            "routing": f32(inputs["L1_ffn_routing_token"]),
        })
    resB = run_bass_kernel_spmd(ncB, mapsB, core_ids=list(range(8)))
    zall = np.concatenate([resB.results[c]["zpack"] for c in range(8)], 0)
    ncC = build_C()
    mapsC = []
    for c in range(8):
        b, q = c // 4, c % 4
        d1h = np.zeros((1, 24), np.float32)
        sm = np.zeros((1, 8), np.float32)
        for r in range(8):
            if r // 4 == b:
                sm[0, r] = 1.0
                if r % 4 < q:
                    d = q - (r % 4) - 1
                    d1h[0, r * 3 + d] = 1.0
        mapsC.append({
            "x1": resB.results[c]["x1_out"], "zall": zall, "dist1h": d1h, "smask": sm,
            "tok_off": np.full((1, 1), q * TPC, np.float32),
            "mix": f32(inputs["L1_attn_mix"]), "Dvec": f32(inputs["L1_attn_D"]),
            "log_tau": f32(inputs["L1_attn_log_tau"]),
            "log_dt": f32(inputs["L1_attn_log_dt"]).reshape(1),
            "norm1_w": f32(inputs["L1_attn_norm1_w"]), "norm1_b": f32(inputs["L1_attn_norm1_b"]),
            "norm2_w": f32(inputs["L1_attn_norm2_w"]), "norm2_b": f32(inputs["L1_attn_norm2_b"]),
            "fc1_w": f32(inputs["L1_attn_fc1_w"]), "fc1_b": f32(inputs["L1_attn_fc1_b"]),
            "fc2_w": f32(inputs["L1_attn_fc2_w"]), "fc2_b": f32(inputs["L1_attn_fc2_b"]),
            "routing": f32(inputs["L1_ffn_routing_token"]),
            "gamma_light": f32(inputs["L1_ffn_gamma_light"]),
            "w1_light": f32(inputs["L1_ffn_w1_light"]), "b1_light": f32(inputs["L1_ffn_b1_light"]),
            "w2_light": f32(inputs["L1_ffn_w2_light"]), "b2_light": f32(inputs["L1_ffn_b2_light"]),
            "gamma_heavy": f32(inputs["L1_ffn_gamma_heavy"]),
            "w1_heavy": f32(inputs["L1_ffn_w1_heavy"]), "b1_heavy": f32(inputs["L1_ffn_b1_heavy"]),
            "w2_heavy": f32(inputs["L1_ffn_w2_heavy"]), "b2_heavy": f32(inputs["L1_ffn_b2_heavy"]),
        })
    resC = run_bass_kernel_spmd(ncC, mapsC, core_ids=list(range(8)))
    out = np.zeros((2, 8192, 1024), np.float32)
    for c in range(8):
        b, q = c // 4, c % 4
        out[b, q * TPC:(q + 1) * TPC] = resC.results[c]["out"]
    return out



def build_F():
    nc = bass.Bass("TRN2", target_bir_lowering=False)
    x = nc.dram_tensor("x", [TPC, D], F32, kind="ExternalInput").ap()
    wq = nc.dram_tensor("Wq", [D, D], F32, kind="ExternalInput").ap()
    wk = nc.dram_tensor("Wk", [D, D], F32, kind="ExternalInput").ap()
    wv = nc.dram_tensor("Wv", [D, D], F32, kind="ExternalInput").ap()
    wo = nc.dram_tensor("Wo", [D, D], F32, kind="ExternalInput").ap()
    gate_w = nc.dram_tensor("gate_w", [D, 8], F32, kind="ExternalInput").ap()
    w1 = nc.dram_tensor("w1", [8 * D, 4096], F32, kind="ExternalInput").ap()
    w2 = nc.dram_tensor("w2", [8 * 4096, D], F32, kind="ExternalInput").ap()
    pred_mask = nc.dram_tensor("pred_mask", [1, 8], F32, kind="ExternalInput").ap()
    dist1h = nc.dram_tensor("dist1h", [1, 24], F32, kind="ExternalInput").ap()
    smask_d = nc.dram_tensor("smask", [1, 8], F32, kind="ExternalInput").ap()
    tok_off = nc.dram_tensor("tok_off", [1, 1], F32, kind="ExternalInput").ap()
    names = {
        "mix": [D, 64], "Dvec": [D], "log_tau": [NST], "log_dt": [1],
        "norm1_w": [D], "norm1_b": [D], "norm2_w": [D], "norm2_b": [D],
        "fc1_w": [4096, D], "fc1_b": [4096], "fc2_w": [D, 4096], "fc2_b": [D],
        "routing": [D], "gamma_light": [D], "w1_light": [D, 512], "b1_light": [512],
        "w2_light": [512, D], "b2_light": [D], "gamma_heavy": [D],
        "w1_heavy": [D, 4096], "b1_heavy": [4096], "w2_heavy": [4096, D], "b2_heavy": [D],
    }
    dram = {k: nc.dram_tensor(k, shp, F32, kind="ExternalInput").ap() for k, shp in names.items()}
    out = nc.dram_tensor("out", [TPC, D], F32, kind="ExternalOutput").ap()
    dram["psnap"] = nc.dram_tensor("psnap", [64, NT * NSUB * 1024], BF16).ap()
    dram["hn0T"] = nc.dram_tensor("hn0T_sc", [D, TPC], BF16).ap()
    dram["fc1T"] = nc.dram_tensor("fc1T_sc", [D, 4096], BF16).ap()
    dram["fc2T"] = nc.dram_tensor("fc2T_sc", [4096, D], BF16).ap()
    ccin1 = nc.dram_tensor("ccin1", [64, H * 65], F32).ap()
    ccout1 = nc.dram_tensor("ccout1", [8 * 64, H * 65], F32).ap()
    ccin2 = nc.dram_tensor("ccin2", [66, 1024], F32).ap()
    ccout2 = nc.dram_tensor("ccout2", [8 * 66, 1024], F32).ap()

    P = Prog(nc)
    bufs = {}
    emit_const_tiles(P, nc, bufs)
    bufs["x_res"] = x_res = Buf(nc, "x_res", [128, NT, D], F32)
    bufs["xnT"] = xnT = Buf(nc, "xnT", [128, DC, TPC], BF16)
    c_store = Buf(nc, "c_store", [128, NT * 8], F32)
    bufs["ps_tr"] = ps_tr = Buf(nc, "ps_tr", [128, 1024], BF16, psum=True)
    for qq in range(4):
        P.dma("y", lambda e, qq=qq: e.dma_start(
            x_res.t[:, qq * 4:(qq + 1) * 4, :],
            x.rearrange("(n p) d -> p n d", p=128)[:, qq * 4:(qq + 1) * 4, :]),
            w=[("x_res", i) for i in range(qq * 4, qq * 4 + 4)])
    bufs["gate_sb"] = gate_sb = Buf(nc, "gate_sb", [128, DC, 8], BF16)
    P.dma("g", lambda e: e.dma_start(gate_sb.ap(),
                                     gate_w.rearrange("(a p) e -> p a e", p=128)),
          w=["gate_sb"])
    emit_stream_transpose_w(P, nc, bufs, "fc1", dram["fc1_w"], dram["fc1T"], 4096, 1024, ps_tr)
    emit_stream_transpose_w(P, nc, bufs, "fc2", dram["fc2_w"], dram["fc2T"], 1024, 4096, ps_tr)
    load_w_transposed(P, nc, bufs, "WkT", wk, D, D, ps_tr)
    load_w_transposed(P, nc, bufs, "WvT", wv, D, D, ps_tr)
    emit_ln_xnT(P, nc, bufs, "x_res", x_res, xnT, None, ps_tr, "ln0")
    snaps = Buf(nc, "snaps", [128, NT // 2, H * 65], BF16)
    stot_sb = Buf(nc, "stot_sb", [64, H * 65], F32)
    emit_l0_kv_pass(P, nc, bufs, snaps, stot_sb)
    P.dma("y", lambda e: e.dma_start(ccin1, stot_sb.t[0:64, :]), r=["stot_sb"], w=["ccin1"])
    P.cc(0, lambda e: e.collective_compute(
        "AllGather", ALU.bypass, replica_groups=[list(range(8))],
        ins=[ccin1.opt()], outs=[ccout1.opt()]),
        r=["ccin1"], w=["ccout1"])
    bufs["Jfull"] = Jf = Buf(nc, "Jfull", [128, 128], BF16)
    for pb in (0, 64):
        for fb in (0, 64):
            P.op("v", lambda e, pb=pb, fb=fb: e.tensor_copy(
                Jf.t[pb:pb + 64, fb:fb + 64], bufs["id_b"].t[pb:pb + 64, pb:pb + 64]),
                r=["id_b"], w=["Jfull"])
    offdup = Buf(nc, "offdup", [128, H * 65], F32)
    stin = Buf(nc, "stin", [64, 8, H * 65], F32)
    msk = Buf(nc, "msk", [64, 8], F32)
    mrow = Buf(nc, "mrow", [1, 8], F32)
    offb = Buf(nc, "offb", [64, H * 65], F32)
    offc = Buf(nc, "offc", [64, H * 65], F32)
    ps_off = Buf(nc, "ps_off", [64, 512], F32, psum=True)
    ps_dup = Buf(nc, "ps_dup", [128, 1536], F32, psum=True)
    for r in range(8):
        P.dma("y", lambda e, r=r: e.dma_start(stin.t[0:64, r, :],
                                              ccout1[r * 64:(r + 1) * 64, :]),
              r=["ccout1"], w=[("stin", r)])
    P.dma("g", lambda e: e.dma_start(mrow.ap(), pred_mask), w=["mrow"])
    P.mm([lambda e: e.matmul(ps_off.t[0:64, 0:8], bufs["ones"].t[0:1, 0:64], mrow.ap(),
                             start=True, stop=True)],
         r=["mrow", "ones"], w=["ps_off"])
    P.op("v", lambda e: e.tensor_copy(msk.t[0:64, :], ps_off.t[0:64, 0:8]),
         r=["ps_off"], w=["msk"])
    P.op("g", lambda e: e.memset(offb.t[0:64, :], 0.0), w=[("off", 0)])
    for r in range(8):
        src_b, dst_b = (offb, offc) if r % 2 == 0 else (offc, offb)
        P.op("v", lambda e, r=r, src_b=src_b, dst_b=dst_b: e.scalar_tensor_tensor(
            dst_b.t[0:64, :], stin.t[0:64, r, :], msk.t[0:64, r:r + 1], src_b.t[0:64, :],
            ALU.mult, ALU.add),
            r=[("stin", r), "msk", ("off", r)], w=[("off", r + 1)])
    ob16 = Buf(nc, "ob16", [64, H * 65], BF16)
    P.op("v", lambda e: e.tensor_copy(ob16.t[0:64, :], offc.t[0:64, :]),
         r=[("off", 8)], w=["ob16"])
    P.mm([(lambda e, c0=c0, cw=cw: e.matmul(ps_dup.t[:, c0:c0 + cw], Jf.t[0:64, :],
                                            ob16.t[0:64, c0:c0 + cw], start=True, stop=True))
          for c0, cw in ((0, 512), (512, 512), (1024, 16))],
         r=["ob16", "Jfull"], w=["ps_dup"])
    P.op("v", lambda e: e.tensor_copy(offdup.ap(), ps_dup.t[:, 0:1040]),
         r=["ps_dup"], w=["offdup"])
    P.barrier()
    ps_dup.free(); ps_off.free(); ob16.free(); offc.free(); offb.free(); mrow.free(); msk.free(); stin.free()
    load_w_transposed(P, nc, bufs, "WqT", wq, D, D, ps_tr)
    load_w_transposed(P, nc, bufs, "WoT", wo, D, D, ps_tr)
    emit_l0_attn_main(P, nc, bufs, snaps, offdup)
    P.barrier()
    bufs["WoT"].free(); bufs["WqT"].free()
    offdup.free(); bufs["Jfull"].free(); stot_sb.free(); snaps.free()
    bufs["WvT"].free(); bufs["WkT"].free()
    emit_gating(P, nc, bufs, c_store)
    ps_tr.free()
    emit_moe(P, nc, bufs, w1, w2, c_store)
    P.barrier()
    gate_sb.free(); c_store.free(); xnT.free()
    bufs["hmask"] = Buf(nc, "hmask", [128, NT], F32)
    bufs["xm"] = Buf(nc, "xm", [128, NT, 1024], BF16)
    bufs["u_bf"] = Buf(nc, "u_bf", [128, NT, 1024], BF16)
    bufs["st_sb"] = Buf(nc, "st_sb", [128, NT], F32)
    bufs["s_sb"] = Buf(nc, "s_sb", [128, NT], F32)
    bufs["ps_trb3"] = Buf(nc, "ps_trb3", [128, 1024], BF16, psum=True)
    emit_l1_preamble(P, nc, bufs, {"snaps": True, "store": True, "free_tmp": True}, dram)
    P.dma("y", lambda e: e.dma_start(ccin2[0:64, :], bufs["srun"].t[0:64, :]),
          r=["srun"], w=["ccin2"])
    P.dma("y", lambda e: e.dma_start(
        ccin2[64:66, :].rearrange("r c -> (r c)").rearrange("(p f) -> p f", p=128),
        bufs["s_sb"].ap()),
        r=[("s_sb", i) for i in range(NT)], w=[("ccin2", "s")])
    P.cc(1, lambda e: e.collective_compute(
        "AllGather", ALU.bypass, replica_groups=[list(range(8))],
        ins=[ccin2.opt()], outs=[ccout2.opt()]),
        r=["ccin2", ("ccin2", "s")], w=["ccout2"])
    zall = ccout2
    bufs["S_in"] = S_in = Buf(nc, "S_in", [64, 1024], F32)
    s_full = Buf(nc, "s_full", [128, 64], F32)
    bufs["s_full"] = s_full
    zin = Buf(nc, "zin", [64, 8, 1024], BF16)
    sfin = Buf(nc, "sfin", [128, 8, 16], F32)
    ohb = Buf(nc, "ohb", [64, 24], F32)
    sm = Buf(nc, "smb", [128, 8], F32)
    rowb = Buf(nc, "rowb", [1, 32], F32)
    dec = Buf(nc, "dec", [64, 12], F32)
    acc = Buf(nc, "z_acc", [64, 1024], F32)
    ps_sm = Buf(nc, "ps_sm", [128, 512], F32, psum=True)
    for r in range(8):
        P.dma("g", lambda e, r=r: e.dma_start(zin.t[0:64, r, :], zall[r * 66:r * 66 + 64, :]),
              r=["ccout2"], w=[("zin", r)])
        P.dma("y", lambda e, r=r: e.dma_start(
            sfin.t[:, r, :],
            zall[r * 66 + 64:r * 66 + 66, :].rearrange("r c -> (r c)").rearrange("(p f) -> p f", p=128)),
            r=["ccout2"], w=[("sfin", r)])
    P.dma("g", lambda e: e.dma_start(rowb.t[0:1, 0:24], dist1h), w=["rowb"])
    P.dma("g", lambda e: e.dma_start(rowb.t[0:1, 24:32], smask_d), w=[("rowb", "sm")])
    P.mm([lambda e: e.matmul(ps_sm.t[:, 0:32], bufs["ones"].t[0:1, 0:128], rowb.t[0:1, 0:32],
                             start=True, stop=True)],
         r=["rowb", ("rowb", "sm"), "ones"], w=["ps_sm"])
    P.op("v", lambda e: e.tensor_copy(ohb.t[0:64, 0:24], ps_sm.t[0:64, 0:24]),
         r=["ps_sm"], w=["ohb"])
    P.op("v", lambda e: e.tensor_copy(sm.ap(), ps_sm.t[:, 24:32]),
         r=["ps_sm"], w=["smb"])
    for d_ in range(3):
        P.op("v", lambda e, d_=d_: e.tensor_scalar(dec.t[0:64, d_:d_ + 1], bufs["na_col_ap"],
                                                   2048.0 * (d_ + 1), None, ALU.mult),
             r=["na_col"], w=[("dec", d_)])
        P.op("s", lambda e, d_=d_: e.activation(dec.t[0:64, d_:d_ + 1], dec.t[0:64, d_:d_ + 1], AF.Exp),
             r=[("dec", d_)], w=[("dec", d_)])
    for r in range(8):
        P.op("v", lambda e, r=r: e.tensor_tensor(dec.t[0:64, 3 + r:4 + r],
                                                 ohb.t[0:64, r * 3:r * 3 + 1],
                                                 dec.t[0:64, 0:1], ALU.mult),
             r=["ohb", ("dec", 0)], w=[("decr", r)])
        for d_ in (1, 2):
            P.op("v", lambda e, r=r, d_=d_: e.scalar_tensor_tensor(
                dec.t[0:64, 3 + r:4 + r], dec.t[0:64, d_:d_ + 1],
                ohb.t[0:64, r * 3 + d_:r * 3 + d_ + 1], dec.t[0:64, 3 + r:4 + r],
                ALU.mult, ALU.add),
                r=["ohb", ("dec", d_), ("decr", r)], w=[("decr", r)])
    P.op("g", lambda e: e.memset(acc.t[0:64, :], 0.0), w=[("zacc", 0)])
    for r in range(8):
        src_b, dst_b = (acc, S_in) if r % 2 == 0 else (S_in, acc)
        P.op("v", lambda e, r=r, src_b=src_b, dst_b=dst_b: e.scalar_tensor_tensor(
            dst_b.t[0:64, :], zin.t[0:64, r, :], dec.t[0:64, 3 + r:4 + r], src_b.t[0:64, :],
            ALU.mult, ALU.add),
            r=[("zin", r), ("decr", r), ("zacc", r)], w=[("zacc", r + 1)])
    for slot in range(4):
        P.op("v", lambda e, slot=slot: e.tensor_scalar(
            s_full.t[:, slot * 16:(slot + 1) * 16], sfin.t[:, slot, :],
            sm.t[:, slot:slot + 1], None, ALU.mult),
            r=[("sfin", slot), "smb"], w=[("s_full", slot)])
        P.op("v", lambda e, slot=slot: e.scalar_tensor_tensor(
            s_full.t[:, slot * 16:(slot + 1) * 16], sfin.t[:, slot + 4, :],
            sm.t[:, slot + 4:slot + 5],
            s_full.t[:, slot * 16:(slot + 1) * 16],
            ALU.mult, ALU.add),
            r=[("sfin", slot + 4), "smb", ("s_full", slot)], w=[("s_full", slot), "s_full"])
    P.barrier()
    emit_routing_select(P, nc, bufs, s_full, ps_sm, tok_off)
    P.barrier()
    ps_sm.free(); acc.free(); dec.free(); rowb.free(); sm.free(); ohb.free(); sfin.free(); zin.free()
    emit_l1_rest(P, nc, bufs, dram)
    s_full.free(); S_in.free()
    bufs["lam"].free(); bufs["wsn"].free(); bufs["_erev"].free(); bufs["E_small_b"].free()
    bufs["E_small"].free(); bufs["_ldt"].free(); bufs["_tau"].free(); bufs["sp_scratch"].free()
    bufs["s_sb"].free(); bufs["st_sb"].free(); bufs["u_bf"].free(); bufs["xm"].free()
    bufs["ps_trb3"].free()
    emit_light(P, nc, bufs, dram)
    emit_heavy(P, nc, bufs, dram)
    for qq in range(4):
        P.dma("y", lambda e, qq=qq: e.dma_start(
            out.rearrange("(n p) d -> p n d", p=128)[:, qq * 4:(qq + 1) * 4, :],
            x_res.t[:, qq * 4:(qq + 1) * 4, :]),
            r=[("x_res", i) for i in range(qq * 4, qq * 4 + 4)], w=[("out", qq)])
    P.barrier()
    P.finalize()
    P.close()
    return nc


def kernel(**inputs):
    shards = shard_inputs(inputs)
    f32 = lambda a: np.ascontiguousarray(np.asarray(a, dtype=np.float32))
    ncF = build_F()
    maps = []
    for c in range(8):
        b, q = c // 4, c % 4
        pm = np.zeros((1, 8), np.float32)
        for r in range(4 * b, c):
            pm[0, r] = 1.0
        d1h = np.zeros((1, 24), np.float32)
        sm = np.zeros((1, 8), np.float32)
        for r in range(8):
            if r // 4 == b:
                sm[0, r] = 1.0
                if r % 4 < q:
                    d1h[0, r * 3 + (q - (r % 4) - 1)] = 1.0
        maps.append({
            "x": shards[c], "Wq": f32(inputs["L0_attn_W_q"]), "Wk": f32(inputs["L0_attn_W_k"]),
            "Wv": f32(inputs["L0_attn_W_v"]), "Wo": f32(inputs["L0_attn_W_o"]),
            "gate_w": f32(inputs["L0_ffn_gate_w"]),
            "w1": f32(inputs["L0_ffn_w1"]).reshape(8 * 1024, 4096),
            "w2": f32(inputs["L0_ffn_w2"]).reshape(8 * 4096, 1024),
            "pred_mask": pm, "dist1h": d1h, "smask": sm,
            "tok_off": np.full((1, 1), q * TPC, np.float32),
            "mix": f32(inputs["L1_attn_mix"]), "Dvec": f32(inputs["L1_attn_D"]),
            "log_tau": f32(inputs["L1_attn_log_tau"]),
            "log_dt": f32(inputs["L1_attn_log_dt"]).reshape(1),
            "norm1_w": f32(inputs["L1_attn_norm1_w"]), "norm1_b": f32(inputs["L1_attn_norm1_b"]),
            "norm2_w": f32(inputs["L1_attn_norm2_w"]), "norm2_b": f32(inputs["L1_attn_norm2_b"]),
            "fc1_w": f32(inputs["L1_attn_fc1_w"]), "fc1_b": f32(inputs["L1_attn_fc1_b"]),
            "fc2_w": f32(inputs["L1_attn_fc2_w"]), "fc2_b": f32(inputs["L1_attn_fc2_b"]),
            "routing": f32(inputs["L1_ffn_routing_token"]),
            "gamma_light": f32(inputs["L1_ffn_gamma_light"]),
            "w1_light": f32(inputs["L1_ffn_w1_light"]), "b1_light": f32(inputs["L1_ffn_b1_light"]),
            "w2_light": f32(inputs["L1_ffn_w2_light"]), "b2_light": f32(inputs["L1_ffn_b2_light"]),
            "gamma_heavy": f32(inputs["L1_ffn_gamma_heavy"]),
            "w1_heavy": f32(inputs["L1_ffn_w1_heavy"]), "b1_heavy": f32(inputs["L1_ffn_b1_heavy"]),
            "w2_heavy": f32(inputs["L1_ffn_w2_heavy"]), "b2_heavy": f32(inputs["L1_ffn_b2_heavy"]),
        })
    res = run_bass_kernel_spmd(ncF, maps, core_ids=list(range(8)))
    out = np.zeros((2, 8192, 1024), np.float32)
    for c in range(8):
        b, q = c // 4, c % 4
        out[b, q * TPC:(q + 1) * TPC] = res.results[c]["out"]
    return out
